# version 09
# baseline (speedup 1.0000x reference)
import numpy as np
import ml_dtypes
import concourse.bass as bass
import concourse.mybir as mybir
from concourse.bass_utils import run_bass_kernel_spmd

F32 = mybir.dt.float32
BF16 = mybir.dt.bfloat16
FP8 = mybir.dt.float8e4
ALU = mybir.AluOpType
ACTF = mybir.ActivationFunctionType
AX = mybir.AxisListType
BF = ml_dtypes.bfloat16

B, L, D = 16, 2048, 2048
NCORES = 8
SEQS = 2
H, DH, NH = 8, 128, 2
KD = H * DH
TT = 512
NTILE = L // TT
CH = 64
CPT = TT // CH
FF = 4 * D
NKC = D // 128
EPS_LN = 1e-5
EPS_ORMS = 1e-5
FP8MAX = 448.0

PG_BETA, PG_PA, PG_PW, PG_MQ, PG_GEND, PG_NBGR, PG_W, PG_DECAY, PG_GRQ = range(9)
NPG = 9



class Task:
    __slots__ = ("tid", "eng", "emit", "deps", "milestone", "dma_sem", "dma_val",
                 "snapshot", "waits", "need_inc", "label", "always_sync")

    def __init__(self, tid, eng, emit, label=""):
        self.always_sync = False
        self.tid = tid
        self.eng = eng
        self.emit = emit
        self.deps = []
        self.milestone = None
        self.dma_sem = None
        self.dma_val = None
        self.snapshot = None
        self.waits = None
        self.need_inc = False
        self.label = label


class Prog:
    ENGS = ("pe", "act", "dve", "pool", "sp")

    def __init__(self):
        self.paranoid = False
        self.tasks = []
        self.buf_writer = {}
        self.buf_readers = {}
        self.dma_sem_count = {}

    def new_dma_sem(self, name):
        assert name not in self.dma_sem_count
        self.dma_sem_count[name] = 0
        return name

    def task(self, eng, emit, reads=(), writes=(), deps=(), dma=None, label="",
             always_sync=False):
        t = Task(len(self.tasks), eng, emit, label)
        par = self.paranoid
        if isinstance(par, (set, frozenset, list, tuple)):
            par = eng in par
        t.always_sync = always_sync or bool(par)
        dd = set()
        for d in deps:
            if d is not None:
                dd.add(d.tid if isinstance(d, Task) else int(d))
        for k in reads:
            w = self.buf_writer.get(k)
            if w is not None:
                dd.add(w)
        for k in writes:
            w = self.buf_writer.get(k)
            if w is not None:
                dd.add(w)
            for r in self.buf_readers.get(k, ()):
                dd.add(r)
        dd.discard(t.tid)
        t.deps = sorted(dd)
        if dma is not None:
            if dma not in self.dma_sem_count:
                self.new_dma_sem(dma)
            t.dma_sem = dma
            self.dma_sem_count[dma] += 1
            t.dma_val = 16 * self.dma_sem_count[dma]
        for k in reads:
            self.buf_readers.setdefault(k, []).append(t.tid)
        for k in writes:
            self.buf_writer[k] = t.tid
            self.buf_readers[k] = []
        self.tasks.append(t)
        return t

    def plan(self):
        for t in self.tasks:
            for d in t.deps:
                dt = self.tasks[d]
                if dt.dma_sem is None and (dt.eng != t.eng or dt.always_sync):
                    dt.need_inc = True
        counts = {e: 0 for e in self.ENGS}
        for t in self.tasks:
            if t.dma_sem is None and t.need_inc:
                counts[t.eng] += 1
                t.milestone = counts[t.eng]
        clocks = {e: {} for e in self.ENGS}

        def merge(dst, src):
            for k, v in src.items():
                if dst.get(k, 0) < v:
                    dst[k] = v

        per_eng = {e: [] for e in self.ENGS}
        for t in self.tasks:
            clk = clocks[t.eng]
            waits = []
            for d in t.deps:
                dt = self.tasks[d]
                if dt.dma_sem is not None:
                    key = ("dma", dt.dma_sem)
                    if clk.get(key, 0) < dt.dma_val:
                        waits.append((key, dt.dma_val))
                        clk[key] = dt.dma_val
                        merge(clk, dt.snapshot)
                elif dt.eng == t.eng and not dt.always_sync:
                    merge(clk, dt.snapshot)
                else:
                    key = ("eng", dt.eng)
                    if clk.get(key, 0) < dt.milestone:
                        waits.append((key, dt.milestone))
                        clk[key] = dt.milestone
                        merge(clk, dt.snapshot)
            t.waits = waits
            snap = dict(clk)
            if t.dma_sem is not None:
                k = ("dma", t.dma_sem)
                if snap.get(k, 0) < t.dma_val:
                    snap[k] = t.dma_val
            elif t.milestone is not None:
                snap[("eng", t.eng)] = t.milestone
            t.snapshot = snap
            per_eng[t.eng].append(t)
        return per_eng



def build_consts():
    c = {}
    c["ident_bf"] = np.eye(128, dtype=BF)
    c["ident_f32"] = np.eye(128, dtype=np.float32)
    c["ones_row_f32"] = np.ones((1, 128), np.float32)
    oh = np.zeros((128, 64), BF)
    for h in range(8):
        oh[:, 8 * h + h] = 1
    c["onehot8"] = oh
    c["ones_col_bf"] = np.ones((128, 1), BF)
    NEG = -1e30
    sub = np.array([2 * i if i < 64 else 2 * (i - 64) + 1 for i in range(128)])
    maskA = np.where(sub[None, :] < sub[:, None], 0.0, NEG).astype(np.float32)
    c["maskA"] = maskA
    subo = 2 * np.arange(64) + 1
    maskWT = np.where(sub[:, None] <= subo[None, :], 0.0, NEG).astype(np.float32)
    c["maskWT"] = maskWT
    return c


CONSTS = build_consts()
C_SHAPES = {k: (list(v.shape), F32 if v.dtype == np.float32 else BF16)
            for k, v in CONSTS.items()}



class View:

    def __init__(self, ap, key):
        self.ap = ap
        self.key = key

    def __getitem__(self, idx):
        return self.ap[idx]

    @property
    def full(self):
        return self.ap


class Builder:
    def __init__(self, nc, taps=(), paranoid=False):
        self.nc = nc
        self.p = Prog()
        self.p.paranoid = paranoid
        self.taps = set(taps)
        self.tap_tensors = {}
        self.views = {}
        self.hbm = {}

    def setup_memory(self):
        nc = self.nc
        self.p0 = nc.alloc_psum_tensor("p0", [128, 512], F32)
        self.p1 = nc.alloc_psum_tensor("p1", [128, 512], F32)
        self.p2 = nc.alloc_psum_tensor("p2", [128, 512], F32)
        self.p3 = nc.alloc_psum_tensor("p3", [128, 512], F32)
        self.p4 = nc.alloc_psum_tensor("p4", [128, 512], F32)
        self.p5 = nc.alloc_psum_tensor("p5", [128, 512], F32)
        self.tbf = nc.alloc_psum_tensor("tbf", [128, 1024], BF16)
        self.tf32 = nc.alloc_psum_tensor("tf32", [128, 512], F32)

        for name, (shape, dt) in C_SHAPES.items():
            setattr(self, "c_" + name, nc.alloc_sbuf_tensor("c_" + name, shape, dt))
        self.convw = nc.alloc_sbuf_tensor("convw", [128, 40 * 4], F32)
        self.wba = nc.alloc_sbuf_tensor("wba", [128, NKC * 24], BF16)
        self.dtb_col = nc.alloc_sbuf_tensor("dtb_col", [8, 1], F32)
        self.nega_col = nc.alloc_sbuf_tensor("nega_col", [8, 1], F32)
        self.onw_col = nc.alloc_sbuf_tensor("onw_col", [128, 1], F32)
        self.scal = nc.alloc_sbuf_tensor("scal", [128, 32], F32)
        self.q8scr = nc.alloc_sbuf_tensor("q8scr", [128, 1024], FP8)
        rem = nc.sbuf_bytes_remaining
        arn = rem // 2
        self.AR = nc.alloc_sbuf_tensor("AR", [128, arn], BF16)
        self.ar_off = 0
        self.ar_n = arn

    def view(self, name, parts, n, dtype, offset=None):
        if dtype == F32:
            units = 2 * n
        elif dtype == FP8:
            units = (n + 1) // 2
        else:
            units = n
        units = (units + 31) & ~31
        if offset is None:
            offset = self.ar_off
            self.ar_off += units
            assert self.ar_off <= self.ar_n, f"arena overflow at {name} ({self.ar_off}/{self.ar_n})"
        ap = self.AR[0:parts, offset:offset + units]
        if dtype == F32:
            ap = ap.bitcast(F32)[0:parts, 0:n]
        elif dtype == FP8:
            ap = ap.bitcast(FP8)[0:parts, 0:n]
        else:
            ap = ap[0:parts, 0:n]
        v = View(ap, name)
        self.views[name] = v
        return v

    def dram(self, name, shape, dtype, kind=None):
        if kind:
            h = self.nc.dram_tensor(name, shape, dtype, kind=kind)
        else:
            h = self.nc.dram_tensor(name, shape, dtype)
        self.hbm[name] = h
        return h

    def tap(self, name, src_ap, shape, dtype, reads, deps=()):
        if name not in self.taps:
            return
        t = self.dram("tap_" + name, shape, dtype, kind="ExternalOutput")
        tk = self.p.task("act", lambda e, t=t, s=src_ap: e.dma_start(out=t.ap(), in_=s),
                         reads=list(reads), deps=list(deps), dma="tap_" + name,
                         label="tap_" + name)
        self.tap_list.append(tk)
        self.tap_tensors[name] = t

    def declare_io(self):
        d = self.dram
        self.x_in = d("x", [SEQS, L, D], F32, kind="ExternalInput")
        self.w_in = {}
        for nm, shp in [("Wq", [KD, D]), ("Wk", [2 * KD, D]), ("Wv", [2 * KD, D]),
                        ("Wb", [16, D]), ("Wa", [8, D]), ("Wg", [KD, D]),
                        ("Wo", [D, KD]), ("wfc", [FF, D]), ("wproj", [D, FF])]:
            self.w_in[nm] = d(nm, shp, F32, kind="ExternalInput")
        for nm in ["A_log", "dt_bias"]:
            self.w_in[nm] = d(nm, [8, 1], F32, kind="ExternalInput")
        self.w_in["o_norm_w"] = d("o_norm_w", [128, 1], F32, kind="ExternalInput")
        for nm in ["conv_q", "conv_k", "conv_v"]:
            shp = [KD, 4] if nm == "conv_q" else [2 * KD, 4]
            self.w_in[nm] = d(nm, shp, F32, kind="ExternalInput")
        for nm, (shape, dt) in C_SHAPES.items():
            self.w_in[nm] = d("c" + nm, shape, dt, kind="ExternalInput")
        self.out = d("out", [SEQS, L, D], F32, kind="ExternalOutput")
        self.h_wbf = {}
        for nm, shp in [("Wq", [KD, D]), ("Wk", [2 * KD, D]), ("Wv", [2 * KD, D]),
                        ("Wg", [KD, D]), ("Wo", [D, KD])]:
            self.h_wbf[nm] = d("bf_" + nm, shp, BF16)
        self.h_wt = {}
        for nm, O, IN in [("Wq", KD, D), ("Wk", 2 * KD, D), ("Wv", 2 * KD, D),
                          ("Wg", KD, D), ("Wo", D, KD)]:
            self.h_wt[nm] = d("wt_" + nm, [O // 128, IN // 128, 128, 128], BF16)
        self.h_x1 = d("x1h", [D, SEQS * L], F32)
        self.h_r = d("rh", [FF, SEQS * L], F32)
        self.h_pc = d("pch", [D, SEQS * L], F32)
        self.h_q8bf = {"wfc": d("q8bf_fc", [FF, D], BF16),
                       "wproj": d("q8bf_pj", [D, FF], BF16)}
        self.h_cc_in = d("cc_in", [1, 4], F32)
        self.h_cc_out = d("cc_out", [1, 4], F32)

    def setup_views(self):
        v = self.view
        self.wmtile = [v(f"wmtile{i}", 128, 1024, F32) for i in range(3)]
        self.q8v = v("q8v", 128, 2048, FP8)
        self.wmax = v("wmax", 128, 4, F32)
        self.a_base = self.ar_off
        self.xtm = [v("xtm0", 128, 2048, F32), v("xtm1", 128, 2048, F32)]
        self.xfm = v("xfm", 128, 16 * 512, F32)
        self.xb = v("xb", 128, 16 * 512, BF16)
        self.kbuf = v("kbuf", 128, 8 * 1024, BF16)
        self.vbuf = v("vbuf", 128, 8 * 1024, BF16)
        self.qbuf = v("qbuf", 128, 8 * 512, BF16)
        self.sgbuf = v("sgbuf", 128, 8 * 512, BF16)
        self.pre = [v(f"pre{i}", 128, 520, BF16) for i in range(2)]
        self.cs = [v(f"cs{i}", 128, 512, BF16) for i in range(2)]
        self.hist = v("hist", 128, 40 * 4, BF16)
        rows = ["G", "g", "tmpB", "tmpC"]
        self.r_ = {nm: v("row_" + nm, 8, 512, F32) for nm in rows}
        self.lnrk = v("row_lnrk", 8, 1024, F32)
        self.PWs = v("row_PWs", 8, 1024, F32)
        self.beta_s = v("row_beta_s", 8, 1024, F32)
        self.PA_s = v("row_PA_s", 8, 1024, F32)
        self.MQ_s = v("row_MQ_s", 8, 1024, F32)
        self.GEND_s = v("row_GEND_s", 8, 1024, F32)
        self.param = v("param", 128, CPT * NPG * 8, F32)
        self.S = v("S", 128, 8 * 128, F32)
        self.Sbf = v("Sbf", 128, 8 * 128, BF16)
        self.eA = v("eA", 128, 512, F32)
        self.eW = View(self.eA.ap[:, 0:256], "eA")
        self.kw = v("kw", 128, 512, BF16)
        self.Ngen = [v(f"Ngen{i}", 128, 512, BF16) for i in range(2)]
        self.NTgen = [v(f"NTgen{i}", 128, 512, BF16) for i in range(2)]
        self.U0 = v("U0", 128, 512, BF16)
        self.Ucur = [v(f"Ucur{i}", 128, 512, BF16) for i in range(2)]
        self.watt = View(self.U0.ap[:, 0:256], "U0")
        self.t1s = v("t1s", 128, 512, BF16)
        self.osq = View(self.t1s.ap[0:64, :], "t1s")
        self.dWT = View(self.t1s.ap[:, 0:256], "t1s")
        self.dA = View(self.Ngen[0].ap, "Ngen0")
        self.oraw = View(self.eA.ap[0:64, 0:512], "eA")
        self.rmsacc = v("rmsacc", 64, 16, F32)
        self.on_ = View(self.Ngen[1].ap[0:64, :], "Ngen1")
        self.ofm = v("ofm", 128, 8 * 512, BF16)
        self.wg_ = [v("wg0", 128, 16 * 128, BF16), v("wg1", 128, 16 * 128, BF16)]
        self.ba_bf = View(self.wg_[0].ap[0:24, :], "wg0")
        self.amax = v("amax", 128, 4, F32)
        self.rbank = v("rbank", 1, 768, F32)
        self.a_end = self.ar_off
        assert self.ar_off <= self.ar_n

    def prologue(self):
        p, nc = self.p, self.nc
        for nm in C_SHAPES:
            t = getattr(self, "c_" + nm)
            src = self.w_in[nm]
            p.task("sp", lambda e, t=t, s=src: e.dma_start(out=t[:, :], in_=s.ap()),
                   writes=[f"c_{nm}"], dma="cld", label=f"ld_{nm}")
        for i, nm in enumerate(["conv_k", "conv_v", "conv_q"]):
            nch = 16 if nm != "conv_q" else 8
            base = (0, 16, 32)[i]
            src = self.w_in[nm]
            for c in range(nch):
                cw = self.convw
                p.task("sp", lambda e, cw=cw, s=src, c=c, base=base:
                       e.dma_start(out=cw[:, (base + c) * 4:(base + c) * 4 + 4],
                                   in_=s[c * 128:(c + 1) * 128, :]),
                       writes=[f"convw{base + c}"], dma="cld", label="ld_conv")
        p.task("sp", lambda e: e.dma_start(out=self.dtb_col[:, :], in_=self.w_in["dt_bias"].ap()),
               writes=["dtb"], dma="cld")
        p.task("sp", lambda e: e.dma_start(out=self.scal[0:8, 0:1], in_=self.w_in["A_log"].ap()),
               writes=["alog_raw"], dma="cld")
        p.task("act", lambda e: e.activation(out=self.scal[0:8, 1:2], in_=self.scal[0:8, 0:1],
                                             func=ACTF.Exp),
               reads=["alog_raw"], writes=["alog_exp"], label="expA")
        p.task("dve", lambda e: e.tensor_scalar(out=self.nega_col[:, :], in0=self.scal[0:8, 1:2],
                                                scalar1=-1.0, scalar2=None, op0=ALU.mult),
               reads=["alog_exp"], writes=["nega"], label="negA")
        p.task("sp", lambda e: e.dma_start(out=self.onw_col[:, :], in_=self.w_in["o_norm_w"].ap()),
               writes=["onw"], dma="cld")

        p.task("pool", lambda e: e.dma_start(out=self.ba_bf[0:16, :], in_=self.w_in["Wb"].ap()),
               writes=["wg0"], dma="bald", label="ba_ld1")
        p.task("pool", lambda e: e.dma_start(out=self.ba_bf[16:24, :], in_=self.w_in["Wa"].ap()),
               writes=["wg0"], dma="bald", label="ba_ld2")
        for k in range(NKC):
            p.task("pe", lambda e, k=k: e.transpose(
                self.tbf[0:128, 0:24], self.ba_bf[0:24, k * 128:(k + 1) * 128],
                self.c_ident_bf[0:24, 0:24]),
                   reads=["wg0", "c_ident_bf"], writes=["tbf"], label="ba_tr")
            p.task("dve", lambda e, k=k: e.tensor_copy(
                out=self.wba[:, k * 24:(k + 1) * 24], in_=self.tbf[0:128, 0:24]),
                   reads=["tbf"], writes=[f"wba{k}"], label="ba_ev")

        for nm in ["Wk", "Wv", "Wq", "Wg", "Wo"]:
            src, dst = self.w_in[nm], self.h_wbf[nm]
            p.task("pool", lambda e, s=src, d=dst: e.dma_start(out=d.ap(), in_=s.ap()),
                   reads=[], writes=[f"hbf_{nm}"], dma=f"wcast_{nm}", label=f"cast_{nm}")
        for nm, O, IN in [("Wk", 2 * KD, D), ("Wv", 2 * KD, D), ("Wq", KD, D),
                          ("Wg", KD, D), ("Wo", D, KD)]:
            wbf, wt = self.h_wbf[nm], self.h_wt[nm]
            for ki in range(IN // 128):
                for og in range(O // 512):
                    buf = self.wg_[(ki * (O // 512) + og) % 2]
                    key = f"wg{(ki * (O // 512) + og) % 2}"
                    p.task("sp", lambda e, w=wbf, ki=ki, og=og, b=buf:
                           e.dma_start(out=b[:, 0:512],
                                       in_=w[og * 512:(og + 1) * 512, ki * 128:(ki + 1) * 128],
                                       transpose=True),
                           reads=[f"hbf_{nm}"], writes=[key], dma=key, label=f"wtt_{nm}")
                    if nm == "Wo":
                        p.task("dve", lambda e, b=buf: e.tensor_scalar(
                            out=b[:, 0:512], in0=b[:, 0:512], scalar1=self.onw_col[:, :],
                            scalar2=None, op0=ALU.mult),
                               reads=[key, "onw"], writes=[key], label="wo_scale")
                    src_ap = buf[:, 0:512].rearrange("p (m c) -> p m c", c=128)
                    dst_ap = wt[og * 4:(og + 1) * 4, ki, :, :].rearrange("m p c -> p m c")
                    p.task("act", lambda e, s=src_ap, d=dst_ap: e.dma_start(out=d, in_=s),
                           reads=[key, f"hbf_{nm}"], writes=[f"hwt_{nm}"], dma=f"wtst_{nm}",
                           label=f"wtst_{nm}")

        self.wm_pieces = []
        self._wm_idx = 0
        for wi, nm in enumerate(["wfc", "wproj"]):
            w = self.w_in[nm]
            R, Cc = (FF, D) if nm == "wfc" else (D, FF)
            w2 = w.ap().rearrange("(a p) b -> a p (b)", p=128)
            p.task("dve", lambda e, wi=wi: e.memset(self.wmax[:, wi:wi + 1], 0.0),
                   writes=[f"wmax{wi}"], label="wm0", always_sync=True)

            def mk_piece(a, cgs, wi=wi, w2=w2):
                def emit():
                    idx = self._wm_idx
                    self._wm_idx += 1
                    buf = self.wmtile[idx % 3]
                    key = f"wmt{idx % 3}"
                    p.task("pool", lambda e, a=a, c=cgs, b=buf, w2=w2:
                           e.dma_start(out=b[:, :], in_=w2[a, :, c * 1024:(c + 1) * 1024]),
                           writes=[key], dma=key, label="wm_ld")
                    p.task("dve", lambda e, b=buf, wi=wi: e.tensor_reduce(
                        out=self.scal[:, 8 + wi:9 + wi], in_=b[:, :], axis=AX.X,
                        op=ALU.max, apply_absolute_value=True),
                           reads=[key], writes=[f"wmr{wi}"], label="wm_red")
                    p.task("dve", lambda e, wi=wi: e.tensor_tensor(
                        out=self.wmax[:, wi:wi + 1], in0=self.wmax[:, wi:wi + 1],
                        in1=self.scal[:, 8 + wi:9 + wi], op=ALU.max),
                           reads=[f"wmr{wi}", f"wmax{wi}"], writes=[f"wmax{wi}"],
                           label="wm_acc")
                return emit
            for a in range(R // 128):
                for cgs in range(Cc // 1024):
                    self.wm_pieces.append(mk_piece(a, cgs))

    def phase_a_tile(self, s, t):
        p = self.p
        tt0 = t * TT
        gt0 = s * L + tt0
        tag = f"s{s}t{t}"

        for half in range(2):
            for bi in range(2):
                blk = half * 2 + bi
                buf = self.xtm[bi]
                key = f"xtm{bi}"
                p.task("sp", lambda e, blk=blk, b=buf, s=s:
                       e.dma_start(out=b[:, :],
                                   in_=self.x_in[s, tt0 + blk * 128: tt0 + (blk + 1) * 128, :]),
                       writes=[key], dma=key, label=f"{tag}_xld{blk}")
            for f in range(16):
                for bi in range(2):
                    p.task("pe", lambda e, f=f, bi=bi: e.transpose(
                        self.tf32[:, bi * 128:(bi + 1) * 128],
                        self.xtm[bi][:, f * 128:(f + 1) * 128], self.c_ident_f32[:, :]),
                           reads=[f"xtm{bi}", "c_ident_f32"], writes=["tf32"],
                           label=f"{tag}_xtr")
                co = half * 256
                p.task("dve", lambda e, f=f, co=co: e.tensor_copy(
                    out=self.xfm[:, f * 512 + co: f * 512 + co + 256], in_=self.tf32[:, 0:256]),
                       reads=["tf32"], writes=[f"xfm{f}"], label=f"{tag}_xev")
                p.task("act", lambda e, f=f, co=co: e.activation(
                    out=self.xb[:, f * 512 + co: f * 512 + co + 256],
                    in_=self.xfm[:, f * 512 + co: f * 512 + co + 256],
                    func=ACTF.Copy),
                       reads=[f"xfm{f}"], writes=[f"xb{f}"], label=f"{tag}_xevb")

        mlist = ([("kv", "Wk", m) for m in range(16)] +
                 [("kv", "Wv", m) for m in range(16)] +
                 [("ba", None, 0), ("ba", None, 1), ("ba", None, 2)] +
                 [("q", "Wq", m) for m in range(8)] +
                 [("g", "Wg", m) for m in range(8)])
        pbanks = [self.p0, self.p1]
        rot = 0
        prot = 0
        for kind, wname, m in mlist:
            pb = pbanks[prot % 2]
            pkey = f"p{prot % 2}"
            prot += 1
            if kind == "ba":
                off = m * 8
                for k in range(NKC):
                    p.task("pe", lambda e, k=k, off=off, pb=pb: e.matmul(
                        pb[0:8, 0:512], self.wba[:, k * 24 + off: k * 24 + off + 8],
                        self.xb[:, k * 512:(k + 1) * 512],
                        start=(k == 0), stop=(k == NKC - 1)),
                           reads=[f"wba{k}", f"xb{k}"], writes=[pkey],
                           label=f"{tag}_mm_ba{m}")
                self.emit_ba_evac(m, pb, pkey, tag)
                continue
            wt = self.h_wt[wname]
            nk = NKC if wname != "Wo" else 8
            buf = self.wg_[rot % 2]
            wkey = f"wg{rot % 2}"
            rot += 1
            p.task("sp", lambda e, wt=wt, m=m, b=buf, nk=nk:
                   e.dma_start(out=b[:, 0:nk * 128].rearrange("p (k c) -> p k c", c=128),
                               in_=wt[m, :, :, :].rearrange("k p c -> p k c")),
                   reads=[f"hwt_{wname}"], writes=[wkey], dma=wkey,
                   label=f"{tag}_wld_{wname}{m}")
            for k in range(nk):
                p.task("pe", lambda e, k=k, b=buf, pb=pb: e.matmul(
                    pb[:, 0:512], b[:, k * 128:(k + 1) * 128],
                    self.xb[:, k * 512:(k + 1) * 512],
                    start=(k == 0), stop=(k == nk - 1)),
                       reads=[wkey, f"xb{k}"], writes=[pkey],
                       label=f"{tag}_mm_{wname}{m}")
            if kind == "kv":
                cv = m if wname == "Wk" else 16 + m
                dst = self.kbuf if wname == "Wk" else self.vbuf
                nn, hh = m // 8, m % 8
                self.emit_conv(pb, pkey, cv, dst, hh, nn, s, t, tag)
            elif kind == "q":
                cv = 32 + m
                self.emit_conv(pb, pkey, cv, self.qbuf, m, None, s, t, tag)
            elif kind == "g":
                p.task("act", lambda e, m=m, pb=pb: e.activation(
                    out=self.sgbuf[:, m * 512:(m + 1) * 512], in_=pb[:, 0:512],
                    func=ACTF.Silu),
                       reads=[pkey], writes=[f"sg{m}"], label=f"{tag}_sg{m}")

        self.emit_norms(tag)
        self.emit_rows(tag)
        for c in range(CPT):
            self.emit_param(c, tag)
        self.tap(f"kbuf_{tag}", self.kbuf.ap, [128, 8192], BF16,
                 reads=[f"k{h}" for h in range(8)])
        self.tap(f"vbuf_{tag}", self.vbuf.ap, [128, 8192], BF16,
                 reads=[f"v{h}" for h in range(8)])
        self.tap(f"qbuf_{tag}", self.qbuf.ap, [128, 4096], BF16,
                 reads=[f"q{h}" for h in range(8)])
        self.tap(f"sgbuf_{tag}", self.sgbuf.ap, [128, 4096], BF16,
                 reads=[f"sg{h}" for h in range(8)])
        self.tap(f"G_{tag}", self.r_["G"].ap, [8, 512], F32, reads=["row_G"])
        self.tap(f"lnrk_{tag}", self.lnrk.ap, [8, 1024], F32, reads=["row_lnrk"])
        self.tap(f"betas_{tag}", self.beta_s.ap, [8, 1024], F32, reads=["row_beta_s"])
        self.tap(f"MQ_{tag}", self.MQ_s.ap, [8, 1024], F32, reads=["row_MQ_s"])
        self.tap(f"param_{tag}", self.param.ap, [128, CPT * NPG * 8], F32,
                 reads=[f"param{c}" for c in range(CPT)])
        for c in range(CPT):
            for G2 in range(2):
                self.emit_scan_group(s, t, c, G2, tag)
        self.tap(f"ofm_{tag}", self.ofm.ap, [128, 4096], BF16,
                 reads=[f"ofm{h}" for h in range(8)])
        self.tap(f"S_{tag}", self.S.ap, [128, 1024], F32,
                 reads=[f"S{h}" for h in range(8)])
        self.emit_epilogue(s, t, gt0, tag)
        self.tap(f"x1_{tag}", self.xfm.ap, [128, 8192], F32,
                 reads=[f"xfm{f}" for f in range(16)])

    def emit_conv(self, pb, pkey, cv, dstbuf, hh, nn, s, t, tag):
        p = self.p
        rot = cv % 2
        pre, cs = self.pre[rot], self.cs[rot]
        prek, csk = f"pre{rot}", f"cs{rot}"
        histk = f"hist{cv}"
        if t == 0:
            p.task("dve", lambda e, pre=pre: e.memset(pre[:, 0:3], 0.0),
                   writes=[prek], label=f"{tag}_hrz{cv}", always_sync=True)
        else:
            p.task("dve", lambda e, pre=pre, cv=cv: e.tensor_copy(
                out=pre[:, 0:3], in_=self.hist[:, cv * 4:cv * 4 + 3]),
                   reads=[histk], writes=[prek], label=f"{tag}_hrs{cv}")
        p.task("dve", lambda e, pre=pre, pb=pb: e.tensor_copy(
            out=pre[:, 3:515], in_=pb[:, 0:512]),
               reads=[pkey], writes=[prek], label=f"{tag}_pre{cv}")
        p.task("dve", lambda e, pre=pre, cv=cv: e.tensor_copy(
            out=self.hist[:, cv * 4:cv * 4 + 3], in_=pre[:, 512:515]),
               reads=[prek], writes=[histk], label=f"{tag}_hsv{cv}")
        w = self.convw
        p.task("dve", lambda e, pre=pre, cs=cs, cv=cv: e.tensor_scalar(
            out=cs[:, :], in0=pre[:, 0:512], scalar1=w[:, cv * 4:cv * 4 + 1],
            scalar2=None, op0=ALU.mult),
               reads=[prek, f"convw{cv}"], writes=[csk], label=f"{tag}_cv0_{cv}")
        for j in range(1, 4):
            p.task("dve", lambda e, pre=pre, cs=cs, cv=cv, j=j: e.scalar_tensor_tensor(
                out=cs[:, :], in0=pre[:, j:512 + j], scalar=w[:, cv * 4 + j:cv * 4 + j + 1],
                in1=cs[:, :], op0=ALU.mult, op1=ALU.add),
                   reads=[prek, csk, f"convw{cv}"], writes=[csk], label=f"{tag}_cv{j}_{cv}")
        if nn is None:
            out_ap = dstbuf[:, hh * 512:(hh + 1) * 512]
            dkey = f"q{hh}"
        else:
            out_ap = dstbuf.ap[:, hh * 1024:(hh + 1) * 1024].rearrange(
                "p (c q) -> p c q", q=128)[:, :, nn * 64:(nn + 1) * 64]
            dkey = (f"k{hh}" if dstbuf is self.kbuf else f"v{hh}")
        p.task("act", lambda e, out_ap=out_ap, cs=cs: e.activation(
            out=out_ap, in_=cs[:, :], func=ACTF.Silu),
               reads=[csk], writes=[dkey], label=f"{tag}_silu{cv}")

    def emit_ba_evac(self, m, pb, pkey, tag):
        p = self.p
        r = self.r_
        if m in (0, 1):
            dst = self.beta_s.ap.rearrange("p (c n q) -> p c n q", n=2, q=64)[:, :, m, :]
            tmp = r["tmpB"]
            p.task("act", lambda e, pb=pb, tmp=tmp: e.activation(
                out=tmp[:, :], in_=pb[0:8, 0:512], func=ACTF.Sigmoid),
                   reads=[pkey], writes=["row_tmpB"], label=f"{tag}_bsig{m}")
            p.task("dve", lambda e, dst=dst, tmp=tmp: e.tensor_scalar(
                out=dst, in0=tmp[:, :].rearrange("p (c q) -> p c q", q=64),
                scalar1=2.0, scalar2=None, op0=ALU.mult),
                   reads=["row_tmpB"], writes=["row_beta_s"], label=f"{tag}_bx2_{m}")
        else:
            p.task("act", lambda e, pb=pb: e.activation(
                out=r["tmpB"][:, :], in_=pb[0:8, 0:512], func=ACTF.Exp,
                bias=self.dtb_col[:, :]),
                   reads=[pkey, "dtb"], writes=["row_tmpB"], label=f"{tag}_gexp")
            p.task("act", lambda e: e.activation(
                out=r["tmpC"][:, :], in_=r["tmpB"][:, :], func=ACTF.Ln, bias=1.0),
                   reads=["row_tmpB"], writes=["row_tmpC"], label=f"{tag}_gln")
            p.task("dve", lambda e: e.tensor_scalar(
                out=r["g"][:, :], in0=r["tmpC"][:, :], scalar1=self.nega_col[:, :],
                scalar2=None, op0=ALU.mult),
                   reads=["row_tmpC", "nega"], writes=["row_g"], label=f"{tag}_gmul")

    def emit_norms(self, tag):
        p = self.p
        for half in range(2):
            for h in range(8):
                sq = self.cs[h % 2]
                sqk = f"cs{h % 2}"
                p.task("act", lambda e, h=h, half=half, sq=sq: e.activation(
                    out=sq[:, :], in_=self.kbuf[:, h * 1024 + half * 512: h * 1024 + half * 512 + 512],
                    func=ACTF.Square),
                       reads=[f"k{h}"], writes=[sqk], label=f"{tag}_ksq")
                p.task("pe", lambda e, h=h, sq=sq: e.matmul(
                    self.p5[0:8, 0:512], self.c_onehot8[:, 8 * h:8 * h + 8], sq[:, :],
                    start=(h == 0), stop=(h == 7)),
                       reads=[sqk, "c_onehot8"], writes=["p5"], label=f"{tag}_knrm")
            p.task("dve", lambda e: e.tensor_scalar(
                out=self.r_["tmpC"][:, :], in0=self.p5[0:8, 0:512], scalar1=1e-24,
                scalar2=None, op0=ALU.max),
                   reads=["p5"], writes=["row_tmpC"], label=f"{tag}_kcl")
            p.task("act", lambda e: e.activation(
                out=self.r_["tmpC"][:, :], in_=self.r_["tmpC"][:, :], func=ACTF.Ln),
                   reads=["row_tmpC"], writes=["row_tmpC"], label=f"{tag}_kln")
            p.task("dve", lambda e, half=half: e.tensor_scalar(
                out=self.lnrk[:, half * 512:(half + 1) * 512], in0=self.r_["tmpC"][:, :],
                scalar1=-0.5, scalar2=None, op0=ALU.mult),
                   reads=["row_tmpC"], writes=["row_lnrk"], label=f"{tag}_kl2")
        for h in range(8):
            sq = self.cs[h % 2]
            sqk = f"cs{h % 2}"
            p.task("act", lambda e, h=h, sq=sq: e.activation(
                out=sq[:, :], in_=self.qbuf[:, h * 512:(h + 1) * 512], func=ACTF.Square),
                   reads=[f"q{h}"], writes=[sqk], label=f"{tag}_qsq")
            p.task("pe", lambda e, h=h, sq=sq: e.matmul(
                self.p5[0:8, 0:512], self.c_onehot8[:, 8 * h:8 * h + 8], sq[:, :],
                start=(h == 0), stop=(h == 7)),
                   reads=[sqk, "c_onehot8"], writes=["p5"], label=f"{tag}_qnrm")
        p.task("dve", lambda e: e.tensor_scalar(
            out=self.r_["tmpC"][:, :], in0=self.p5[0:8, 0:512], scalar1=1e-24,
            scalar2=None, op0=ALU.max),
               reads=["p5"], writes=["row_tmpC"], label=f"{tag}_qcl")
        p.task("act", lambda e: e.activation(
            out=self.r_["tmpC"][:, :], in_=self.r_["tmpC"][:, :], func=ACTF.Ln),
               reads=["row_tmpC"], writes=["row_tmpC"], label=f"{tag}_qln")
        p.task("dve", lambda e: e.tensor_scalar(
            out=self.r_["tmpB"][:, :], in0=self.r_["tmpC"][:, :],
            scalar1=-0.5, scalar2=float(-0.5 * np.log(128.0)), op0=ALU.mult, op1=ALU.add),
               reads=["row_tmpC"], writes=["row_tmpB"], label=f"{tag}_ql2")

    def emit_rows(self, tag):
        p = self.p
        r = self.r_

        def ev(ap1024, par):
            return ap1024.rearrange("p (c n q) -> p c n q", n=2, q=64)[:, :, par, :]

        def pv(ap512):
            return ap512.rearrange("p (c q) -> p c q", q=64)

        for c in range(CPT):
            p.task("dve", lambda e, c=c: e.tensor_tensor_scan(
                out=r["G"][:, c * 64:(c + 1) * 64], data0=r["g"][:, c * 64:(c + 1) * 64],
                data1=r["g"][:, c * 64:(c + 1) * 64], initial=0.0,
                op0=ALU.add, op1=ALU.bypass),
                   reads=["row_g"], writes=["row_G"], label=f"{tag}_gcum{c}")
        for c in range(CPT):
            gsrc = r["G"][:, c * 64 + 63:c * 64 + 64]
            for par in range(2):
                dsl = self.GEND_s[:, c * 128 + par * 64: c * 128 + (par + 1) * 64]
                p.task("dve", lambda e, gsrc=gsrc, dsl=dsl: e.tensor_copy(
                    out=dsl, in_=bass.broadcast_tensor_aps(gsrc, dsl)[0]),
                       reads=["row_G"], writes=["row_GEND_s"], label=f"{tag}_ge{c}_{par}")
        for par in range(2):
            p.task("dve", lambda e, par=par: e.tensor_tensor(
                out=ev(self.MQ_s.ap, par), in0=pv(r["G"].ap), in1=pv(r["tmpB"].ap),
                op=ALU.add),
                   reads=["row_G", "row_tmpB"], writes=["row_MQ_s"], label=f"{tag}_mq{par}")
        for par in range(2):
            p.task("dve", lambda e, par=par: e.tensor_tensor(
                out=ev(self.PWs.ap, par), in0=ev(self.lnrk.ap, par),
                in1=pv(r["G"].ap), op=ALU.subtract),
                   reads=["row_lnrk", "row_G"], writes=["row_PWs"], label=f"{tag}_pw{par}")
        for par in range(2):
            bsl = ev(self.beta_s.ap, par)
            p.task("act", lambda e, par=par, bsl=bsl: e.activation(
                out=pv(r["tmpC"].ap), in_=bsl, func=ACTF.Ln),
                   reads=["row_beta_s"], writes=["row_tmpC"], label=f"{tag}_lnb{par}")
            p.task("dve", lambda e, par=par: e.tensor_tensor(
                out=pv(r["tmpC"].ap), in0=pv(r["tmpC"].ap), in1=ev(self.lnrk.ap, par),
                op=ALU.add),
                   reads=["row_tmpC", "row_lnrk"], writes=["row_tmpC"], label=f"{tag}_pb{par}")
            p.task("dve", lambda e, par=par: e.tensor_tensor(
                out=ev(self.PA_s.ap, par), in0=pv(r["G"].ap), in1=pv(r["tmpC"].ap),
                op=ALU.add),
                   reads=["row_tmpC", "row_G"], writes=["row_PA_s"], label=f"{tag}_pa{par}")

    def emit_param(self, c, tag):
        p = self.p
        groups = [(PG_BETA, self.beta_s, "row_beta_s"), (PG_PA, self.PA_s, "row_PA_s"),
                  (PG_PW, self.PWs, "row_PWs"), (PG_MQ, self.MQ_s, "row_MQ_s"),
                  (PG_GEND, self.GEND_s, "row_GEND_s")]
        for g, row, rk in groups:
            src = row[:, c * 128:(c + 1) * 128]
            p.task("pe", lambda e, g=g, src=src: e.transpose(
                self.tf32[:, g * 8:(g + 1) * 8], src, self.c_ident_f32[0:8, 0:8]),
                   reads=[rk, "c_ident_f32"], writes=["tf32"], label=f"{tag}_ptr{c}_{g}")
        pcol = lambda g: self.param[:, c * NPG * 8 + g * 8: c * NPG * 8 + (g + 1) * 8]
        p.task("dve", lambda e, c=c: e.tensor_copy(
            out=self.param[:, c * NPG * 8: c * NPG * 8 + 40], in_=self.tf32[:, 0:40]),
               reads=["tf32"], writes=[f"param{c}"], label=f"{tag}_pev{c}")
        p.task("act", lambda e: e.activation(out=pcol(PG_DECAY), in_=pcol(PG_GEND),
                                             func=ACTF.Exp),
               reads=[f"param{c}"], writes=[f"param{c}"], label=f"{tag}_pd1{c}")
        p.task("dve", lambda e: e.tensor_tensor(out=pcol(PG_W), in0=pcol(PG_PW),
                                                in1=pcol(PG_GEND), op=ALU.add),
               reads=[f"param{c}"], writes=[f"param{c}"], label=f"{tag}_pd2{c}")
        p.task("act", lambda e: e.activation(out=pcol(PG_W), in_=pcol(PG_W),
                                             func=ACTF.Exp),
               reads=[f"param{c}"], writes=[f"param{c}"], label=f"{tag}_pd3{c}")
        p.task("act", lambda e: e.activation(out=pcol(PG_GRQ), in_=pcol(PG_MQ),
                                             func=ACTF.Exp),
               reads=[f"param{c}"], writes=[f"param{c}"], label=f"{tag}_pd4{c}")
        p.task("act", lambda e: e.activation(out=pcol(PG_NBGR), in_=pcol(PG_PA),
                                             func=ACTF.Exp),
               reads=[f"param{c}"], writes=[f"param{c}"], label=f"{tag}_pd5{c}")
        p.task("dve", lambda e: e.tensor_scalar(
            out=pcol(PG_NBGR), in0=pcol(PG_NBGR), scalar1=-1.0, scalar2=None,
            op0=ALU.mult),
               reads=[f"param{c}"], writes=[f"param{c}"], label=f"{tag}_pd6{c}")

    def emit_scan_group(self, s, t, c, G2, tag):
        p = self.p
        hs = [4 * G2 + i for i in range(4)]
        pc = lambda g, h: self.param[:, c * NPG * 8 + g * 8 + h: c * NPG * 8 + g * 8 + h + 1]
        pc64 = lambda g, h: self.param[0:64, c * NPG * 8 + g * 8 + h: c * NPG * 8 + g * 8 + h + 1]
        kslice = lambda h: self.kbuf[:, h * 1024 + c * 128: h * 1024 + (c + 1) * 128]
        vslice = lambda h: self.vbuf[:, h * 1024 + c * 128: h * 1024 + (c + 1) * 128]
        qslice = lambda h: self.qbuf[:, h * 512 + c * 64: h * 512 + (c + 1) * 64]
        pk = f"param{c}"
        g = f"{tag}c{c}g{G2}"

        for si, h in enumerate(hs):
            p.task("pe", lambda e, si=si, h=h: e.transpose(
                self.tf32[0:1, si * 128:(si + 1) * 128], pc(PG_PW, h),
                self.c_ident_f32[:, :]),
                   reads=[pk, "c_ident_f32"], writes=["tf32"], label=g + "_rtP")
        for si, h in enumerate(hs):
            p.task("pe", lambda e, si=si, h=h: e.transpose(
                self.p1[0:1, 256 + si * 64: 256 + (si + 1) * 64], pc64(PG_MQ, h),
                self.c_ident_f32[0:64, 0:64]),
                   reads=[pk, "c_ident_f32"], writes=["p1"], label=g + "_rtM")
        p.task("dve", lambda e: e.tensor_copy(out=self.rbank[0:1, 0:512],
                                              in_=self.tf32[0:1, 0:512]),
               reads=["tf32"], writes=["rbank_pw"], label=g + "_rbp")
        p.task("dve", lambda e: e.tensor_copy(out=self.rbank[0:1, 512:768],
                                              in_=self.p1[0:1, 256:512]),
               reads=["p1"], writes=["rbank_mq"], label=g + "_rbm")
        for si, h in enumerate(hs):
            p.task("pe", lambda e, si=si, h=h: e.matmul(
                self.tf32[:, si * 128:(si + 1) * 128], self.c_ones_row_f32[:, :],
                self.rbank[0:1, si * 128:(si + 1) * 128], start=True, stop=True,
                skip_group_check=True),
                   reads=["c_ones_row_f32", "rbank_pw"], writes=["tf32"], label=g + "_oA")
        for si, h in enumerate(hs):
            p.task("dve", lambda e, si=si, h=h: e.scalar_tensor_tensor(
                out=self.eA[:, si * 128:(si + 1) * 128], in0=self.tf32[:, si * 128:(si + 1) * 128],
                scalar=pc(PG_PA, h), in1=self.c_maskA[:, :], op0=ALU.add, op1=ALU.add),
                   reads=["tf32", pk, "c_maskA"], writes=["eA"], label=g + "_eA")
        p.task("act", lambda e: e.activation(out=self.dA[:, :], in_=self.eA[:, :],
                                             func=ACTF.Exp),
               reads=["eA"], writes=["Ngen0"], label=g + "_expA")
        for si, h in enumerate(hs):
            p.task("pe", lambda e, si=si, h=h: e.transpose(
                self.tbf[:, si * 128:(si + 1) * 128], kslice(h), self.c_ident_bf[:, :]),
                   reads=[f"k{h}", "c_ident_bf"], writes=["tbf"], label=g + "_trK")
        for si, h in enumerate(hs):
            p.task("pe", lambda e, si=si, h=h: e.transpose(
                self.tbf[:, 512 + si * 128: 512 + (si + 1) * 128], vslice(h),
                self.c_ident_bf[:, :]),
                   reads=[f"v{h}", "c_ident_bf"], writes=["tbf"], label=g + "_trV")
        for si, h in enumerate(hs):
            p.task("dve", lambda e, si=si, h=h: e.tensor_scalar(
                out=self.kw[:, si * 128:(si + 1) * 128], in0=self.tbf[:, si * 128:(si + 1) * 128],
                scalar1=pc(PG_W, h), scalar2=None, op0=ALU.mult),
                   reads=["tbf", pk], writes=["kw"], label=g + "_kw")
        for si, h in enumerate(hs):
            p.task("pe", lambda e, si=si, h=h: e.matmul(
                self.p2[:, si * 128:(si + 1) * 128], kslice(h), kslice(h),
                start=True, stop=True, skip_group_check=True),
                   reads=[f"k{h}"], writes=["p2"], label=g + "_KK")
        p.task("dve", lambda e: e.scalar_tensor_tensor(
            out=self.Ngen[0][:, :], in0=self.p2[:, :], scalar=-1.0, in1=self.dA[:, :],
            op0=ALU.mult, op1=ALU.mult),
               reads=["p2", "Ngen0"], writes=["Ngen0"], label=g + "_mkN")
        if g.startswith("s0t0c0g0"):
            self.tap("scan_N1", self.Ngen[0].ap, [128, 512], BF16, reads=["Ngen0"])
            self.tap("scan_dA", self.dA.ap, [128, 512], BF16, reads=["Ngen0"])
        for si in range(4):
            p.task("pe", lambda e, si=si: e.transpose(
                self.tbf[:, si * 128:(si + 1) * 128], self.Ngen[0][:, si * 128:(si + 1) * 128],
                self.c_ident_bf[:, :]),
                   reads=["Ngen0", "c_ident_bf"], writes=["tbf"], label=g + "_trN")
        p.task("dve", lambda e: e.tensor_copy(out=self.NTgen[0][:, :], in_=self.tbf[:, 0:512]),
               reads=["tbf"], writes=["NTgen0"], label=g + "_evNT")
        if g.startswith("s0t0c0g0"):
            self.tap("scan_NT1", self.NTgen[0].ap, [128, 512], BF16, reads=["NTgen0"])
        for si, h in enumerate(hs):
            p.task("pe", lambda e, si=si, h=h: e.matmul(
                self.p0[:, si * 128:(si + 1) * 128], kslice(h),
                self.Sbf[:, h * 128:(h + 1) * 128], start=True, stop=True,
                skip_group_check=True),
                   reads=[f"k{h}", f"Sbf{h}"], writes=["p0"], label=g + "_KS")
        for si, h in enumerate(hs):
            p.task("dve", lambda e, si=si, h=h: e.tensor_scalar(
                out=self.t1s[:, si * 128:(si + 1) * 128], in0=self.p0[:, si * 128:(si + 1) * 128],
                scalar1=pc(PG_NBGR, h), scalar2=None, op0=ALU.mult),
                   reads=["p0", pk], writes=["t1s"], label=g + "_t1")
        for si, h in enumerate(hs):
            p.task("dve", lambda e, si=si, h=h: e.scalar_tensor_tensor(
                out=self.U0[:, si * 128:(si + 1) * 128],
                in0=self.tbf[:, 512 + si * 128: 512 + (si + 1) * 128],
                scalar=pc(PG_BETA, h), in1=self.t1s[:, si * 128:(si + 1) * 128],
                op0=ALU.mult, op1=ALU.add),
                   reads=["tbf", pk, "t1s"], writes=["U0"], label=g + "_R")
        for j in range(7):
            if j >= 1:
                src = (j - 1) % 2
                dst = j % 2
                for si in range(4):
                    p.task("pe", lambda e, si=si, src=src: e.matmul(
                        self.p3[:, si * 128:(si + 1) * 128],
                        self.NTgen[src][:, si * 128:(si + 1) * 128],
                        self.Ngen[src][:, si * 128:(si + 1) * 128],
                        start=True, stop=True, skip_group_check=True),
                           reads=[f"NTgen{src}", f"Ngen{src}"], writes=["p3"],
                           label=g + f"_sqN{j}")
                for si in range(4):
                    p.task("pe", lambda e, si=si, src=src: e.matmul(
                        self.p4[:, si * 128:(si + 1) * 128],
                        self.Ngen[src][:, si * 128:(si + 1) * 128],
                        self.NTgen[src][:, si * 128:(si + 1) * 128],
                        start=True, stop=True, skip_group_check=True),
                           reads=[f"NTgen{src}", f"Ngen{src}"], writes=["p4"],
                           label=g + f"_sqNT{j}")
                if j < 6:
                    p.task("act", lambda e, dst=dst: e.activation(
                        out=self.Ngen[dst][:, :], in_=self.p3[:, :], func=ACTF.Copy),
                           reads=["p3"], writes=[f"Ngen{dst}"], label=g + f"_evN{j}")
                p.task("dve", lambda e, dst=dst: e.tensor_copy(
                    out=self.NTgen[dst][:, :], in_=self.p4[:, :]),
                       reads=["p4"], writes=[f"NTgen{dst}"], label=g + f"_evNT{j}")
            ntb = f"NTgen{j % 2}"
            u_rhs = self.U0 if j == 0 else self.Ucur[(j - 1) % 2]
            u_rhs_k = "U0" if j == 0 else f"Ucur{(j - 1) % 2}"
            for si in range(4):
                p.task("pe", lambda e, si=si, j=j, u=u_rhs: e.matmul(
                    self.p5[:, si * 128:(si + 1) * 128],
                    self.NTgen[j % 2][:, si * 128:(si + 1) * 128],
                    u[:, si * 128:(si + 1) * 128],
                    start=(j == 0 and si == 0), stop=(j == 6 and si == 3),
                    skip_group_check=True),
                       reads=[ntb, u_rhs_k], writes=["p5"], label=g + f"_ap{j}")
            p.task("dve", lambda e, j=j: e.tensor_tensor(
                out=self.Ucur[j % 2][:, :], in0=self.p5[:, :], in1=self.U0[:, :],
                op=ALU.add),
                   reads=["p5", "U0"], writes=[f"Ucur{j % 2}"], label=g + f"_U{j}")
        if g.startswith("s0t0c0g0"):
            self.tap("scan_U0", self.U0.ap, [128, 512], BF16, reads=["U0"])
            self.tap("scan_U7", self.Ucur[0].ap, [128, 512], BF16, reads=["Ucur0"])
            self.tap("scan_kw", self.kw.ap, [128, 512], BF16, reads=["kw"])
        u7, u7k = self.Ucur[0], "Ucur0"
        for si, h in enumerate(hs):
            p.task("pe", lambda e, si=si, h=h: e.matmul(
                self.p1[:, si * 64:(si + 1) * 64], kslice(h), qslice(h),
                start=True, stop=True, skip_group_check=True),
                   reads=[f"k{h}", f"q{h}"], writes=["p1"], label=g + "_KQ")
        for si, h in enumerate(hs):
            p.task("pe", lambda e, si=si, h=h: e.matmul(
                self.p1[:, 256 + si * 64: 256 + (si + 1) * 64], self.c_ones_row_f32[:, :],
                self.rbank[0:1, 512 + si * 64: 512 + (si + 1) * 64], start=True, stop=True,
                skip_group_check=True),
                   reads=["c_ones_row_f32", "rbank_mq"], writes=["p1"], label=g + "_oW")
        for si, h in enumerate(hs):
            p.task("dve", lambda e, si=si, h=h: e.scalar_tensor_tensor(
                out=self.eW[:, si * 64:(si + 1) * 64],
                in0=self.p1[:, 256 + si * 64: 256 + (si + 1) * 64],
                scalar=pc(PG_PW, h), in1=self.c_maskWT[:, :], op0=ALU.add, op1=ALU.add),
                   reads=["p1", pk, "c_maskWT"], writes=["eA"], label=g + "_eW")
        p.task("act", lambda e: e.activation(out=self.dWT[:, :], in_=self.eW[:, :],
                                             func=ACTF.Exp),
               reads=["eA"], writes=["t1s"], label=g + "_expW")
        p.task("dve", lambda e: e.tensor_tensor(
            out=self.watt[:, :], in0=self.p1[:, 0:256], in1=self.dWT[:, :], op=ALU.mult),
               reads=["p1", "t1s"], writes=["U0"], label=g + "_mkW")
        if g.startswith("s0t0c0g0"):
            self.tap("scan_watt", self.watt.ap, [128, 256], BF16, reads=["U0"])
        for si, h in enumerate(hs):
            p.task("pe", lambda e, si=si, h=h: e.matmul(
                self.p0[0:64, si * 128:(si + 1) * 128], self.watt[:, si * 64:(si + 1) * 64],
                u7[:, si * 128:(si + 1) * 128], start=True, stop=True,
                skip_group_check=True),
                   reads=["U0", u7k], writes=["p0"], label=g + "_Oi")
        for si, h in enumerate(hs):
            p.task("pe", lambda e, si=si, h=h: e.matmul(
                self.p4[0:64, si * 128:(si + 1) * 128], qslice(h),
                self.Sbf[:, h * 128:(h + 1) * 128], start=True, stop=True,
                skip_group_check=True),
                   reads=[f"q{h}", f"Sbf{h}"], writes=["p4"], label=g + "_Oe")
        for si, h in enumerate(hs):
            p.task("dve", lambda e, si=si, h=h: e.tensor_scalar(
                out=self.oraw[0:64, si * 128:(si + 1) * 128],
                in0=self.p4[0:64, si * 128:(si + 1) * 128],
                scalar1=pc64(PG_GRQ, h), scalar2=None, op0=ALU.mult),
                   reads=["p4", pk], writes=["eA"], label=g + "_t2")
        for si in range(4):
            p.task("dve", lambda e, si=si: e.tensor_tensor(
                out=self.oraw[0:64, si * 128:(si + 1) * 128],
                in0=self.oraw[0:64, si * 128:(si + 1) * 128],
                in1=self.p0[0:64, si * 128:(si + 1) * 128], op=ALU.add),
                   reads=["eA", "p0"], writes=["eA"], label=g + "_oadd")
        if g.startswith("s0t0c0g0"):
            self.tap("scan_oraw", self.oraw.ap, [64, 512], F32, reads=["eA"])
        p.task("act", lambda e: e.activation(out=self.osq[0:64, :], in_=self.oraw[0:64, :],
                                             func=ACTF.Square),
               reads=["eA"], writes=["t1s"], label=g + "_osq")
        for si in range(4):
            p.task("dve", lambda e, si=si: e.reduce_sum(
                out=self.rmsacc[0:64, si:si + 1], in_=self.osq[0:64, si * 128:(si + 1) * 128],
                axis=AX.X),
                   reads=["t1s"], writes=["rms0"], label=g + "_ssum")
        p.task("dve", lambda e: e.tensor_scalar(
            out=self.rmsacc[0:64, 4:8], in0=self.rmsacc[0:64, 0:4],
            scalar1=1.0 / 128.0, scalar2=EPS_ORMS, op0=ALU.mult, op1=ALU.add),
               reads=["rms0"], writes=["rms1"], label=g + "_rmu")
        p.task("dve", lambda e: e.reciprocal(out=self.rmsacc[0:64, 8:12],
                                             in_=self.rmsacc[0:64, 4:8]),
               reads=["rms1"], writes=["rms2"], label=g + "_rrc")
        p.task("act", lambda e: e.activation(out=self.rmsacc[0:64, 12:16],
                                             in_=self.rmsacc[0:64, 8:12], func=ACTF.Sqrt),
               reads=["rms2"], writes=["rms3"], label=g + "_rsq")
        for si, h in enumerate(hs):
            p.task("dve", lambda e, si=si: e.tensor_scalar(
                out=self.on_[0:64, si * 128:(si + 1) * 128],
                in0=self.oraw[0:64, si * 128:(si + 1) * 128],
                scalar1=self.rmsacc[0:64, 12 + si:13 + si], scalar2=None, op0=ALU.mult),
                   reads=["rms3", "eA"], writes=["Ngen1"], label=g + "_on")
        for si in range(4):
            p.task("pe", lambda e, si=si: e.transpose(
                self.tbf[:, si * 128: si * 128 + 64], self.on_[0:64, si * 128:(si + 1) * 128],
                self.c_ident_bf[0:64, 0:64]),
                   reads=["Ngen1", "c_ident_bf"], writes=["tbf"], label=g + "_trO")
        for si, h in enumerate(hs):
            p.task("dve", lambda e, si=si, h=h: e.tensor_tensor(
                out=self.ofm[:, h * 512 + c * 64: h * 512 + (c + 1) * 64],
                in0=self.tbf[:, si * 128: si * 128 + 64],
                in1=self.sgbuf[:, h * 512 + c * 64: h * 512 + (c + 1) * 64], op=ALU.mult),
                   reads=["tbf", f"sg{h}"], writes=[f"ofm{h}"], label=g + "_gate")
        for si, h in enumerate(hs):
            p.task("pe", lambda e, si=si: e.matmul(
                self.p3[:, si * 128:(si + 1) * 128], self.kw[:, si * 128:(si + 1) * 128],
                u7[:, si * 128:(si + 1) * 128], start=True, stop=True,
                skip_group_check=True),
                   reads=["kw", u7k], writes=["p3"], label=g + "_dS")
        for si, h in enumerate(hs):
            p.task("dve", lambda e, si=si, h=h: e.scalar_tensor_tensor(
                out=self.S[:, h * 128:(h + 1) * 128], in0=self.S[:, h * 128:(h + 1) * 128],
                scalar=pc(PG_DECAY, h), in1=self.p3[:, si * 128:(si + 1) * 128],
                op0=ALU.mult, op1=ALU.add),
                   reads=["p3", pk, f"S{h}"], writes=[f"S{h}"], label=g + "_Snew")
        for si, h in enumerate(hs):
            p.task("dve", lambda e, h=h: e.tensor_copy(
                out=self.Sbf[:, h * 128:(h + 1) * 128], in_=self.S[:, h * 128:(h + 1) * 128]),
                   reads=[f"S{h}"], writes=[f"Sbf{h}"], label=g + "_Sbf")
        if g.startswith("s0t0c0g0"):
            self.tap("scan_S1", self.S.ap, [128, 1024], F32,
                     reads=[f"S{h}" for h in hs])

    def emit_epilogue(self, s, t, gt0, tag):
        p = self.p
        pbanks = [self.p0, self.p1]
        for m in range(16):
            pb = pbanks[m % 2]
            pkey = f"p{m % 2}"
            buf = self.wg_[m % 2]
            wkey = f"wg{m % 2}"
            wt = self.h_wt["Wo"]
            p.task("sp", lambda e, m=m, b=buf, wt=wt:
                   e.dma_start(out=b[:, 0:8 * 128].rearrange("p (k c) -> p k c", c=128),
                               in_=wt[m, :, :, :].rearrange("k p c -> p k c")),
                   reads=["hwt_Wo"], writes=[wkey], dma=wkey, label=f"{tag}_wldWo{m}")
            for k in range(8):
                p.task("pe", lambda e, k=k, b=buf, pb=pb: e.matmul(
                    pb[:, 0:512], b[:, k * 128:(k + 1) * 128],
                    self.ofm[:, k * 512:(k + 1) * 512],
                    start=(k == 0), stop=(k == 7)),
                       reads=[wkey, f"ofm{k}"], writes=[pkey], label=f"{tag}_mmWo{m}")
            p.task("dve", lambda e, m=m, pb=pb: e.tensor_tensor(
                out=self.xfm[:, m * 512:(m + 1) * 512], in0=self.xfm[:, m * 512:(m + 1) * 512],
                in1=pb[:, 0:512], op=ALU.add),
                   reads=[pkey, f"xfm{m}"], writes=[f"xfm{m}"], label=f"{tag}_res{m}")
        self.tap(f"resid_{tag}", self.xfm.ap, [128, 8192], F32,
                 reads=[f"xfm{f}" for f in range(16)])
        for f in range(16):
            sq = self.pre[f % 2][0:128, 0:512]
            yb = self.cs[f % 2]
            p.task("act", lambda e, f=f, sq=sq: e.activation(
                out=sq[:, :], in_=self.xfm[:, f * 512:(f + 1) * 512], func=ACTF.Square),
                   reads=[f"xfm{f}"], writes=[f"pre{f % 2}"], label=f"{tag}_ysq{f}")
            p.task("dve", lambda e, f=f, yb=yb: e.tensor_copy(
                out=yb[:, :], in_=self.xfm[:, f * 512:(f + 1) * 512]),
                   reads=[f"xfm{f}"], writes=[f"cs{f % 2}"], label=f"{tag}_ybf{f}")
            p.task("pe", lambda e, f=f, yb=yb: e.matmul(
                self.p5[0:1, 0:512], self.c_ones_col_bf[:, :], yb[:, :],
                start=(f == 0), stop=(f == 15), skip_group_check=True),
                   reads=[f"cs{f % 2}", "c_ones_col_bf"], writes=["p5"],
                   label=f"{tag}_stsum{f}")
            p.task("pe", lambda e, f=f, sq=sq: e.matmul(
                self.p3[0:1, 0:512], self.c_ones_col_bf[:, :], sq[:, :],
                start=(f == 0), stop=(f == 15), skip_group_check=True),
                   reads=[f"pre{f % 2}", "c_ones_col_bf"], writes=["p3"],
                   label=f"{tag}_stsq{f}")
        mu = self.r_["tmpB"][0:1, 0:512]
        m2 = self.r_["tmpC"][0:1, 0:512]
        var = self.r_["g"][0:1, 0:512]
        inv = self.r_["G"][0:1, 0:512]
        bseg = self.lnrk[0:1, 0:512]
        p.task("dve", lambda e: e.tensor_scalar(
            out=mu, in0=self.p5[0:1, 0:512], scalar1=1.0 / D, scalar2=None, op0=ALU.mult),
               reads=["p5"], writes=["row_tmpB"], label=f"{tag}_mu")
        p.task("dve", lambda e: e.tensor_scalar(
            out=m2, in0=self.p3[0:1, 0:512], scalar1=1.0 / D, scalar2=None, op0=ALU.mult),
               reads=["p3"], writes=["row_tmpC"], label=f"{tag}_m2")
        p.task("dve", lambda e: e.tensor_tensor(out=var, in0=mu, in1=mu, op=ALU.mult),
               reads=["row_tmpB"], writes=["row_g"], label=f"{tag}_musq")
        p.task("dve", lambda e: e.tensor_tensor(out=var, in0=m2, in1=var, op=ALU.subtract),
               reads=["row_tmpC", "row_g"], writes=["row_g"], label=f"{tag}_var")
        p.task("dve", lambda e: e.tensor_scalar(
            out=var, in0=var, scalar1=EPS_LN, scalar2=None, op0=ALU.add),
               reads=["row_g"], writes=["row_g"], label=f"{tag}_veps")
        p.task("dve", lambda e: e.reciprocal(out=inv, in_=var),
               reads=["row_g"], writes=["row_G"], label=f"{tag}_vrc")
        p.task("act", lambda e: e.activation(out=inv, in_=inv, func=ACTF.Sqrt),
               reads=["row_G"], writes=["row_G"], label=f"{tag}_inv")
        p.task("dve", lambda e: e.scalar_tensor_tensor(
            out=bseg, in0=inv, scalar=-1.0, in1=mu, op0=ALU.mult, op1=ALU.mult),
               reads=["row_G", "row_tmpB"], writes=["row_lnrk"], label=f"{tag}_bb")
        p.task("pe", lambda e: e.matmul(self.p3[:, 0:512], self.c_ones_row_f32[:, :],
                                        inv, start=True, stop=True,
                                        skip_group_check=True),
               reads=["row_G", "c_ones_row_f32"], writes=["p3"], label=f"{tag}_bca")
        p.task("pe", lambda e: e.matmul(self.p4[:, 0:512], self.c_ones_row_f32[:, :],
                                        bseg, start=True, stop=True,
                                        skip_group_check=True),
               reads=["row_lnrk", "c_ones_row_f32"], writes=["p4"], label=f"{tag}_bcb")
        for f in range(16):
            p.task("dve", lambda e, f=f: e.tensor_tensor(
                out=self.xfm[:, f * 512:(f + 1) * 512], in0=self.xfm[:, f * 512:(f + 1) * 512],
                in1=self.p3[:, 0:512], op=ALU.mult),
                   reads=["p3", f"xfm{f}"], writes=[f"xfm{f}"], label=f"{tag}_x1a{f}")
            p.task("dve", lambda e, f=f: e.tensor_tensor(
                out=self.xfm[:, f * 512:(f + 1) * 512], in0=self.xfm[:, f * 512:(f + 1) * 512],
                in1=self.p4[:, 0:512], op=ALU.add),
                   reads=["p4", f"xfm{f}"], writes=[f"xfm{f}"], label=f"{tag}_x1b{f}")
            p.task("dve", lambda e, f=f: e.tensor_reduce(
                out=self.amax[:, 1:2], in_=self.xfm[:, f * 512:(f + 1) * 512],
                axis=AX.X, op=ALU.max, apply_absolute_value=True),
                   reads=[f"xfm{f}"], writes=["amax1"], label=f"{tag}_am{f}")
            p.task("dve", lambda e: e.tensor_tensor(
                out=self.amax[:, 0:1], in0=self.amax[:, 0:1], in1=self.amax[:, 1:2],
                op=ALU.max),
                   reads=["amax1", "amax0"], writes=["amax0"], label=f"{tag}_ama{f}")
            p.task("sp", lambda e, f=f, gt0=gt0: e.dma_start(
                out=self.h_x1[f * 128:(f + 1) * 128, gt0:gt0 + TT],
                in_=self.xfm[:, f * 512:(f + 1) * 512]),
                   reads=[f"xfm{f}"], writes=[f"hx1_{f}_{gt0 // TT}"], dma=f"x1st{f}",
                   label=f"{tag}_x1st{f}")

    def phase_a_seq(self, s, weave=()):
        p = self.p
        weave = list(weave)
        nw = len(weave)
        ntile = self.dbg_ntile
        if s == 0:
            p.task("dve", lambda e: e.memset(self.amax[:, 0:1], 0.0), writes=["amax0"],
                   always_sync=True)
        for h in range(8):
            p.task("dve", lambda e, h=h: e.memset(self.S[:, h * 128:(h + 1) * 128], 0.0),
                   writes=[f"S{h}"], label=f"s{s}_S0_{h}", always_sync=True)
            p.task("dve", lambda e, h=h: e.memset(self.Sbf[:, h * 128:(h + 1) * 128], 0.0),
                   writes=[f"Sbf{h}"], label=f"s{s}_Sbf0_{h}", always_sync=True)
        for t in range(self.dbg_ntile):
            self.phase_a_tile(s, t)
            lo = nw * t // max(ntile, 1)
            hi = nw * (t + 1) // max(ntile, 1)
            for w in weave[lo:hi]:
                w()

    def emit_scalar_chain(self, name, emitlist, reads, writes, eng="dve"):
        for i, (fn, r, w) in enumerate(emitlist):
            self.p.task(eng, fn, reads=r, writes=w, label=f"{name}{i}")

    def emit_collective(self, name, src_col, dst_col, ccd_base):
        p = self.p
        nco = self.nc
        if getattr(self, "cc_local", False):
            return p.task("dve", lambda e: e.tensor_copy(
                out=self.scal[0:1, dst_col:dst_col + 1],
                in_=self.scal[0:1, src_col:src_col + 1]),
                          reads=[f"scol{src_col}"], writes=[f"scol{dst_col}"],
                          label=name + "_local")

        def emit(e):
            e.dma_start(out=self.h_cc_in[0:1, 0:1], in_=self.scal[0:1, src_col:src_col + 1]
                        ).then_inc(self.ccd, 16)
            e.wait_ge(self.ccd, self.ccd_count[0] + 16)
            self.ccd_count[0] += 16
            e.collective_compute(
                "AllReduce", ALU.max, replica_groups=[list(range(NCORES))],
                ins=[self.h_cc_in[0:1, 0:1]], outs=[self.h_cc_out[0:1, 0:1]]
                ).then_inc(self.ccc, 1)
            self.ccc_count[0] += 1
            e.wait_ge(self.ccc, self.ccc_count[0])
            e.dma_start(out=self.scal[0:1, dst_col:dst_col + 1], in_=self.h_cc_out[0:1, 0:1]
                        ).then_inc(self.ccd, 16)
            self.ccd_count[0] += 16
            return e.wait_ge(self.ccd, self.ccd_count[0])

        return p.task("pool", emit, reads=[f"scol{src_col}"], writes=[f"scol{dst_col}"],
                      label=name)

    def scol(self, i, width=1):
        return self.scal[0:1, i:i + width]

    def scalar_finalize_max(self, acc_col_ap, acc_key, out_col, tag):
        p = self.p
        p.task("pe", lambda e: e.transpose(self.tf32[0:1, 0:128], acc_col_ap,
                                           self.c_ident_f32[:, :]),
               reads=[acc_key, "c_ident_f32"], writes=["tf32"], label=tag + "_tr")
        p.task("dve", lambda e: e.tensor_reduce(
            out=self.scal[0:1, out_col:out_col + 1], in_=self.tf32[0:1, 0:128],
            axis=AX.X, op=ALU.max),
               reads=["tf32"], writes=[f"scol{out_col}"], label=tag + "_red")

    def bcast_col(self, src_col, dst_view, dst_key, tag, parts=128, deps=()):
        p = self.p
        p.task("pe", lambda e: e.matmul(self.tf32[:, 0:1], self.c_ones_row_f32[:, :],
                                        self.scal[0:1, src_col:src_col + 1],
                                        start=True, stop=True, skip_group_check=True),
               reads=[f"scol{src_col}", "c_ones_row_f32"], writes=["tf32"],
               label=tag + "_bc")
        p.task("dve", lambda e: e.tensor_copy(out=dst_view, in_=self.tf32[:, 0:1]),
               reads=["tf32"], writes=[dst_key], deps=deps, label=tag + "_bcv")

    def emit_wscale(self, wi, out_col, tag):
        p = self.p
        self.scalar_finalize_max(self.wmax[:, wi:wi + 1], f"wmax{wi}", 24 + wi, tag)
        p.task("dve", lambda e, wi=wi: e.tensor_scalar(
            out=self.scol(26 + wi), in0=self.scol(24 + wi), scalar1=1.0 / FP8MAX,
            scalar2=1e-6, op0=ALU.mult, op1=ALU.max),
               reads=[f"scol{24 + wi}"], writes=[f"scol{26 + wi}"], label=tag + "_sw")
        p.task("dve", lambda e, wi=wi: e.tensor_scalar(
            out=self.scol(28 + wi), in0=self.scol(26 + wi), scalar1=2.0, scalar2=None,
            op0=ALU.mult),
               reads=[f"scol{26 + wi}"], writes=[f"scol{28 + wi}"], label=tag + "_2s")
        p.task("dve", lambda e, wi=wi, out_col=out_col: e.reciprocal(
            out=self.scol(out_col), in_=self.scol(28 + wi)),
               reads=[f"scol{28 + wi}"], writes=[f"scol{out_col}"], label=tag + "_rc")

    def wquant_pieces(self, nm, rscol_view, rskey, tag):
        p = self.p
        w = self.w_in[nm]
        dst = self.h_q8bf[nm]
        R, Cc = w.ap().shape
        w2 = w.ap().rearrange("(a p) b -> a p b", p=128)
        d2 = dst.ap().rearrange("(a p) b -> a p b", p=128)
        q8 = self.q8v
        pieces = []

        def mk_piece(a, cq):
            def emit():
                idx = self._wm_idx
                self._wm_idx += 1
                buf = self.wmtile[idx % 3]
                key = f"wmt{idx % 3}"
                i = idx % 2
                p.task("pool", lambda e, a=a, cq=cq, b=buf, w2=w2: e.dma_start(
                    out=b[:, :], in_=w2[a, :, cq * 1024:(cq + 1) * 1024]),
                       writes=[key], dma=key, label=tag + "_ld")
                p.task("dve", lambda e, b=buf, i=i: e.tensor_scalar(
                    out=q8[0:128, i * 1024:(i + 1) * 1024], in0=b[:, :],
                    scalar1=rscol_view, scalar2=None, op0=ALU.mult),
                       reads=[key, rskey], writes=[f"q8scr{i}"], label=tag + "_q")
                p.task("pool", lambda e, a=a, cq=cq, i=i, d2=d2: e.dma_start(
                    out=d2[a, :, cq * 1024:(cq + 1) * 1024],
                    in_=q8[0:128, i * 1024:(i + 1) * 1024]),
                       reads=[f"q8scr{i}"], writes=[f"hq8_{nm}"], dma=f"q8st_{nm}{i}",
                       label=tag + "_st")
            return emit
        for a in range(R // 128):
            for cq in range(Cc // 1024):
                pieces.append(mk_piece(a, cq))
        return pieces

    def phase_b(self):
        p = self.p
        fB = self.emit_fence("fenceB")
        self.tap("h_x1_full", self.h_x1.ap(), [D, SEQS * L], F32,
                 reads=[], deps=(fB,))
        self.ar_off = self.a_base
        w8T = self.view("w8T", 128, NKC * FF, FP8)
        land = [self.view(f"land{i}", 128, FF, BF16) for i in range(2)]
        x1t = [self.view(f"x1t{i}", 128, 512, F32) for i in range(4)]
        x8 = self.view("x8", 128, NKC * 512, FP8)
        rsc = [self.view(f"rsc{i}", 128, 512, F32) for i in range(4)]
        mx = self.view("mxB", 128, 4, F32)
        rsx2col = self.view("rsx2col", 128, 1, F32)
        p.task("dve", lambda e: e.memset(mx[:, 0:1], 0.0), writes=["mxB0"],
               deps=(fB,), always_sync=True)
        self.bcast_col(13, rsx2col[:, :], "rsx2col", "bcx", deps=(fB,))
        for k in range(NKC):
            i = k % 2
            lb = land[i]
            p.task("sp", lambda e, k=k, lb=lb: e.dma_start(
                out=lb[:, :], in_=self.h_q8bf["wfc"][0:FF, k * 128:(k + 1) * 128],
                transpose=True),
                   reads=["hq8_wfc"], writes=[f"land{i}"], deps=(fB,), dma=f"land{i}",
                   label=f"bslab{k}")
            p.task("dve", lambda e, k=k, lb=lb: e.tensor_copy(
                out=w8T[:, k * FF:(k + 1) * FF], in_=lb[:, :]),
                   reads=[f"land{i}"], writes=[f"w8T{k}"], label=f"bslabc{k}")
        pbanks = [self.p0, self.p1]
        for tile in range(SEQS * L // TT):
            gt0 = tile * TT
            tg = f"B{tile}"
            for f in range(NKC):
                i = f % 4
                buf = x1t[i]
                p.task("sp", lambda e, f=f, gt0=gt0, b=buf: e.dma_start(
                    out=b[:, :], in_=self.h_x1[f * 128:(f + 1) * 128, gt0:gt0 + TT]),
                       reads=[f"hx1_{f}_{gt0 // TT}"], writes=[f"x1t{i}"], dma=f"x1t{i}",
                       label=tg + f"_xld{f}")
                p.task("dve", lambda e, f=f, b=buf: e.tensor_scalar(
                    out=x8[:, f * 512:(f + 1) * 512], in0=b[:, :],
                    scalar1=rsx2col[:, :], scalar2=None, op0=ALU.mult),
                       reads=[f"x1t{i}", "rsx2col"], writes=[f"x8_{f}"],
                       label=tg + f"_q{f}")
            for m in range(FF // 128):
                pb = pbanks[m % 2]
                pkey = f"p{m % 2}"
                for k in range(NKC):
                    p.task("pe", lambda e, m=m, k=k, pb=pb: e.matmul(
                        pb[:, 0:512], w8T[:, k * FF + m * 128: k * FF + (m + 1) * 128],
                        x8[:, k * 512:(k + 1) * 512],
                        start=(k == 0), stop=(k == NKC - 1)),
                           reads=[f"w8T{k}", f"x8_{k}"], writes=[pkey],
                           label=tg + f"_mm{m}")
                i = m % 4
                rb = rsc[i]
                p.task("act", lambda e, pb=pb, rb=rb: e.activation(
                    out=rb[:, :], in_=pb[:, 0:512], func=ACTF.Relu),
                       reads=[pkey], writes=[f"rsc{i}"], label=tg + f"_relu{m}")
                p.task("dve", lambda e, rb=rb: e.tensor_reduce(
                    out=mx[:, 1:2], in_=rb[:, :], axis=AX.X, op=ALU.max),
                       reads=[f"rsc{i}"], writes=["mxB1"], label=tg + f"_mx{m}")
                p.task("dve", lambda e: e.tensor_tensor(
                    out=mx[:, 0:1], in0=mx[:, 0:1], in1=mx[:, 1:2], op=ALU.max),
                       reads=["mxB1", "mxB0"], writes=["mxB0"], label=tg + f"_mxa{m}")
                p.task("act", lambda e, m=m, gt0=gt0, rb=rb: e.dma_start(
                    out=self.h_r[m * 128:(m + 1) * 128, gt0:gt0 + TT], in_=rb[:, :]),
                       reads=[f"rsc{i}"], writes=[f"hr_{m}_{gt0 // TT}"], dma=f"rst{i}",
                       label=tg + f"_rst{m}")
        self.scalar_finalize_max(mx[:, 0:1], "mxB0", 16, "mxr")
        self.emit_collective("cc_sh", 16, 17, 1)
        p.task("dve", lambda e: e.tensor_tensor(
            out=self.scol(18), in0=self.scol(12), in1=self.scol(26), op=ALU.mult),
               reads=["scol12", "scol26"], writes=["scol18"], label="c_mul")
        p.task("dve", lambda e: e.tensor_scalar(
            out=self.scol(18), in0=self.scol(18), scalar1=4.0, scalar2=None, op0=ALU.mult),
               reads=["scol18"], writes=["scol18"], label="c_4")
        p.task("dve", lambda e: e.tensor_tensor(
            out=self.scol(24), in0=self.scol(17), in1=self.scol(18), op=ALU.mult),
               reads=["scol17", "scol18"], writes=["scol24"], label="mh1")
        p.task("dve", lambda e: e.tensor_tensor(
            out=self.scol(24), in0=self.scol(24), in1=self.scol(24), op=ALU.mult),
               reads=["scol24"], writes=["scol24"], label="mh2")
        p.task("dve", lambda e: e.tensor_scalar(
            out=self.scol(19), in0=self.scol(24), scalar1=1.0 / FP8MAX, scalar2=1e-6,
            op0=ALU.mult, op1=ALU.max),
               reads=["scol24"], writes=["scol19"], label="sh")
        p.task("dve", lambda e: e.tensor_tensor(
            out=self.scol(25), in0=self.scol(18), in1=self.scol(18), op=ALU.mult),
               reads=["scol18"], writes=["scol25"], label="kq1")
        p.task("dve", lambda e: e.tensor_scalar(
            out=self.scol(30), in0=self.scol(19), scalar1=2.0, scalar2=None, op0=ALU.mult),
               reads=["scol19"], writes=["scol30"], label="kq2")
        p.task("dve", lambda e: e.reciprocal(out=self.scol(31), in_=self.scol(30)),
               reads=["scol30"], writes=["scol31"], label="kq3")
        p.task("dve", lambda e: e.tensor_tensor(
            out=self.scol(20), in0=self.scol(25), in1=self.scol(31), op=ALU.mult),
               reads=["scol25", "scol31"], writes=["scol20"], label="kq4")
        p.task("dve", lambda e: e.tensor_tensor(
            out=self.scol(21), in0=self.scol(19), in1=self.scol(27), op=ALU.mult),
               reads=["scol19", "scol27"], writes=["scol21"], label="c21")
        p.task("dve", lambda e: e.tensor_scalar(
            out=self.scol(21), in0=self.scol(21), scalar1=4.0, scalar2=None, op0=ALU.mult),
               reads=["scol21"], writes=["scol21"], label="c22")

    def emit_fence(self, name):
        p = self.p
        exempt = ("hbf_", "hwt_", "hq8_", "scol", "c_", "convw", "wba", "dtb",
                  "onw", "nega", "alog")
        keys = [k for k in list(p.buf_writer.keys())
                if not any(k.startswith(x) for x in exempt)]
        return p.task("dve", lambda e: e.engine_nop(), writes=keys, label=name)

    def phase_ab_transition(self):
        p = self.p
        self.scalar_finalize_max(self.amax[:, 0:1], "amax0", 10, "amx")
        self.emit_collective("cc_sx", 10, 11, 0)
        p.task("dve", lambda e: e.tensor_scalar(
            out=self.scol(12), in0=self.scol(11), scalar1=1.0 / FP8MAX, scalar2=1e-6,
            op0=ALU.mult, op1=ALU.max),
               reads=["scol11"], writes=["scol12"], label="sx")
        p.task("dve", lambda e: e.tensor_scalar(
            out=self.scol(29), in0=self.scol(12), scalar1=2.0, scalar2=None, op0=ALU.mult),
               reads=["scol12"], writes=["scol29"], label="sx2")
        p.task("dve", lambda e: e.reciprocal(out=self.scol(13), in_=self.scol(29)),
               reads=["scol29"], writes=["scol13"], label="rsx2")

    def phase_c(self):
        p = self.p
        fC = self.emit_fence("fenceC")
        self.tap("h_r_full", self.h_r.ap(), [FF, SEQS * L], F32,
                 reads=[], deps=(fC,))
        self.ar_off = self.a_base
        wp8T = self.view("wp8T", 128, 32 * D, FP8)
        landc = [self.view(f"landc{i}", 128, D, BF16) for i in range(2)]
        h8 = self.view("h8", 128, 32 * 512, FP8)
        rt = [self.view(f"rt{i}", 128, 512, F32) for i in range(4)]
        pc = [self.view(f"pc{i}", 128, 512, F32) for i in range(4)]
        x1c = [self.view(f"x1c{i}", 128, 512, F32) for i in range(4)]
        y2 = self.view("y2", 128, 16 * 512, F32)
        otm = [self.view(f"otm{i}", 128, D, F32) for i in range(2)]
        kqcol = self.view("kqcol", 128, 1, F32)
        c2col = self.view("c2col", 128, 1, F32)
        csA = self.view("csA", 128, 512, BF16)
        csB = self.view("csB", 128, 512, BF16)
        sqA = self.view("sqA", 128, 512, BF16)
        sqB = self.view("sqB", 128, 512, BF16)
        stc = self.view("stc", 1, 3072, F32)
        self.bcast_col(20, kqcol[:, :], "kqcol", "bkq", deps=(fC,))
        self.bcast_col(21, c2col[:, :], "c2col", "bc2", deps=(fC,))
        pbanks = [self.p0, self.p1]
        NT8 = SEQS * L // TT
        for half in range(2):
            hk0 = half * 32
            for kk in range(32):
                k = hk0 + kk
                i = kk % 2
                lb = landc[i]
                p.task("sp", lambda e, k=k, lb=lb: e.dma_start(
                    out=lb[:, :], in_=self.h_q8bf["wproj"][0:D, k * 128:(k + 1) * 128],
                    transpose=True),
                       reads=["hq8_wproj"], writes=[f"landc{i}"], deps=(fC,),
                       dma=f"landc{i}", label=f"cslab{k}")
                p.task("dve", lambda e, kk=kk, lb=lb: e.tensor_copy(
                    out=wp8T[:, kk * D:(kk + 1) * D], in_=lb[:, :]),
                       reads=[f"landc{i}"], writes=[f"wp8T{kk}"], label=f"cslabc{kk}")
            for tile in range(NT8):
                gt0 = tile * TT
                tg = f"C{half}_{tile}"
                for kk in range(32):
                    k = hk0 + kk
                    i = kk % 4
                    buf = rt[i]
                    p.task("sp", lambda e, k=k, gt0=gt0, b=buf: e.dma_start(
                        out=b[:, :], in_=self.h_r[k * 128:(k + 1) * 128, gt0:gt0 + TT]),
                           reads=[f"hr_{k}_{gt0 // TT}"], writes=[f"rt{i}"], dma=f"rt{i}",
                           label=tg + f"_rld{kk}")
                    p.task("dve", lambda e, kk=kk, b=buf: e.scalar_tensor_tensor(
                        out=h8[:, kk * 512:(kk + 1) * 512], in0=b[:, :],
                        scalar=kqcol[:, :], in1=b[:, :], op0=ALU.mult, op1=ALU.mult),
                           reads=[f"rt{i}", "kqcol"], writes=[f"h8_{kk}"],
                           label=tg + f"_hq{kk}")
                for m in range(NKC):
                    pb = pbanks[m % 2]
                    pkey = f"p{m % 2}"
                    for kk in range(32):
                        p.task("pe", lambda e, m=m, kk=kk, pb=pb: e.matmul(
                            pb[:, 0:512], wp8T[:, kk * D + m * 128: kk * D + (m + 1) * 128],
                            h8[:, kk * 512:(kk + 1) * 512],
                            start=(kk == 0), stop=(kk == 31)),
                               reads=[f"wp8T{kk}", f"h8_{kk}"], writes=[pkey],
                               label=tg + f"_mm{m}")
                    if half == 0:
                        i = m % 4
                        ob = pc[i]
                        p.task("dve", lambda e, pb=pb, ob=ob: e.tensor_copy(
                            out=ob[:, :], in_=pb[:, 0:512]),
                               reads=[pkey], writes=[f"pc{i}"], label=tg + f"_pev{m}")
                        p.task("act", lambda e, m=m, gt0=gt0, ob=ob: e.dma_start(
                            out=self.h_pc[m * 128:(m + 1) * 128, gt0:gt0 + TT], in_=ob[:, :]),
                               reads=[f"pc{i}"], writes=[f"hpc_{m}_{tile}"], dma=f"pcst{i}",
                               label=tg + f"_pst{m}")
                    else:
                        i = m % 4
                        ob = pc[i]
                        xb = x1c[i]
                        p.task("sp", lambda e, m=m, gt0=gt0, ob=ob: e.dma_start(
                            out=ob[:, :], in_=self.h_pc[m * 128:(m + 1) * 128, gt0:gt0 + TT]),
                               reads=[f"hpc_{m}_{tile}"], writes=[f"pc{i}"], dma=f"pcld{i}",
                               label=tg + f"_pld{m}")
                        p.task("sp", lambda e, m=m, gt0=gt0, xb=xb: e.dma_start(
                            out=xb[:, :], in_=self.h_x1[m * 128:(m + 1) * 128, gt0:gt0 + TT]),
                               reads=[f"hx1_{m}_{gt0 // TT}"], writes=[f"x1c{i}"],
                               dma=f"x1c{i}", label=tg + f"_xld{m}")
                        p.task("dve", lambda e, m=m, pb=pb, ob=ob: e.tensor_tensor(
                            out=y2[:, m * 512:(m + 1) * 512], in0=pb[:, 0:512],
                            in1=ob[:, :], op=ALU.add),
                               reads=[pkey, f"pc{i}"], writes=[f"y2_{m}"],
                               label=tg + f"_acc{m}")
                        p.task("dve", lambda e, m=m: e.tensor_scalar(
                            out=y2[:, m * 512:(m + 1) * 512], in0=y2[:, m * 512:(m + 1) * 512],
                            scalar1=c2col[:, :], scalar2=None, op0=ALU.mult),
                               reads=["c2col", f"y2_{m}"], writes=[f"y2_{m}"],
                               label=tg + f"_dq{m}")
                        p.task("dve", lambda e, m=m, xb=xb: e.tensor_tensor(
                            out=y2[:, m * 512:(m + 1) * 512], in0=y2[:, m * 512:(m + 1) * 512],
                            in1=xb[:, :], op=ALU.add),
                               reads=[f"x1c{i}", f"y2_{m}"], writes=[f"y2_{m}"],
                               label=tg + f"_res{m}")
                if half == 1:
                    self.emit_ln2_out(tile, gt0, y2, csA, csB, sqA, sqB, stc, otm, tg)

    def emit_ln2_out(self, tile, gt0, y2, csA, csB, sqA, sqB, stc, otm, tg):
        p = self.p
        for f in range(NKC):
            sq = (sqA, sqB)[f % 2]
            yb = (csA, csB)[f % 2]
            sqk = ("sqA", "sqB")[f % 2]
            ybk = ("csA", "csB")[f % 2]
            p.task("act", lambda e, f=f, sq=sq: e.activation(
                out=sq[:, :], in_=y2[:, f * 512:(f + 1) * 512], func=ACTF.Square),
                   reads=[f"y2_{f}"], writes=[sqk], label=tg + f"_ysq{f}")
            p.task("dve", lambda e, f=f, yb=yb: e.tensor_copy(
                out=yb[:, :], in_=y2[:, f * 512:(f + 1) * 512]),
                   reads=[f"y2_{f}"], writes=[ybk], label=tg + f"_ybf{f}")
            p.task("pe", lambda e, f=f, yb=yb: e.matmul(
                self.p5[0:1, 0:512], self.c_ones_col_bf[:, :], yb[:, :],
                start=(f == 0), stop=(f == 15), skip_group_check=True),
                   reads=[ybk, "c_ones_col_bf"], writes=["p5"], label=tg + f"_ss{f}")
            p.task("pe", lambda e, f=f, sq=sq: e.matmul(
                self.p3[0:1, 0:512], self.c_ones_col_bf[:, :], sq[:, :],
                start=(f == 0), stop=(f == 15), skip_group_check=True),
                   reads=[sqk, "c_ones_col_bf"], writes=["p3"], label=tg + f"_sq{f}")
        st = stc
        p.task("dve", lambda e: e.tensor_scalar(
            out=st[0:1, 0:512], in0=self.p5[0:1, 0:512], scalar1=1.0 / D, scalar2=None,
            op0=ALU.mult), reads=["p5"], writes=["ct_mu"], label=tg + "_mu")
        p.task("dve", lambda e: e.tensor_scalar(
            out=st[0:1, 512:1024], in0=self.p3[0:1, 0:512], scalar1=1.0 / D, scalar2=None,
            op0=ALU.mult), reads=["p3"], writes=["ct_m2"], label=tg + "_m2")
        p.task("dve", lambda e: e.tensor_tensor(
            out=st[0:1, 1024:1536], in0=st[0:1, 0:512], in1=st[0:1, 0:512], op=ALU.mult),
               reads=["ct_mu"], writes=["ct_t"], label=tg + "_musq")
        p.task("dve", lambda e: e.tensor_tensor(
            out=st[0:1, 1024:1536], in0=st[0:1, 512:1024], in1=st[0:1, 1024:1536],
            op=ALU.subtract), reads=["ct_m2", "ct_t"], writes=["ct_t"], label=tg + "_var")
        p.task("dve", lambda e: e.tensor_scalar(
            out=st[0:1, 1024:1536], in0=st[0:1, 1024:1536], scalar1=EPS_LN, scalar2=None,
            op0=ALU.add), reads=["ct_t"], writes=["ct_t"], label=tg + "_veps")
        p.task("dve", lambda e: e.reciprocal(out=st[0:1, 1536:2048], in_=st[0:1, 1024:1536]),
               reads=["ct_t"], writes=["ct_rc"], label=tg + "_vrc")
        p.task("act", lambda e: e.activation(out=st[0:1, 1536:2048], in_=st[0:1, 1536:2048],
                                             func=ACTF.Sqrt),
               reads=["ct_rc"], writes=["ct_rc"], label=tg + "_inv")
        p.task("dve", lambda e: e.scalar_tensor_tensor(
            out=st[0:1, 2048:2560], in0=st[0:1, 1536:2048], scalar=-1.0, in1=st[0:1, 0:512],
            op0=ALU.mult, op1=ALU.mult),
               reads=["ct_rc", "ct_mu"], writes=["ct_b"], label=tg + "_bb")
        p.task("pe", lambda e: e.matmul(self.p3[:, 0:512], self.c_ones_row_f32[:, :],
                                        st[0:1, 1536:2048], start=True, stop=True,
                                        skip_group_check=True),
               reads=["ct_rc", "c_ones_row_f32"], writes=["p3"], label=tg + "_bca")
        p.task("pe", lambda e: e.matmul(self.p4[:, 0:512], self.c_ones_row_f32[:, :],
                                        st[0:1, 2048:2560], start=True, stop=True,
                                        skip_group_check=True),
               reads=["ct_b", "c_ones_row_f32"], writes=["p4"], label=tg + "_bcb")
        for f in range(NKC):
            p.task("dve", lambda e, f=f: e.tensor_tensor(
                out=y2[:, f * 512:(f + 1) * 512], in0=y2[:, f * 512:(f + 1) * 512],
                in1=self.p3[:, 0:512], op=ALU.mult),
                   reads=["p3", f"y2_{f}"], writes=[f"y2_{f}"], label=tg + f"_na{f}")
            p.task("dve", lambda e, f=f: e.tensor_tensor(
                out=y2[:, f * 512:(f + 1) * 512], in0=y2[:, f * 512:(f + 1) * 512],
                in1=self.p4[:, 0:512], op=ALU.add),
                   reads=["p4", f"y2_{f}"], writes=[f"y2_{f}"], label=tg + f"_nb{f}")
        s = gt0 // L
        l0 = gt0 % L
        for b in range(4):
            ob = otm[b % 2]
            obk = f"otm{b % 2}"
            for f in range(NKC):
                p.task("pe", lambda e, f=f, b=b: e.transpose(
                    self.tf32[:, (f % 4) * 128:(f % 4 + 1) * 128],
                    y2[:, f * 512 + b * 128: f * 512 + (b + 1) * 128],
                    self.c_ident_f32[:, :]),
                       reads=[f"y2_{f}", "c_ident_f32"], writes=["tf32"],
                       label=tg + f"_otr{b}_{f}")
                if f % 4 == 3:
                    p.task("dve", lambda e, f=f, ob=ob: e.tensor_copy(
                        out=ob[:, (f - 3) * 128:(f + 1) * 128], in_=self.tf32[:, 0:512]),
                           reads=["tf32"], writes=[obk], label=tg + f"_oev{b}_{f}")
            p.task("act", lambda e, s=s, l0=l0, b=b, ob=ob: e.dma_start(
                out=self.out[s, l0 + b * 128: l0 + (b + 1) * 128, :], in_=ob[:, :]),
                   reads=[obk], writes=[f"outst_{tile}_{b}"], dma=f"outst{b % 2}",
                   label=tg + f"_ost{b}")

    def finalize(self):
        p = self.p
        self.tap("scal_final", self.scal[0:1, 0:32], [1, 32], F32,
                 reads=[f"scol{i}" for i in (10, 11, 12, 13, 16, 17, 18, 19, 20, 21)
                        if f"scol{i}" in p.buf_writer])
        keys = [k for k in p.buf_writer.keys() if k.startswith("outst_")]
        p.task("sp", lambda e: e.nop(), reads=keys, deps=self.tap_list,
               label="final_fence")

    def build(self, phases="abc", ntile=NTILE, nseq=SEQS):
        self.dbg_ntile = ntile
        self.dbg_nseq = nseq
        self.setup_memory()
        self.declare_io()
        self.setup_views()
        self.tap_list = []
        self.prologue()
        wv0 = ()
        wq_pj = []
        if "b" in phases:
            self.rswfc = View(self.wmax.ap[:, 2:4], "rswfc")
            n_fc = (FF // 128) * (D // 1024)
            wm_fc, wm_pj = self.wm_pieces[:n_fc], self.wm_pieces[n_fc:]

            def sc_fc():
                self.emit_wscale(0, 6, "wsc_fc")
                self.bcast_col(6, self.rswfc[:, 0:1], "rswfc0", "br6")

            def sc_pj():
                self.emit_wscale(1, 7, "wsc_pj")
                self.bcast_col(7, self.rswfc[:, 1:2], "rswfc1", "br7")
            wq_fc = self.wquant_pieces("wfc", self.rswfc[:, 0:1], "rswfc0", "qfc")
            wq_pj = self.wquant_pieces("wproj", self.rswfc[:, 1:2], "rswfc1", "qpj")
            inter = [x for pair in zip(wm_pj, wq_fc) for x in pair]
            wv0 = wm_fc + [sc_fc] + inter + [sc_pj]
        self.phase_a_seq(0, weave=wv0)
        if nseq > 1:
            self.phase_a_seq(1, weave=wq_pj)
        else:
            for w in wq_pj:
                w()
        if "b" in phases:
            self.phase_ab_transition()
            self.phase_b()
        if "c" in phases:
            self.phase_c()
        self.finalize()




ENG2BLOCK = {"pe": "tensor", "act": "scalar", "dve": "vector", "pool": "gpsimd",
             "sp": "sync"}


def emit_program(bld):
    p = bld.p
    nc = bld.nc
    bulk = {"cld"}
    per_eng = p.plan()
    final_dma = dict(p.dma_sem_count)
    sems = {}
    for e in Prog.ENGS:
        sems[("eng", e)] = nc.alloc_semaphore("sem_" + e)
    for name in p.dma_sem_count:
        sems[("dma", name)] = nc.alloc_semaphore("dsem_" + name)
    bld.ccd = nc.alloc_semaphore("ccd")
    bld.ccc = nc.alloc_semaphore("ccc")
    bld.ccd_count = [0]
    bld.ccc_count = [0]

    def stream(ename):
        def emitter(eng):
            for t in per_eng[ename]:
                for (key, val) in t.waits:
                    if key[0] == "dma" and key[1] in bulk:
                        val = 16 * final_dma[key[1]]
                    eng.wait_ge(sems[key], val)
                last = t.emit(eng)
                if t.dma_sem is not None:
                    last.then_inc(sems[("dma", t.dma_sem)], 16)
                elif t.milestone is not None:
                    last.then_inc(sems[("eng", t.eng)], 1)
        return emitter

    with nc.Block() as block:
        block.tensor(stream("pe"))
        block.scalar(stream("act"))
        block.vector(stream("dve"))
        block.gpsimd(stream("pool"))
        block.sync(stream("sp"))
    return per_eng


def build_program(taps=(), phases="abc", ntile=NTILE, nseq=SEQS,
                  paranoid=frozenset({"dve"}), max_tasks=None, cc_local=False):
    nc = bass.Bass("TRN2", target_bir_lowering=False, debug=False,
                   detect_race_conditions=False)
    bld = Builder(nc, taps, paranoid=paranoid)
    bld.cc_local = cc_local
    bld.build(phases=phases, ntile=ntile, nseq=nseq)
    if max_tasks is not None:
        bld.p.tasks = bld.p.tasks[:max_tasks]
    emit_program(bld)
    return nc, bld



_CACHE = {}


def _in_maps(inputs):
    x = np.ascontiguousarray(inputs["L0_attn_x"])
    base = {
        "Wq": inputs["L0_attn_Wq"], "Wk": inputs["L0_attn_Wk"],
        "Wv": inputs["L0_attn_Wv"], "Wb": inputs["L0_attn_Wb"],
        "Wa": inputs["L0_attn_Wa"], "Wg": inputs["L0_attn_Wg"],
        "Wo": inputs["L0_attn_Wo"],
        "A_log": np.ascontiguousarray(inputs["L0_attn_A_log"]).reshape(8, 1),
        "dt_bias": np.ascontiguousarray(inputs["L0_attn_dt_bias"]).reshape(8, 1),
        "conv_q": inputs["L0_attn_conv_q"], "conv_k": inputs["L0_attn_conv_k"],
        "conv_v": inputs["L0_attn_conv_v"],
        "o_norm_w": np.ascontiguousarray(inputs["L0_attn_o_norm_w"]).reshape(128, 1),
        "wfc": inputs["L0_ffn_w_fc"], "wproj": inputs["L0_ffn_w_proj"],
    }
    base = {k: np.ascontiguousarray(v) for k, v in base.items()}
    for nm, arr in CONSTS.items():
        base["c" + nm] = arr
    maps = []
    for c in range(NCORES):
        m = dict(base)
        m["x"] = np.ascontiguousarray(x[SEQS * c:SEQS * (c + 1)])
        maps.append(m)
    return maps


def kernel(**inputs):
    if "nc" not in _CACHE:
        _CACHE["nc"] = build_program()[0]
    nc = _CACHE["nc"]
    maps = _in_maps(inputs)
    res = run_bass_kernel_spmd(nc, maps, core_ids=list(range(NCORES)))
    out = np.concatenate([res.results[c]["out"] for c in range(NCORES)], axis=0)
    return out.astype(np.float32)


# version 10
# speedup vs baseline: 1.0257x; 1.0257x over previous
import numpy as np
import ml_dtypes
import concourse.bass as bass
import concourse.mybir as mybir
from concourse.bass_utils import run_bass_kernel_spmd

F32 = mybir.dt.float32
BF16 = mybir.dt.bfloat16
FP8 = mybir.dt.float8e4
ALU = mybir.AluOpType
ACTF = mybir.ActivationFunctionType
AX = mybir.AxisListType
BF = ml_dtypes.bfloat16

B, L, D = 16, 2048, 2048
NCORES = 8
SEQS = 2
H, DH, NH = 8, 128, 2
KD = H * DH
TT = 512
NTILE = L // TT
CH = 64
CPT = TT // CH
FF = 4 * D
NKC = D // 128
EPS_LN = 1e-5
EPS_ORMS = 1e-5
FP8MAX = 448.0

PG_BETA, PG_PA, PG_PW, PG_MQ, PG_GEND, PG_NBGR, PG_W, PG_DECAY, PG_GRQ = range(9)
NPG = 9



class Task:
    __slots__ = ("tid", "eng", "emit", "deps", "milestone", "dma_sem", "dma_val",
                 "snapshot", "waits", "need_inc", "label", "always_sync")

    def __init__(self, tid, eng, emit, label=""):
        self.always_sync = False
        self.tid = tid
        self.eng = eng
        self.emit = emit
        self.deps = []
        self.milestone = None
        self.dma_sem = None
        self.dma_val = None
        self.snapshot = None
        self.waits = None
        self.need_inc = False
        self.label = label


class Prog:
    ENGS = ("pe", "act", "dve", "pool", "sp")

    def __init__(self):
        self.paranoid = False
        self.tasks = []
        self.buf_writer = {}
        self.buf_readers = {}
        self.dma_sem_count = {}

    def new_dma_sem(self, name):
        assert name not in self.dma_sem_count
        self.dma_sem_count[name] = 0
        return name

    def task(self, eng, emit, reads=(), writes=(), deps=(), dma=None, label="",
             always_sync=False):
        t = Task(len(self.tasks), eng, emit, label)
        par = self.paranoid
        if isinstance(par, (set, frozenset, list, tuple)):
            par = eng in par
        t.always_sync = always_sync or bool(par)
        dd = set()
        for d in deps:
            if d is not None:
                dd.add(d.tid if isinstance(d, Task) else int(d))
        for k in reads:
            w = self.buf_writer.get(k)
            if w is not None:
                dd.add(w)
        for k in writes:
            w = self.buf_writer.get(k)
            if w is not None:
                dd.add(w)
            for r in self.buf_readers.get(k, ()):
                dd.add(r)
        dd.discard(t.tid)
        t.deps = sorted(dd)
        if dma is not None:
            if dma not in self.dma_sem_count:
                self.new_dma_sem(dma)
            t.dma_sem = dma
            self.dma_sem_count[dma] += 1
            t.dma_val = 16 * self.dma_sem_count[dma]
        for k in reads:
            self.buf_readers.setdefault(k, []).append(t.tid)
        for k in writes:
            self.buf_writer[k] = t.tid
            self.buf_readers[k] = []
        self.tasks.append(t)
        return t

    def plan(self):
        for t in self.tasks:
            for d in t.deps:
                dt = self.tasks[d]
                if dt.dma_sem is None and (dt.eng != t.eng or dt.always_sync):
                    dt.need_inc = True
        counts = {e: 0 for e in self.ENGS}
        for t in self.tasks:
            if t.dma_sem is None and t.need_inc:
                counts[t.eng] += 1
                t.milestone = counts[t.eng]
        clocks = {e: {} for e in self.ENGS}

        def merge(dst, src):
            for k, v in src.items():
                if dst.get(k, 0) < v:
                    dst[k] = v

        per_eng = {e: [] for e in self.ENGS}
        for t in self.tasks:
            clk = clocks[t.eng]
            waits = []
            for d in t.deps:
                dt = self.tasks[d]
                if dt.dma_sem is not None:
                    key = ("dma", dt.dma_sem)
                    if clk.get(key, 0) < dt.dma_val:
                        waits.append((key, dt.dma_val))
                        clk[key] = dt.dma_val
                        merge(clk, dt.snapshot)
                elif dt.eng == t.eng and not dt.always_sync:
                    merge(clk, dt.snapshot)
                else:
                    key = ("eng", dt.eng)
                    if clk.get(key, 0) < dt.milestone:
                        waits.append((key, dt.milestone))
                        clk[key] = dt.milestone
                        merge(clk, dt.snapshot)
            t.waits = waits
            snap = dict(clk)
            if t.dma_sem is not None:
                k = ("dma", t.dma_sem)
                if snap.get(k, 0) < t.dma_val:
                    snap[k] = t.dma_val
            elif t.milestone is not None:
                snap[("eng", t.eng)] = t.milestone
            t.snapshot = snap
            per_eng[t.eng].append(t)
        return per_eng



def build_consts():
    c = {}
    c["ident_bf"] = np.eye(128, dtype=BF)
    c["ident_f32"] = np.eye(128, dtype=np.float32)
    c["ones_row_f32"] = np.ones((1, 128), np.float32)
    oh = np.zeros((128, 64), BF)
    for h in range(8):
        oh[:, 8 * h + h] = 1
    c["onehot8"] = oh
    c["ones_col_bf"] = np.ones((128, 1), BF)
    NEG = -1e30
    sub = np.array([2 * i if i < 64 else 2 * (i - 64) + 1 for i in range(128)])
    maskA = np.where(sub[None, :] < sub[:, None], 0.0, NEG).astype(np.float32)
    c["maskA"] = maskA
    subo = 2 * np.arange(64) + 1
    maskWT = np.where(sub[:, None] <= subo[None, :], 0.0, NEG).astype(np.float32)
    c["maskWT"] = maskWT
    return c


CONSTS = build_consts()
C_SHAPES = {k: (list(v.shape), F32 if v.dtype == np.float32 else BF16)
            for k, v in CONSTS.items()}



class View:

    def __init__(self, ap, key):
        self.ap = ap
        self.key = key

    def __getitem__(self, idx):
        return self.ap[idx]

    @property
    def full(self):
        return self.ap


class Builder:
    def __init__(self, nc, taps=(), paranoid=False):
        self.nc = nc
        self.p = Prog()
        self.p.paranoid = paranoid
        self.taps = set(taps)
        self.tap_tensors = {}
        self.views = {}
        self.hbm = {}

    def setup_memory(self):
        nc = self.nc
        self.p0 = nc.alloc_psum_tensor("p0", [128, 512], F32)
        self.p1 = nc.alloc_psum_tensor("p1", [128, 512], F32)
        self.p2 = nc.alloc_psum_tensor("p2", [128, 512], F32)
        self.p3 = nc.alloc_psum_tensor("p3", [128, 512], F32)
        self.p4 = nc.alloc_psum_tensor("p4", [128, 512], F32)
        self.p5 = nc.alloc_psum_tensor("p5", [128, 512], F32)
        self.tbf = nc.alloc_psum_tensor("tbf", [128, 1024], BF16)
        self.tf32 = nc.alloc_psum_tensor("tf32", [128, 512], F32)

        for name, (shape, dt) in C_SHAPES.items():
            setattr(self, "c_" + name, nc.alloc_sbuf_tensor("c_" + name, shape, dt))
        self.convw = nc.alloc_sbuf_tensor("convw", [128, 40 * 4], F32)
        self.wba = nc.alloc_sbuf_tensor("wba", [128, NKC * 24], BF16)
        self.dtb_col = nc.alloc_sbuf_tensor("dtb_col", [8, 1], F32)
        self.nega_col = nc.alloc_sbuf_tensor("nega_col", [8, 1], F32)
        self.onw_col = nc.alloc_sbuf_tensor("onw_col", [128, 1], F32)
        self.scal = nc.alloc_sbuf_tensor("scal", [128, 32], F32)
        self.q8scr = nc.alloc_sbuf_tensor("q8scr", [128, 1024], FP8)
        rem = nc.sbuf_bytes_remaining
        arn = rem // 2
        self.AR = nc.alloc_sbuf_tensor("AR", [128, arn], BF16)
        self.ar_off = 0
        self.ar_n = arn

    def view(self, name, parts, n, dtype, offset=None):
        if dtype == F32:
            units = 2 * n
        elif dtype == FP8:
            units = (n + 1) // 2
        else:
            units = n
        units = (units + 31) & ~31
        if offset is None:
            offset = self.ar_off
            self.ar_off += units
            assert self.ar_off <= self.ar_n, f"arena overflow at {name} ({self.ar_off}/{self.ar_n})"
        ap = self.AR[0:parts, offset:offset + units]
        if dtype == F32:
            ap = ap.bitcast(F32)[0:parts, 0:n]
        elif dtype == FP8:
            ap = ap.bitcast(FP8)[0:parts, 0:n]
        else:
            ap = ap[0:parts, 0:n]
        v = View(ap, name)
        self.views[name] = v
        return v

    def dram(self, name, shape, dtype, kind=None):
        if kind:
            h = self.nc.dram_tensor(name, shape, dtype, kind=kind)
        else:
            h = self.nc.dram_tensor(name, shape, dtype)
        self.hbm[name] = h
        return h

    def tap(self, name, src_ap, shape, dtype, reads, deps=()):
        if name not in self.taps:
            return
        t = self.dram("tap_" + name, shape, dtype, kind="ExternalOutput")
        tk = self.p.task("act", lambda e, t=t, s=src_ap: e.dma_start(out=t.ap(), in_=s),
                         reads=list(reads), deps=list(deps), dma="tap_" + name,
                         label="tap_" + name)
        self.tap_list.append(tk)
        self.tap_tensors[name] = t

    def declare_io(self):
        d = self.dram
        self.x_in = d("x", [SEQS, L, D], F32, kind="ExternalInput")
        self.w_in = {}
        for nm, shp in [("Wq", [KD, D]), ("Wk", [2 * KD, D]), ("Wv", [2 * KD, D]),
                        ("Wb", [16, D]), ("Wa", [8, D]), ("Wg", [KD, D]),
                        ("Wo", [D, KD]), ("wfc", [FF, D]), ("wproj", [D, FF])]:
            self.w_in[nm] = d(nm, shp, F32, kind="ExternalInput")
        for nm in ["A_log", "dt_bias"]:
            self.w_in[nm] = d(nm, [8, 1], F32, kind="ExternalInput")
        self.w_in["o_norm_w"] = d("o_norm_w", [128, 1], F32, kind="ExternalInput")
        for nm in ["conv_q", "conv_k", "conv_v"]:
            shp = [KD, 4] if nm == "conv_q" else [2 * KD, 4]
            self.w_in[nm] = d(nm, shp, F32, kind="ExternalInput")
        for nm, (shape, dt) in C_SHAPES.items():
            self.w_in[nm] = d("c" + nm, shape, dt, kind="ExternalInput")
        self.out = d("out", [SEQS, L, D], F32, kind="ExternalOutput")
        self.h_wbf = {}
        for nm, shp in [("Wq", [KD, D]), ("Wk", [2 * KD, D]), ("Wv", [2 * KD, D]),
                        ("Wg", [KD, D]), ("Wo", [D, KD])]:
            self.h_wbf[nm] = d("bf_" + nm, shp, BF16)
        self.h_wt = {}
        for nm, O, IN in [("Wq", KD, D), ("Wk", 2 * KD, D), ("Wv", 2 * KD, D),
                          ("Wg", KD, D), ("Wo", D, KD)]:
            self.h_wt[nm] = d("wt_" + nm, [O // 128, IN // 128, 128, 128], BF16)
        self.h_x1 = d("x1h", [D, SEQS * L], F32)
        self.h_r = d("rh", [FF, SEQS * L], F32)
        self.h_pc = d("pch", [D, SEQS * L], F32)
        self.h_q8bf = {"wfc": d("q8bf_fc", [FF, D], BF16),
                       "wproj": d("q8bf_pj", [D, FF], BF16)}
        self.h_cc_in = d("cc_in", [1, 4], F32)
        self.h_cc_out = d("cc_out", [1, 4], F32)

    def setup_views(self):
        v = self.view
        self.wmtile = [v(f"wmtile{i}", 128, 1024, F32) for i in range(3)]
        self.q8v = v("q8v", 128, 2048, FP8)
        self.wmax = v("wmax", 128, 4, F32)
        self.a_base = self.ar_off
        self.xtm = [v("xtm0", 128, 2048, F32), v("xtm1", 128, 2048, F32)]
        self.xfm = v("xfm", 128, 16 * 512, F32)
        self.xb = v("xb", 128, 16 * 512, BF16)
        self.kbuf = v("kbuf", 128, 8 * 1024, BF16)
        self.vbuf = v("vbuf", 128, 8 * 1024, BF16)
        self.qbuf = v("qbuf", 128, 8 * 512, BF16)
        self.sgbuf = v("sgbuf", 128, 8 * 512, BF16)
        self.pre = [v(f"pre{i}", 128, 520, BF16) for i in range(2)]
        self.cs = [v(f"cs{i}", 128, 512, BF16) for i in range(2)]
        self.hist = v("hist", 128, 40 * 4, BF16)
        rows = ["G", "g", "tmpB", "tmpC"]
        self.r_ = {nm: v("row_" + nm, 8, 512, F32) for nm in rows}
        self.lnrk = v("row_lnrk", 8, 1024, F32)
        self.PWs = v("row_PWs", 8, 1024, F32)
        self.beta_s = v("row_beta_s", 8, 1024, F32)
        self.PA_s = v("row_PA_s", 8, 1024, F32)
        self.MQ_s = v("row_MQ_s", 8, 1024, F32)
        self.GEND_s = v("row_GEND_s", 8, 1024, F32)
        self.param = v("param", 128, CPT * NPG * 8, F32)
        self.S = v("S", 128, 8 * 128, F32)
        self.Sbf = v("Sbf", 128, 8 * 128, BF16)
        self.eA = v("eA", 128, 512, F32)
        self.eW = View(self.eA.ap[:, 0:256], "eA")
        self.kw = v("kw", 128, 512, BF16)
        self.Ngen = [v(f"Ngen{i}", 128, 512, BF16) for i in range(2)]
        self.NTgen = [v(f"NTgen{i}", 128, 512, BF16) for i in range(2)]
        self.U0 = v("U0", 128, 512, BF16)
        self.Ucur = [v(f"Ucur{i}", 128, 512, BF16) for i in range(2)]
        self.watt = View(self.U0.ap[:, 0:256], "U0")
        self.t1s = v("t1s", 128, 512, BF16)
        self.osq = View(self.t1s.ap[0:64, :], "t1s")
        self.dWT = View(self.t1s.ap[:, 0:256], "t1s")
        self.dA = View(self.Ngen[0].ap, "Ngen0")
        self.oraw = View(self.eA.ap[0:64, 0:512], "eA")
        self.rmsacc = v("rmsacc", 64, 16, F32)
        self.on_ = View(self.Ngen[1].ap[0:64, :], "Ngen1")
        self.ofm = v("ofm", 128, 8 * 512, BF16)
        self.wg_ = [v("wg0", 128, 16 * 128, BF16), v("wg1", 128, 16 * 128, BF16)]
        self.ba_bf = View(self.wg_[0].ap[0:24, :], "wg0")
        self.amax = v("amax", 128, 4, F32)
        self.rbank = v("rbank", 1, 768, F32)
        self.a_end = self.ar_off
        assert self.ar_off <= self.ar_n

    def prologue(self):
        p, nc = self.p, self.nc
        for nm in C_SHAPES:
            t = getattr(self, "c_" + nm)
            src = self.w_in[nm]
            p.task("sp", lambda e, t=t, s=src: e.dma_start(out=t[:, :], in_=s.ap()),
                   writes=[f"c_{nm}"], dma="cld", label=f"ld_{nm}")
        for i, nm in enumerate(["conv_k", "conv_v", "conv_q"]):
            nch = 16 if nm != "conv_q" else 8
            base = (0, 16, 32)[i]
            src = self.w_in[nm]
            for c in range(nch):
                cw = self.convw
                p.task("sp", lambda e, cw=cw, s=src, c=c, base=base:
                       e.dma_start(out=cw[:, (base + c) * 4:(base + c) * 4 + 4],
                                   in_=s[c * 128:(c + 1) * 128, :]),
                       writes=[f"convw{base + c}"], dma="cld", label="ld_conv")
        p.task("sp", lambda e: e.dma_start(out=self.dtb_col[:, :], in_=self.w_in["dt_bias"].ap()),
               writes=["dtb"], dma="cld")
        p.task("sp", lambda e: e.dma_start(out=self.scal[0:8, 0:1], in_=self.w_in["A_log"].ap()),
               writes=["alog_raw"], dma="cld")
        p.task("act", lambda e: e.activation(out=self.scal[0:8, 1:2], in_=self.scal[0:8, 0:1],
                                             func=ACTF.Exp),
               reads=["alog_raw"], writes=["alog_exp"], label="expA")
        p.task("dve", lambda e: e.tensor_scalar(out=self.nega_col[:, :], in0=self.scal[0:8, 1:2],
                                                scalar1=-1.0, scalar2=None, op0=ALU.mult),
               reads=["alog_exp"], writes=["nega"], label="negA")
        p.task("sp", lambda e: e.dma_start(out=self.onw_col[:, :], in_=self.w_in["o_norm_w"].ap()),
               writes=["onw"], dma="cld")

        p.task("pool", lambda e: e.dma_start(out=self.ba_bf[0:16, :], in_=self.w_in["Wb"].ap()),
               writes=["wg0"], dma="bald", label="ba_ld1")
        p.task("pool", lambda e: e.dma_start(out=self.ba_bf[16:24, :], in_=self.w_in["Wa"].ap()),
               writes=["wg0"], dma="bald", label="ba_ld2")
        for k in range(NKC):
            p.task("pe", lambda e, k=k: e.transpose(
                self.tbf[0:128, 0:24], self.ba_bf[0:24, k * 128:(k + 1) * 128],
                self.c_ident_bf[0:24, 0:24]),
                   reads=["wg0", "c_ident_bf"], writes=["tbf"], label="ba_tr")
            p.task("dve", lambda e, k=k: e.tensor_copy(
                out=self.wba[:, k * 24:(k + 1) * 24], in_=self.tbf[0:128, 0:24]),
                   reads=["tbf"], writes=[f"wba{k}"], label="ba_ev")

        for nm in ["Wk", "Wv", "Wq", "Wg", "Wo"]:
            src, dst = self.w_in[nm], self.h_wbf[nm]
            p.task("pool", lambda e, s=src, d=dst: e.dma_start(out=d.ap(), in_=s.ap()),
                   reads=[], writes=[f"hbf_{nm}"], dma=f"wcast_{nm}", label=f"cast_{nm}")
        for nm, O, IN in [("Wk", 2 * KD, D), ("Wv", 2 * KD, D), ("Wq", KD, D),
                          ("Wg", KD, D), ("Wo", D, KD)]:
            wbf, wt = self.h_wbf[nm], self.h_wt[nm]
            for ki in range(IN // 128):
                for og in range(O // 512):
                    buf = self.wg_[(ki * (O // 512) + og) % 2]
                    key = f"wg{(ki * (O // 512) + og) % 2}"
                    p.task("sp", lambda e, w=wbf, ki=ki, og=og, b=buf:
                           e.dma_start(out=b[:, 0:512],
                                       in_=w[og * 512:(og + 1) * 512, ki * 128:(ki + 1) * 128],
                                       transpose=True),
                           reads=[f"hbf_{nm}"], writes=[key], dma=key, label=f"wtt_{nm}")
                    if nm == "Wo":
                        p.task("dve", lambda e, b=buf: e.tensor_scalar(
                            out=b[:, 0:512], in0=b[:, 0:512], scalar1=self.onw_col[:, :],
                            scalar2=None, op0=ALU.mult),
                               reads=[key, "onw"], writes=[key], label="wo_scale")
                    src_ap = buf[:, 0:512].rearrange("p (m c) -> p m c", c=128)
                    dst_ap = wt[og * 4:(og + 1) * 4, ki, :, :].rearrange("m p c -> p m c")
                    p.task("act", lambda e, s=src_ap, d=dst_ap: e.dma_start(out=d, in_=s),
                           reads=[key, f"hbf_{nm}"], writes=[f"hwt_{nm}"], dma=f"wtst_{nm}",
                           label=f"wtst_{nm}")

        self.wm_pieces = []
        self._wm_idx = 0
        for wi, nm in enumerate(["wfc", "wproj"]):
            w = self.w_in[nm]
            R, Cc = (FF, D) if nm == "wfc" else (D, FF)
            w2 = w.ap().rearrange("(a p) b -> a p (b)", p=128)
            p.task("dve", lambda e, wi=wi: e.memset(self.wmax[:, wi:wi + 1], 0.0),
                   writes=[f"wmax{wi}"], label="wm0", always_sync=True)

            def mk_piece(a, cgs, wi=wi, w2=w2):
                def emit():
                    idx = self._wm_idx
                    self._wm_idx += 1
                    buf = self.wmtile[idx % 3]
                    key = f"wmt{idx % 3}"
                    p.task("act", lambda e, a=a, c=cgs, b=buf, w2=w2:
                           e.dma_start(out=b[:, :], in_=w2[a, :, c * 1024:(c + 1) * 1024]),
                           writes=[key], dma=key, label="wm_ld")
                    p.task("dve", lambda e, b=buf, wi=wi: e.tensor_reduce(
                        out=self.scal[:, 8 + wi:9 + wi], in_=b[:, :], axis=AX.X,
                        op=ALU.max, apply_absolute_value=True),
                           reads=[key], writes=[f"wmr{wi}"], label="wm_red")
                    p.task("dve", lambda e, wi=wi: e.tensor_tensor(
                        out=self.wmax[:, wi:wi + 1], in0=self.wmax[:, wi:wi + 1],
                        in1=self.scal[:, 8 + wi:9 + wi], op=ALU.max),
                           reads=[f"wmr{wi}", f"wmax{wi}"], writes=[f"wmax{wi}"],
                           label="wm_acc")
                return emit
            for a in range(R // 128):
                for cgs in range(Cc // 1024):
                    self.wm_pieces.append(mk_piece(a, cgs))

    def phase_a_tile(self, s, t):
        p = self.p
        tt0 = t * TT
        gt0 = s * L + tt0
        tag = f"s{s}t{t}"

        for half in range(2):
            for bi in range(2):
                blk = half * 2 + bi
                buf = self.xtm[bi]
                key = f"xtm{bi}"
                p.task("sp", lambda e, blk=blk, b=buf, s=s:
                       e.dma_start(out=b[:, :],
                                   in_=self.x_in[s, tt0 + blk * 128: tt0 + (blk + 1) * 128, :]),
                       writes=[key], dma=key, label=f"{tag}_xld{blk}")
            for f in range(16):
                for bi in range(2):
                    p.task("pe", lambda e, f=f, bi=bi: e.transpose(
                        self.tf32[:, bi * 128:(bi + 1) * 128],
                        self.xtm[bi][:, f * 128:(f + 1) * 128], self.c_ident_f32[:, :]),
                           reads=[f"xtm{bi}", "c_ident_f32"], writes=["tf32"],
                           label=f"{tag}_xtr")
                co = half * 256
                p.task("dve", lambda e, f=f, co=co: e.tensor_copy(
                    out=self.xfm[:, f * 512 + co: f * 512 + co + 256], in_=self.tf32[:, 0:256]),
                       reads=["tf32"], writes=[f"xfm{f}"], label=f"{tag}_xev")
                p.task("act", lambda e, f=f, co=co: e.activation(
                    out=self.xb[:, f * 512 + co: f * 512 + co + 256],
                    in_=self.xfm[:, f * 512 + co: f * 512 + co + 256],
                    func=ACTF.Copy),
                       reads=[f"xfm{f}"], writes=[f"xb{f}"], label=f"{tag}_xevb")

        mlist = ([("kv", "Wk", m) for m in range(16)] +
                 [("kv", "Wv", m) for m in range(16)] +
                 [("ba", None, 0), ("ba", None, 1), ("ba", None, 2)] +
                 [("q", "Wq", m) for m in range(8)] +
                 [("g", "Wg", m) for m in range(8)])
        pbanks = [self.p0, self.p1]
        rot = 0
        prot = 0
        for kind, wname, m in mlist:
            pb = pbanks[prot % 2]
            pkey = f"p{prot % 2}"
            prot += 1
            if kind == "ba":
                off = m * 8
                for k in range(NKC):
                    p.task("pe", lambda e, k=k, off=off, pb=pb: e.matmul(
                        pb[0:8, 0:512], self.wba[:, k * 24 + off: k * 24 + off + 8],
                        self.xb[:, k * 512:(k + 1) * 512],
                        start=(k == 0), stop=(k == NKC - 1)),
                           reads=[f"wba{k}", f"xb{k}"], writes=[pkey],
                           label=f"{tag}_mm_ba{m}")
                self.emit_ba_evac(m, pb, pkey, tag)
                continue
            wt = self.h_wt[wname]
            nk = NKC if wname != "Wo" else 8
            buf = self.wg_[rot % 2]
            wkey = f"wg{rot % 2}"
            rot += 1
            p.task("sp", lambda e, wt=wt, m=m, b=buf, nk=nk:
                   e.dma_start(out=b[:, 0:nk * 128].rearrange("p (k c) -> p k c", c=128),
                               in_=wt[m, :, :, :].rearrange("k p c -> p k c")),
                   reads=[f"hwt_{wname}"], writes=[wkey], dma=wkey,
                   label=f"{tag}_wld_{wname}{m}")
            for k in range(nk):
                p.task("pe", lambda e, k=k, b=buf, pb=pb: e.matmul(
                    pb[:, 0:512], b[:, k * 128:(k + 1) * 128],
                    self.xb[:, k * 512:(k + 1) * 512],
                    start=(k == 0), stop=(k == nk - 1)),
                       reads=[wkey, f"xb{k}"], writes=[pkey],
                       label=f"{tag}_mm_{wname}{m}")
            if kind == "kv":
                cv = m if wname == "Wk" else 16 + m
                dst = self.kbuf if wname == "Wk" else self.vbuf
                nn, hh = m // 8, m % 8
                self.emit_conv(pb, pkey, cv, dst, hh, nn, s, t, tag)
            elif kind == "q":
                cv = 32 + m
                self.emit_conv(pb, pkey, cv, self.qbuf, m, None, s, t, tag)
            elif kind == "g":
                p.task("act", lambda e, m=m, pb=pb: e.activation(
                    out=self.sgbuf[:, m * 512:(m + 1) * 512], in_=pb[:, 0:512],
                    func=ACTF.Silu),
                       reads=[pkey], writes=[f"sg{m}"], label=f"{tag}_sg{m}")

        self.emit_norms(tag)
        self.emit_rows(tag)
        for c in range(CPT):
            self.emit_param(c, tag)
        self.tap(f"kbuf_{tag}", self.kbuf.ap, [128, 8192], BF16,
                 reads=[f"k{h}" for h in range(8)])
        self.tap(f"vbuf_{tag}", self.vbuf.ap, [128, 8192], BF16,
                 reads=[f"v{h}" for h in range(8)])
        self.tap(f"qbuf_{tag}", self.qbuf.ap, [128, 4096], BF16,
                 reads=[f"q{h}" for h in range(8)])
        self.tap(f"sgbuf_{tag}", self.sgbuf.ap, [128, 4096], BF16,
                 reads=[f"sg{h}" for h in range(8)])
        self.tap(f"G_{tag}", self.r_["G"].ap, [8, 512], F32, reads=["row_G"])
        self.tap(f"lnrk_{tag}", self.lnrk.ap, [8, 1024], F32, reads=["row_lnrk"])
        self.tap(f"betas_{tag}", self.beta_s.ap, [8, 1024], F32, reads=["row_beta_s"])
        self.tap(f"MQ_{tag}", self.MQ_s.ap, [8, 1024], F32, reads=["row_MQ_s"])
        self.tap(f"param_{tag}", self.param.ap, [128, CPT * NPG * 8], F32,
                 reads=[f"param{c}" for c in range(CPT)])
        for c in range(CPT):
            for G2 in range(2):
                self.emit_scan_group(s, t, c, G2, tag)
        self.tap(f"ofm_{tag}", self.ofm.ap, [128, 4096], BF16,
                 reads=[f"ofm{h}" for h in range(8)])
        self.tap(f"S_{tag}", self.S.ap, [128, 1024], F32,
                 reads=[f"S{h}" for h in range(8)])
        self.emit_epilogue(s, t, gt0, tag)
        self.tap(f"x1_{tag}", self.xfm.ap, [128, 8192], F32,
                 reads=[f"xfm{f}" for f in range(16)])

    def emit_conv(self, pb, pkey, cv, dstbuf, hh, nn, s, t, tag):
        p = self.p
        rot = cv % 2
        pre, cs = self.pre[rot], self.cs[rot]
        prek, csk = f"pre{rot}", f"cs{rot}"
        histk = f"hist{cv}"
        if t == 0:
            p.task("dve", lambda e, pre=pre: e.memset(pre[:, 0:3], 0.0),
                   writes=[prek], label=f"{tag}_hrz{cv}", always_sync=True)
        else:
            p.task("dve", lambda e, pre=pre, cv=cv: e.tensor_copy(
                out=pre[:, 0:3], in_=self.hist[:, cv * 4:cv * 4 + 3]),
                   reads=[histk], writes=[prek], label=f"{tag}_hrs{cv}")
        p.task("dve", lambda e, pre=pre, pb=pb: e.tensor_copy(
            out=pre[:, 3:515], in_=pb[:, 0:512]),
               reads=[pkey], writes=[prek], label=f"{tag}_pre{cv}")
        p.task("dve", lambda e, pre=pre, cv=cv: e.tensor_copy(
            out=self.hist[:, cv * 4:cv * 4 + 3], in_=pre[:, 512:515]),
               reads=[prek], writes=[histk], label=f"{tag}_hsv{cv}")
        w = self.convw
        p.task("dve", lambda e, pre=pre, cs=cs, cv=cv: e.tensor_scalar(
            out=cs[:, :], in0=pre[:, 0:512], scalar1=w[:, cv * 4:cv * 4 + 1],
            scalar2=None, op0=ALU.mult),
               reads=[prek, f"convw{cv}"], writes=[csk], label=f"{tag}_cv0_{cv}")
        for j in range(1, 4):
            p.task("dve", lambda e, pre=pre, cs=cs, cv=cv, j=j: e.scalar_tensor_tensor(
                out=cs[:, :], in0=pre[:, j:512 + j], scalar=w[:, cv * 4 + j:cv * 4 + j + 1],
                in1=cs[:, :], op0=ALU.mult, op1=ALU.add),
                   reads=[prek, csk, f"convw{cv}"], writes=[csk], label=f"{tag}_cv{j}_{cv}")
        if nn is None:
            out_ap = dstbuf[:, hh * 512:(hh + 1) * 512]
            dkey = f"q{hh}"
        else:
            out_ap = dstbuf.ap[:, hh * 1024:(hh + 1) * 1024].rearrange(
                "p (c q) -> p c q", q=128)[:, :, nn * 64:(nn + 1) * 64]
            dkey = (f"k{hh}" if dstbuf is self.kbuf else f"v{hh}")
        p.task("act", lambda e, out_ap=out_ap, cs=cs: e.activation(
            out=out_ap, in_=cs[:, :], func=ACTF.Silu),
               reads=[csk], writes=[dkey], label=f"{tag}_silu{cv}")

    def emit_ba_evac(self, m, pb, pkey, tag):
        p = self.p
        r = self.r_
        if m in (0, 1):
            dst = self.beta_s.ap.rearrange("p (c n q) -> p c n q", n=2, q=64)[:, :, m, :]
            tmp = r["tmpB"]
            p.task("act", lambda e, pb=pb, tmp=tmp: e.activation(
                out=tmp[:, :], in_=pb[0:8, 0:512], func=ACTF.Sigmoid),
                   reads=[pkey], writes=["row_tmpB"], label=f"{tag}_bsig{m}")
            p.task("dve", lambda e, dst=dst, tmp=tmp: e.tensor_scalar(
                out=dst, in0=tmp[:, :].rearrange("p (c q) -> p c q", q=64),
                scalar1=2.0, scalar2=None, op0=ALU.mult),
                   reads=["row_tmpB"], writes=["row_beta_s"], label=f"{tag}_bx2_{m}")
        else:
            p.task("act", lambda e, pb=pb: e.activation(
                out=r["tmpB"][:, :], in_=pb[0:8, 0:512], func=ACTF.Exp,
                bias=self.dtb_col[:, :]),
                   reads=[pkey, "dtb"], writes=["row_tmpB"], label=f"{tag}_gexp")
            p.task("act", lambda e: e.activation(
                out=r["tmpC"][:, :], in_=r["tmpB"][:, :], func=ACTF.Ln, bias=1.0),
                   reads=["row_tmpB"], writes=["row_tmpC"], label=f"{tag}_gln")
            p.task("dve", lambda e: e.tensor_scalar(
                out=r["g"][:, :], in0=r["tmpC"][:, :], scalar1=self.nega_col[:, :],
                scalar2=None, op0=ALU.mult),
                   reads=["row_tmpC", "nega"], writes=["row_g"], label=f"{tag}_gmul")

    def emit_norms(self, tag):
        p = self.p
        for half in range(2):
            for h in range(8):
                sq = self.cs[h % 2]
                sqk = f"cs{h % 2}"
                p.task("act", lambda e, h=h, half=half, sq=sq: e.activation(
                    out=sq[:, :], in_=self.kbuf[:, h * 1024 + half * 512: h * 1024 + half * 512 + 512],
                    func=ACTF.Square),
                       reads=[f"k{h}"], writes=[sqk], label=f"{tag}_ksq")
                p.task("pe", lambda e, h=h, sq=sq: e.matmul(
                    self.p5[0:8, 0:512], self.c_onehot8[:, 8 * h:8 * h + 8], sq[:, :],
                    start=(h == 0), stop=(h == 7)),
                       reads=[sqk, "c_onehot8"], writes=["p5"], label=f"{tag}_knrm")
            p.task("dve", lambda e: e.tensor_scalar(
                out=self.r_["tmpC"][:, :], in0=self.p5[0:8, 0:512], scalar1=1e-24,
                scalar2=None, op0=ALU.max),
                   reads=["p5"], writes=["row_tmpC"], label=f"{tag}_kcl")
            p.task("act", lambda e: e.activation(
                out=self.r_["tmpC"][:, :], in_=self.r_["tmpC"][:, :], func=ACTF.Ln),
                   reads=["row_tmpC"], writes=["row_tmpC"], label=f"{tag}_kln")
            p.task("dve", lambda e, half=half: e.tensor_scalar(
                out=self.lnrk[:, half * 512:(half + 1) * 512], in0=self.r_["tmpC"][:, :],
                scalar1=-0.5, scalar2=None, op0=ALU.mult),
                   reads=["row_tmpC"], writes=["row_lnrk"], label=f"{tag}_kl2")
        for h in range(8):
            sq = self.cs[h % 2]
            sqk = f"cs{h % 2}"
            p.task("act", lambda e, h=h, sq=sq: e.activation(
                out=sq[:, :], in_=self.qbuf[:, h * 512:(h + 1) * 512], func=ACTF.Square),
                   reads=[f"q{h}"], writes=[sqk], label=f"{tag}_qsq")
            p.task("pe", lambda e, h=h, sq=sq: e.matmul(
                self.p5[0:8, 0:512], self.c_onehot8[:, 8 * h:8 * h + 8], sq[:, :],
                start=(h == 0), stop=(h == 7)),
                   reads=[sqk, "c_onehot8"], writes=["p5"], label=f"{tag}_qnrm")
        p.task("dve", lambda e: e.tensor_scalar(
            out=self.r_["tmpC"][:, :], in0=self.p5[0:8, 0:512], scalar1=1e-24,
            scalar2=None, op0=ALU.max),
               reads=["p5"], writes=["row_tmpC"], label=f"{tag}_qcl")
        p.task("act", lambda e: e.activation(
            out=self.r_["tmpC"][:, :], in_=self.r_["tmpC"][:, :], func=ACTF.Ln),
               reads=["row_tmpC"], writes=["row_tmpC"], label=f"{tag}_qln")
        p.task("dve", lambda e: e.tensor_scalar(
            out=self.r_["tmpB"][:, :], in0=self.r_["tmpC"][:, :],
            scalar1=-0.5, scalar2=float(-0.5 * np.log(128.0)), op0=ALU.mult, op1=ALU.add),
               reads=["row_tmpC"], writes=["row_tmpB"], label=f"{tag}_ql2")

    def emit_rows(self, tag):
        p = self.p
        r = self.r_

        def ev(ap1024, par):
            return ap1024.rearrange("p (c n q) -> p c n q", n=2, q=64)[:, :, par, :]

        def pv(ap512):
            return ap512.rearrange("p (c q) -> p c q", q=64)

        for c in range(CPT):
            p.task("dve", lambda e, c=c: e.tensor_tensor_scan(
                out=r["G"][:, c * 64:(c + 1) * 64], data0=r["g"][:, c * 64:(c + 1) * 64],
                data1=r["g"][:, c * 64:(c + 1) * 64], initial=0.0,
                op0=ALU.add, op1=ALU.bypass),
                   reads=["row_g"], writes=["row_G"], label=f"{tag}_gcum{c}")
        for c in range(CPT):
            gsrc = r["G"][:, c * 64 + 63:c * 64 + 64]
            for par in range(2):
                dsl = self.GEND_s[:, c * 128 + par * 64: c * 128 + (par + 1) * 64]
                p.task("dve", lambda e, gsrc=gsrc, dsl=dsl: e.tensor_copy(
                    out=dsl, in_=bass.broadcast_tensor_aps(gsrc, dsl)[0]),
                       reads=["row_G"], writes=["row_GEND_s"], label=f"{tag}_ge{c}_{par}")
        for par in range(2):
            p.task("dve", lambda e, par=par: e.tensor_tensor(
                out=ev(self.MQ_s.ap, par), in0=pv(r["G"].ap), in1=pv(r["tmpB"].ap),
                op=ALU.add),
                   reads=["row_G", "row_tmpB"], writes=["row_MQ_s"], label=f"{tag}_mq{par}")
        for par in range(2):
            p.task("dve", lambda e, par=par: e.tensor_tensor(
                out=ev(self.PWs.ap, par), in0=ev(self.lnrk.ap, par),
                in1=pv(r["G"].ap), op=ALU.subtract),
                   reads=["row_lnrk", "row_G"], writes=["row_PWs"], label=f"{tag}_pw{par}")
        for par in range(2):
            bsl = ev(self.beta_s.ap, par)
            p.task("act", lambda e, par=par, bsl=bsl: e.activation(
                out=pv(r["tmpC"].ap), in_=bsl, func=ACTF.Ln),
                   reads=["row_beta_s"], writes=["row_tmpC"], label=f"{tag}_lnb{par}")
            p.task("dve", lambda e, par=par: e.tensor_tensor(
                out=pv(r["tmpC"].ap), in0=pv(r["tmpC"].ap), in1=ev(self.lnrk.ap, par),
                op=ALU.add),
                   reads=["row_tmpC", "row_lnrk"], writes=["row_tmpC"], label=f"{tag}_pb{par}")
            p.task("dve", lambda e, par=par: e.tensor_tensor(
                out=ev(self.PA_s.ap, par), in0=pv(r["G"].ap), in1=pv(r["tmpC"].ap),
                op=ALU.add),
                   reads=["row_tmpC", "row_G"], writes=["row_PA_s"], label=f"{tag}_pa{par}")

    def emit_param(self, c, tag):
        p = self.p
        groups = [(PG_BETA, self.beta_s, "row_beta_s"), (PG_PA, self.PA_s, "row_PA_s"),
                  (PG_PW, self.PWs, "row_PWs"), (PG_MQ, self.MQ_s, "row_MQ_s"),
                  (PG_GEND, self.GEND_s, "row_GEND_s")]
        for g, row, rk in groups:
            src = row[:, c * 128:(c + 1) * 128]
            p.task("pe", lambda e, g=g, src=src: e.transpose(
                self.tf32[:, g * 8:(g + 1) * 8], src, self.c_ident_f32[0:8, 0:8]),
                   reads=[rk, "c_ident_f32"], writes=["tf32"], label=f"{tag}_ptr{c}_{g}")
        pcol = lambda g: self.param[:, c * NPG * 8 + g * 8: c * NPG * 8 + (g + 1) * 8]
        p.task("dve", lambda e, c=c: e.tensor_copy(
            out=self.param[:, c * NPG * 8: c * NPG * 8 + 40], in_=self.tf32[:, 0:40]),
               reads=["tf32"], writes=[f"param{c}"], label=f"{tag}_pev{c}")
        p.task("act", lambda e: e.activation(out=pcol(PG_DECAY), in_=pcol(PG_GEND),
                                             func=ACTF.Exp),
               reads=[f"param{c}"], writes=[f"param{c}"], label=f"{tag}_pd1{c}")
        p.task("dve", lambda e: e.tensor_tensor(out=pcol(PG_W), in0=pcol(PG_PW),
                                                in1=pcol(PG_GEND), op=ALU.add),
               reads=[f"param{c}"], writes=[f"param{c}"], label=f"{tag}_pd2{c}")
        p.task("act", lambda e: e.activation(out=pcol(PG_W), in_=pcol(PG_W),
                                             func=ACTF.Exp),
               reads=[f"param{c}"], writes=[f"param{c}"], label=f"{tag}_pd3{c}")
        p.task("act", lambda e: e.activation(out=pcol(PG_GRQ), in_=pcol(PG_MQ),
                                             func=ACTF.Exp),
               reads=[f"param{c}"], writes=[f"param{c}"], label=f"{tag}_pd4{c}")
        p.task("act", lambda e: e.activation(out=pcol(PG_NBGR), in_=pcol(PG_PA),
                                             func=ACTF.Exp),
               reads=[f"param{c}"], writes=[f"param{c}"], label=f"{tag}_pd5{c}")
        p.task("dve", lambda e: e.tensor_scalar(
            out=pcol(PG_NBGR), in0=pcol(PG_NBGR), scalar1=-1.0, scalar2=None,
            op0=ALU.mult),
               reads=[f"param{c}"], writes=[f"param{c}"], label=f"{tag}_pd6{c}")

    def emit_scan_group(self, s, t, c, G2, tag):
        p = self.p
        hs = [4 * G2 + i for i in range(4)]
        pc = lambda g, h: self.param[:, c * NPG * 8 + g * 8 + h: c * NPG * 8 + g * 8 + h + 1]
        pc64 = lambda g, h: self.param[0:64, c * NPG * 8 + g * 8 + h: c * NPG * 8 + g * 8 + h + 1]
        kslice = lambda h: self.kbuf[:, h * 1024 + c * 128: h * 1024 + (c + 1) * 128]
        vslice = lambda h: self.vbuf[:, h * 1024 + c * 128: h * 1024 + (c + 1) * 128]
        qslice = lambda h: self.qbuf[:, h * 512 + c * 64: h * 512 + (c + 1) * 64]
        pk = f"param{c}"
        g = f"{tag}c{c}g{G2}"

        for si, h in enumerate(hs):
            p.task("pe", lambda e, si=si, h=h: e.transpose(
                self.tf32[0:1, si * 128:(si + 1) * 128], pc(PG_PW, h),
                self.c_ident_f32[:, :]),
                   reads=[pk, "c_ident_f32"], writes=["tf32"], label=g + "_rtP")
        for si, h in enumerate(hs):
            p.task("pe", lambda e, si=si, h=h: e.transpose(
                self.p1[0:1, 256 + si * 64: 256 + (si + 1) * 64], pc64(PG_MQ, h),
                self.c_ident_f32[0:64, 0:64]),
                   reads=[pk, "c_ident_f32"], writes=["p1"], label=g + "_rtM")
        p.task("dve", lambda e: e.tensor_copy(out=self.rbank[0:1, 0:512],
                                              in_=self.tf32[0:1, 0:512]),
               reads=["tf32"], writes=["rbank_pw"], label=g + "_rbp")
        p.task("dve", lambda e: e.tensor_copy(out=self.rbank[0:1, 512:768],
                                              in_=self.p1[0:1, 256:512]),
               reads=["p1"], writes=["rbank_mq"], label=g + "_rbm")
        for si, h in enumerate(hs):
            p.task("pe", lambda e, si=si, h=h: e.matmul(
                self.tf32[:, si * 128:(si + 1) * 128], self.c_ones_row_f32[:, :],
                self.rbank[0:1, si * 128:(si + 1) * 128], start=True, stop=True,
                skip_group_check=True),
                   reads=["c_ones_row_f32", "rbank_pw"], writes=["tf32"], label=g + "_oA")
        for si, h in enumerate(hs):
            p.task("dve", lambda e, si=si, h=h: e.scalar_tensor_tensor(
                out=self.eA[:, si * 128:(si + 1) * 128], in0=self.tf32[:, si * 128:(si + 1) * 128],
                scalar=pc(PG_PA, h), in1=self.c_maskA[:, :], op0=ALU.add, op1=ALU.add),
                   reads=["tf32", pk, "c_maskA"], writes=["eA"], label=g + "_eA")
        p.task("act", lambda e: e.activation(out=self.dA[:, :], in_=self.eA[:, :],
                                             func=ACTF.Exp),
               reads=["eA"], writes=["Ngen0"], label=g + "_expA")
        for si, h in enumerate(hs):
            p.task("pe", lambda e, si=si, h=h: e.transpose(
                self.tbf[:, si * 128:(si + 1) * 128], kslice(h), self.c_ident_bf[:, :]),
                   reads=[f"k{h}", "c_ident_bf"], writes=["tbf"], label=g + "_trK")
        for si, h in enumerate(hs):
            p.task("pe", lambda e, si=si, h=h: e.transpose(
                self.tbf[:, 512 + si * 128: 512 + (si + 1) * 128], vslice(h),
                self.c_ident_bf[:, :]),
                   reads=[f"v{h}", "c_ident_bf"], writes=["tbf"], label=g + "_trV")
        for si, h in enumerate(hs):
            p.task("dve", lambda e, si=si, h=h: e.tensor_scalar(
                out=self.kw[:, si * 128:(si + 1) * 128], in0=self.tbf[:, si * 128:(si + 1) * 128],
                scalar1=pc(PG_W, h), scalar2=None, op0=ALU.mult),
                   reads=["tbf", pk], writes=["kw"], label=g + "_kw")
        for si, h in enumerate(hs):
            p.task("pe", lambda e, si=si, h=h: e.matmul(
                self.p2[:, si * 128:(si + 1) * 128], kslice(h), kslice(h),
                start=True, stop=True, skip_group_check=True),
                   reads=[f"k{h}"], writes=["p2"], label=g + "_KK")
        p.task("dve", lambda e: e.scalar_tensor_tensor(
            out=self.Ngen[0][:, :], in0=self.p2[:, :], scalar=-1.0, in1=self.dA[:, :],
            op0=ALU.mult, op1=ALU.mult),
               reads=["p2", "Ngen0"], writes=["Ngen0"], label=g + "_mkN")
        if g.startswith("s0t0c0g0"):
            self.tap("scan_N1", self.Ngen[0].ap, [128, 512], BF16, reads=["Ngen0"])
            self.tap("scan_dA", self.dA.ap, [128, 512], BF16, reads=["Ngen0"])
        for si in range(4):
            p.task("pe", lambda e, si=si: e.transpose(
                self.tbf[:, si * 128:(si + 1) * 128], self.Ngen[0][:, si * 128:(si + 1) * 128],
                self.c_ident_bf[:, :]),
                   reads=["Ngen0", "c_ident_bf"], writes=["tbf"], label=g + "_trN")
        p.task("dve", lambda e: e.tensor_copy(out=self.NTgen[0][:, :], in_=self.tbf[:, 0:512]),
               reads=["tbf"], writes=["NTgen0"], label=g + "_evNT")
        if g.startswith("s0t0c0g0"):
            self.tap("scan_NT1", self.NTgen[0].ap, [128, 512], BF16, reads=["NTgen0"])
        for si, h in enumerate(hs):
            p.task("pe", lambda e, si=si, h=h: e.matmul(
                self.p0[:, si * 128:(si + 1) * 128], kslice(h),
                self.Sbf[:, h * 128:(h + 1) * 128], start=True, stop=True,
                skip_group_check=True),
                   reads=[f"k{h}", f"Sbf{h}"], writes=["p0"], label=g + "_KS")
        for si, h in enumerate(hs):
            p.task("dve", lambda e, si=si, h=h: e.tensor_scalar(
                out=self.t1s[:, si * 128:(si + 1) * 128], in0=self.p0[:, si * 128:(si + 1) * 128],
                scalar1=pc(PG_NBGR, h), scalar2=None, op0=ALU.mult),
                   reads=["p0", pk], writes=["t1s"], label=g + "_t1")
        for si, h in enumerate(hs):
            p.task("dve", lambda e, si=si, h=h: e.scalar_tensor_tensor(
                out=self.U0[:, si * 128:(si + 1) * 128],
                in0=self.tbf[:, 512 + si * 128: 512 + (si + 1) * 128],
                scalar=pc(PG_BETA, h), in1=self.t1s[:, si * 128:(si + 1) * 128],
                op0=ALU.mult, op1=ALU.add),
                   reads=["tbf", pk, "t1s"], writes=["U0"], label=g + "_R")
        for j in range(7):
            if j >= 1:
                src = (j - 1) % 2
                dst = j % 2
                for si in range(4):
                    p.task("pe", lambda e, si=si, src=src: e.matmul(
                        self.p3[:, si * 128:(si + 1) * 128],
                        self.NTgen[src][:, si * 128:(si + 1) * 128],
                        self.Ngen[src][:, si * 128:(si + 1) * 128],
                        start=True, stop=True, skip_group_check=True),
                           reads=[f"NTgen{src}", f"Ngen{src}"], writes=["p3"],
                           label=g + f"_sqN{j}")
                for si in range(4):
                    p.task("pe", lambda e, si=si, src=src: e.matmul(
                        self.p4[:, si * 128:(si + 1) * 128],
                        self.Ngen[src][:, si * 128:(si + 1) * 128],
                        self.NTgen[src][:, si * 128:(si + 1) * 128],
                        start=True, stop=True, skip_group_check=True),
                           reads=[f"NTgen{src}", f"Ngen{src}"], writes=["p4"],
                           label=g + f"_sqNT{j}")
                if j < 6:
                    p.task("act", lambda e, dst=dst: e.activation(
                        out=self.Ngen[dst][:, :], in_=self.p3[:, :], func=ACTF.Copy),
                           reads=["p3"], writes=[f"Ngen{dst}"], label=g + f"_evN{j}")
                p.task("dve", lambda e, dst=dst: e.tensor_copy(
                    out=self.NTgen[dst][:, :], in_=self.p4[:, :]),
                       reads=["p4"], writes=[f"NTgen{dst}"], label=g + f"_evNT{j}")
            ntb = f"NTgen{j % 2}"
            u_rhs = self.U0 if j == 0 else self.Ucur[(j - 1) % 2]
            u_rhs_k = "U0" if j == 0 else f"Ucur{(j - 1) % 2}"
            for si in range(4):
                p.task("pe", lambda e, si=si, j=j, u=u_rhs: e.matmul(
                    self.p5[:, si * 128:(si + 1) * 128],
                    self.NTgen[j % 2][:, si * 128:(si + 1) * 128],
                    u[:, si * 128:(si + 1) * 128],
                    start=(j == 0 and si == 0), stop=(j == 6 and si == 3),
                    skip_group_check=True),
                       reads=[ntb, u_rhs_k], writes=["p5"], label=g + f"_ap{j}")
            p.task("dve", lambda e, j=j: e.tensor_tensor(
                out=self.Ucur[j % 2][:, :], in0=self.p5[:, :], in1=self.U0[:, :],
                op=ALU.add),
                   reads=["p5", "U0"], writes=[f"Ucur{j % 2}"], label=g + f"_U{j}")
        if g.startswith("s0t0c0g0"):
            self.tap("scan_U0", self.U0.ap, [128, 512], BF16, reads=["U0"])
            self.tap("scan_U7", self.Ucur[0].ap, [128, 512], BF16, reads=["Ucur0"])
            self.tap("scan_kw", self.kw.ap, [128, 512], BF16, reads=["kw"])
        u7, u7k = self.Ucur[0], "Ucur0"
        for si, h in enumerate(hs):
            p.task("pe", lambda e, si=si, h=h: e.matmul(
                self.p1[:, si * 64:(si + 1) * 64], kslice(h), qslice(h),
                start=True, stop=True, skip_group_check=True),
                   reads=[f"k{h}", f"q{h}"], writes=["p1"], label=g + "_KQ")
        for si, h in enumerate(hs):
            p.task("pe", lambda e, si=si, h=h: e.matmul(
                self.p1[:, 256 + si * 64: 256 + (si + 1) * 64], self.c_ones_row_f32[:, :],
                self.rbank[0:1, 512 + si * 64: 512 + (si + 1) * 64], start=True, stop=True,
                skip_group_check=True),
                   reads=["c_ones_row_f32", "rbank_mq"], writes=["p1"], label=g + "_oW")
        for si, h in enumerate(hs):
            p.task("dve", lambda e, si=si, h=h: e.scalar_tensor_tensor(
                out=self.eW[:, si * 64:(si + 1) * 64],
                in0=self.p1[:, 256 + si * 64: 256 + (si + 1) * 64],
                scalar=pc(PG_PW, h), in1=self.c_maskWT[:, :], op0=ALU.add, op1=ALU.add),
                   reads=["p1", pk, "c_maskWT"], writes=["eA"], label=g + "_eW")
        p.task("act", lambda e: e.activation(out=self.dWT[:, :], in_=self.eW[:, :],
                                             func=ACTF.Exp),
               reads=["eA"], writes=["t1s"], label=g + "_expW")
        p.task("dve", lambda e: e.tensor_tensor(
            out=self.watt[:, :], in0=self.p1[:, 0:256], in1=self.dWT[:, :], op=ALU.mult),
               reads=["p1", "t1s"], writes=["U0"], label=g + "_mkW")
        if g.startswith("s0t0c0g0"):
            self.tap("scan_watt", self.watt.ap, [128, 256], BF16, reads=["U0"])
        for si, h in enumerate(hs):
            p.task("pe", lambda e, si=si, h=h: e.matmul(
                self.p0[0:64, si * 128:(si + 1) * 128], self.watt[:, si * 64:(si + 1) * 64],
                u7[:, si * 128:(si + 1) * 128], start=True, stop=True,
                skip_group_check=True),
                   reads=["U0", u7k], writes=["p0"], label=g + "_Oi")
        for si, h in enumerate(hs):
            p.task("pe", lambda e, si=si, h=h: e.matmul(
                self.p4[0:64, si * 128:(si + 1) * 128], qslice(h),
                self.Sbf[:, h * 128:(h + 1) * 128], start=True, stop=True,
                skip_group_check=True),
                   reads=[f"q{h}", f"Sbf{h}"], writes=["p4"], label=g + "_Oe")
        for si, h in enumerate(hs):
            p.task("dve", lambda e, si=si, h=h: e.tensor_scalar(
                out=self.oraw[0:64, si * 128:(si + 1) * 128],
                in0=self.p4[0:64, si * 128:(si + 1) * 128],
                scalar1=pc64(PG_GRQ, h), scalar2=None, op0=ALU.mult),
                   reads=["p4", pk], writes=["eA"], label=g + "_t2")
        for si in range(4):
            p.task("dve", lambda e, si=si: e.tensor_tensor(
                out=self.oraw[0:64, si * 128:(si + 1) * 128],
                in0=self.oraw[0:64, si * 128:(si + 1) * 128],
                in1=self.p0[0:64, si * 128:(si + 1) * 128], op=ALU.add),
                   reads=["eA", "p0"], writes=["eA"], label=g + "_oadd")
        if g.startswith("s0t0c0g0"):
            self.tap("scan_oraw", self.oraw.ap, [64, 512], F32, reads=["eA"])
        p.task("act", lambda e: e.activation(out=self.osq[0:64, :], in_=self.oraw[0:64, :],
                                             func=ACTF.Square),
               reads=["eA"], writes=["t1s"], label=g + "_osq")
        for si in range(4):
            p.task("dve", lambda e, si=si: e.reduce_sum(
                out=self.rmsacc[0:64, si:si + 1], in_=self.osq[0:64, si * 128:(si + 1) * 128],
                axis=AX.X),
                   reads=["t1s"], writes=["rms0"], label=g + "_ssum")
        p.task("dve", lambda e: e.tensor_scalar(
            out=self.rmsacc[0:64, 4:8], in0=self.rmsacc[0:64, 0:4],
            scalar1=1.0 / 128.0, scalar2=EPS_ORMS, op0=ALU.mult, op1=ALU.add),
               reads=["rms0"], writes=["rms1"], label=g + "_rmu")
        p.task("dve", lambda e: e.reciprocal(out=self.rmsacc[0:64, 8:12],
                                             in_=self.rmsacc[0:64, 4:8]),
               reads=["rms1"], writes=["rms2"], label=g + "_rrc")
        p.task("act", lambda e: e.activation(out=self.rmsacc[0:64, 12:16],
                                             in_=self.rmsacc[0:64, 8:12], func=ACTF.Sqrt),
               reads=["rms2"], writes=["rms3"], label=g + "_rsq")
        for si, h in enumerate(hs):
            p.task("dve", lambda e, si=si: e.tensor_scalar(
                out=self.on_[0:64, si * 128:(si + 1) * 128],
                in0=self.oraw[0:64, si * 128:(si + 1) * 128],
                scalar1=self.rmsacc[0:64, 12 + si:13 + si], scalar2=None, op0=ALU.mult),
                   reads=["rms3", "eA"], writes=["Ngen1"], label=g + "_on")
        for si in range(4):
            p.task("pe", lambda e, si=si: e.transpose(
                self.tbf[:, si * 128: si * 128 + 64], self.on_[0:64, si * 128:(si + 1) * 128],
                self.c_ident_bf[0:64, 0:64]),
                   reads=["Ngen1", "c_ident_bf"], writes=["tbf"], label=g + "_trO")
        for si, h in enumerate(hs):
            p.task("dve", lambda e, si=si, h=h: e.tensor_tensor(
                out=self.ofm[:, h * 512 + c * 64: h * 512 + (c + 1) * 64],
                in0=self.tbf[:, si * 128: si * 128 + 64],
                in1=self.sgbuf[:, h * 512 + c * 64: h * 512 + (c + 1) * 64], op=ALU.mult),
                   reads=["tbf", f"sg{h}"], writes=[f"ofm{h}"], label=g + "_gate")
        for si, h in enumerate(hs):
            p.task("pe", lambda e, si=si: e.matmul(
                self.p3[:, si * 128:(si + 1) * 128], self.kw[:, si * 128:(si + 1) * 128],
                u7[:, si * 128:(si + 1) * 128], start=True, stop=True,
                skip_group_check=True),
                   reads=["kw", u7k], writes=["p3"], label=g + "_dS")
        for si, h in enumerate(hs):
            p.task("dve", lambda e, si=si, h=h: e.scalar_tensor_tensor(
                out=self.S[:, h * 128:(h + 1) * 128], in0=self.S[:, h * 128:(h + 1) * 128],
                scalar=pc(PG_DECAY, h), in1=self.p3[:, si * 128:(si + 1) * 128],
                op0=ALU.mult, op1=ALU.add),
                   reads=["p3", pk, f"S{h}"], writes=[f"S{h}"], label=g + "_Snew")
        for si, h in enumerate(hs):
            p.task("dve", lambda e, h=h: e.tensor_copy(
                out=self.Sbf[:, h * 128:(h + 1) * 128], in_=self.S[:, h * 128:(h + 1) * 128]),
                   reads=[f"S{h}"], writes=[f"Sbf{h}"], label=g + "_Sbf")
        if g.startswith("s0t0c0g0"):
            self.tap("scan_S1", self.S.ap, [128, 1024], F32,
                     reads=[f"S{h}" for h in hs])

    def emit_epilogue(self, s, t, gt0, tag):
        p = self.p
        pbanks = [self.p0, self.p1]
        for m in range(16):
            pb = pbanks[m % 2]
            pkey = f"p{m % 2}"
            buf = self.wg_[m % 2]
            wkey = f"wg{m % 2}"
            wt = self.h_wt["Wo"]
            p.task("sp", lambda e, m=m, b=buf, wt=wt:
                   e.dma_start(out=b[:, 0:8 * 128].rearrange("p (k c) -> p k c", c=128),
                               in_=wt[m, :, :, :].rearrange("k p c -> p k c")),
                   reads=["hwt_Wo"], writes=[wkey], dma=wkey, label=f"{tag}_wldWo{m}")
            for k in range(8):
                p.task("pe", lambda e, k=k, b=buf, pb=pb: e.matmul(
                    pb[:, 0:512], b[:, k * 128:(k + 1) * 128],
                    self.ofm[:, k * 512:(k + 1) * 512],
                    start=(k == 0), stop=(k == 7)),
                       reads=[wkey, f"ofm{k}"], writes=[pkey], label=f"{tag}_mmWo{m}")
            p.task("dve", lambda e, m=m, pb=pb: e.tensor_tensor(
                out=self.xfm[:, m * 512:(m + 1) * 512], in0=self.xfm[:, m * 512:(m + 1) * 512],
                in1=pb[:, 0:512], op=ALU.add),
                   reads=[pkey, f"xfm{m}"], writes=[f"xfm{m}"], label=f"{tag}_res{m}")
        self.tap(f"resid_{tag}", self.xfm.ap, [128, 8192], F32,
                 reads=[f"xfm{f}" for f in range(16)])
        for f in range(16):
            sq = self.pre[f % 2][0:128, 0:512]
            yb = self.cs[f % 2]
            p.task("act", lambda e, f=f, sq=sq: e.activation(
                out=sq[:, :], in_=self.xfm[:, f * 512:(f + 1) * 512], func=ACTF.Square),
                   reads=[f"xfm{f}"], writes=[f"pre{f % 2}"], label=f"{tag}_ysq{f}")
            p.task("dve", lambda e, f=f, yb=yb: e.tensor_copy(
                out=yb[:, :], in_=self.xfm[:, f * 512:(f + 1) * 512]),
                   reads=[f"xfm{f}"], writes=[f"cs{f % 2}"], label=f"{tag}_ybf{f}")
            p.task("pe", lambda e, f=f, yb=yb: e.matmul(
                self.p5[0:1, 0:512], self.c_ones_col_bf[:, :], yb[:, :],
                start=(f == 0), stop=(f == 15), skip_group_check=True),
                   reads=[f"cs{f % 2}", "c_ones_col_bf"], writes=["p5"],
                   label=f"{tag}_stsum{f}")
            p.task("pe", lambda e, f=f, sq=sq: e.matmul(
                self.p3[0:1, 0:512], self.c_ones_col_bf[:, :], sq[:, :],
                start=(f == 0), stop=(f == 15), skip_group_check=True),
                   reads=[f"pre{f % 2}", "c_ones_col_bf"], writes=["p3"],
                   label=f"{tag}_stsq{f}")
        mu = self.r_["tmpB"][0:1, 0:512]
        m2 = self.r_["tmpC"][0:1, 0:512]
        var = self.r_["g"][0:1, 0:512]
        inv = self.r_["G"][0:1, 0:512]
        bseg = self.lnrk[0:1, 0:512]
        p.task("dve", lambda e: e.tensor_scalar(
            out=mu, in0=self.p5[0:1, 0:512], scalar1=1.0 / D, scalar2=None, op0=ALU.mult),
               reads=["p5"], writes=["row_tmpB"], label=f"{tag}_mu")
        p.task("dve", lambda e: e.tensor_scalar(
            out=m2, in0=self.p3[0:1, 0:512], scalar1=1.0 / D, scalar2=None, op0=ALU.mult),
               reads=["p3"], writes=["row_tmpC"], label=f"{tag}_m2")
        p.task("dve", lambda e: e.tensor_tensor(out=var, in0=mu, in1=mu, op=ALU.mult),
               reads=["row_tmpB"], writes=["row_g"], label=f"{tag}_musq")
        p.task("dve", lambda e: e.tensor_tensor(out=var, in0=m2, in1=var, op=ALU.subtract),
               reads=["row_tmpC", "row_g"], writes=["row_g"], label=f"{tag}_var")
        p.task("dve", lambda e: e.tensor_scalar(
            out=var, in0=var, scalar1=EPS_LN, scalar2=None, op0=ALU.add),
               reads=["row_g"], writes=["row_g"], label=f"{tag}_veps")
        p.task("dve", lambda e: e.reciprocal(out=inv, in_=var),
               reads=["row_g"], writes=["row_G"], label=f"{tag}_vrc")
        p.task("act", lambda e: e.activation(out=inv, in_=inv, func=ACTF.Sqrt),
               reads=["row_G"], writes=["row_G"], label=f"{tag}_inv")
        p.task("dve", lambda e: e.scalar_tensor_tensor(
            out=bseg, in0=inv, scalar=-1.0, in1=mu, op0=ALU.mult, op1=ALU.mult),
               reads=["row_G", "row_tmpB"], writes=["row_lnrk"], label=f"{tag}_bb")
        p.task("pe", lambda e: e.matmul(self.p3[:, 0:512], self.c_ones_row_f32[:, :],
                                        inv, start=True, stop=True,
                                        skip_group_check=True),
               reads=["row_G", "c_ones_row_f32"], writes=["p3"], label=f"{tag}_bca")
        p.task("pe", lambda e: e.matmul(self.p4[:, 0:512], self.c_ones_row_f32[:, :],
                                        bseg, start=True, stop=True,
                                        skip_group_check=True),
               reads=["row_lnrk", "c_ones_row_f32"], writes=["p4"], label=f"{tag}_bcb")
        for f in range(16):
            p.task("dve", lambda e, f=f: e.tensor_tensor(
                out=self.xfm[:, f * 512:(f + 1) * 512], in0=self.xfm[:, f * 512:(f + 1) * 512],
                in1=self.p3[:, 0:512], op=ALU.mult),
                   reads=["p3", f"xfm{f}"], writes=[f"xfm{f}"], label=f"{tag}_x1a{f}")
            p.task("dve", lambda e, f=f: e.tensor_tensor(
                out=self.xfm[:, f * 512:(f + 1) * 512], in0=self.xfm[:, f * 512:(f + 1) * 512],
                in1=self.p4[:, 0:512], op=ALU.add),
                   reads=["p4", f"xfm{f}"], writes=[f"xfm{f}"], label=f"{tag}_x1b{f}")
            p.task("dve", lambda e, f=f: e.tensor_reduce(
                out=self.amax[:, 1:2], in_=self.xfm[:, f * 512:(f + 1) * 512],
                axis=AX.X, op=ALU.max, apply_absolute_value=True),
                   reads=[f"xfm{f}"], writes=["amax1"], label=f"{tag}_am{f}")
            p.task("dve", lambda e: e.tensor_tensor(
                out=self.amax[:, 0:1], in0=self.amax[:, 0:1], in1=self.amax[:, 1:2],
                op=ALU.max),
                   reads=["amax1", "amax0"], writes=["amax0"], label=f"{tag}_ama{f}")
            p.task("sp", lambda e, f=f, gt0=gt0: e.dma_start(
                out=self.h_x1[f * 128:(f + 1) * 128, gt0:gt0 + TT],
                in_=self.xfm[:, f * 512:(f + 1) * 512]),
                   reads=[f"xfm{f}"], writes=[f"hx1_{f}_{gt0 // TT}"], dma=f"x1st{f}",
                   label=f"{tag}_x1st{f}")

    def phase_a_seq(self, s, weave=()):
        p = self.p
        weave = list(weave)
        nw = len(weave)
        ntile = self.dbg_ntile
        if s == 0:
            p.task("dve", lambda e: e.memset(self.amax[:, 0:1], 0.0), writes=["amax0"],
                   always_sync=True)
        for h in range(8):
            p.task("dve", lambda e, h=h: e.memset(self.S[:, h * 128:(h + 1) * 128], 0.0),
                   writes=[f"S{h}"], label=f"s{s}_S0_{h}", always_sync=True)
            p.task("dve", lambda e, h=h: e.memset(self.Sbf[:, h * 128:(h + 1) * 128], 0.0),
                   writes=[f"Sbf{h}"], label=f"s{s}_Sbf0_{h}", always_sync=True)
        for t in range(self.dbg_ntile):
            self.phase_a_tile(s, t)
            lo = nw * t // max(ntile, 1)
            hi = nw * (t + 1) // max(ntile, 1)
            for w in weave[lo:hi]:
                w()

    def emit_scalar_chain(self, name, emitlist, reads, writes, eng="dve"):
        for i, (fn, r, w) in enumerate(emitlist):
            self.p.task(eng, fn, reads=r, writes=w, label=f"{name}{i}")

    def emit_collective(self, name, src_col, dst_col, ccd_base):
        p = self.p
        nco = self.nc
        if getattr(self, "cc_local", False):
            return p.task("dve", lambda e: e.tensor_copy(
                out=self.scal[0:1, dst_col:dst_col + 1],
                in_=self.scal[0:1, src_col:src_col + 1]),
                          reads=[f"scol{src_col}"], writes=[f"scol{dst_col}"],
                          label=name + "_local")

        def emit(e):
            e.dma_start(out=self.h_cc_in[0:1, 0:1], in_=self.scal[0:1, src_col:src_col + 1]
                        ).then_inc(self.ccd, 16)
            e.wait_ge(self.ccd, self.ccd_count[0] + 16)
            self.ccd_count[0] += 16
            e.collective_compute(
                "AllReduce", ALU.max, replica_groups=[list(range(NCORES))],
                ins=[self.h_cc_in[0:1, 0:1]], outs=[self.h_cc_out[0:1, 0:1]]
                ).then_inc(self.ccc, 1)
            self.ccc_count[0] += 1
            e.wait_ge(self.ccc, self.ccc_count[0])
            e.dma_start(out=self.scal[0:1, dst_col:dst_col + 1], in_=self.h_cc_out[0:1, 0:1]
                        ).then_inc(self.ccd, 16)
            self.ccd_count[0] += 16
            return e.wait_ge(self.ccd, self.ccd_count[0])

        return p.task("pool", emit, reads=[f"scol{src_col}"], writes=[f"scol{dst_col}"],
                      label=name)

    def scol(self, i, width=1):
        return self.scal[0:1, i:i + width]

    def scalar_finalize_max(self, acc_col_ap, acc_key, out_col, tag):
        p = self.p
        p.task("pe", lambda e: e.transpose(self.tf32[0:1, 0:128], acc_col_ap,
                                           self.c_ident_f32[:, :]),
               reads=[acc_key, "c_ident_f32"], writes=["tf32"], label=tag + "_tr")
        p.task("dve", lambda e: e.tensor_reduce(
            out=self.scal[0:1, out_col:out_col + 1], in_=self.tf32[0:1, 0:128],
            axis=AX.X, op=ALU.max),
               reads=["tf32"], writes=[f"scol{out_col}"], label=tag + "_red")

    def bcast_col(self, src_col, dst_view, dst_key, tag, parts=128, deps=()):
        p = self.p
        p.task("pe", lambda e: e.matmul(self.tf32[:, 0:1], self.c_ones_row_f32[:, :],
                                        self.scal[0:1, src_col:src_col + 1],
                                        start=True, stop=True, skip_group_check=True),
               reads=[f"scol{src_col}", "c_ones_row_f32"], writes=["tf32"],
               label=tag + "_bc")
        p.task("dve", lambda e: e.tensor_copy(out=dst_view, in_=self.tf32[:, 0:1]),
               reads=["tf32"], writes=[dst_key], deps=deps, label=tag + "_bcv")

    def emit_wscale(self, wi, out_col, tag):
        p = self.p
        self.scalar_finalize_max(self.wmax[:, wi:wi + 1], f"wmax{wi}", 24 + wi, tag)
        p.task("dve", lambda e, wi=wi: e.tensor_scalar(
            out=self.scol(26 + wi), in0=self.scol(24 + wi), scalar1=1.0 / FP8MAX,
            scalar2=1e-6, op0=ALU.mult, op1=ALU.max),
               reads=[f"scol{24 + wi}"], writes=[f"scol{26 + wi}"], label=tag + "_sw")
        p.task("dve", lambda e, wi=wi: e.tensor_scalar(
            out=self.scol(28 + wi), in0=self.scol(26 + wi), scalar1=2.0, scalar2=None,
            op0=ALU.mult),
               reads=[f"scol{26 + wi}"], writes=[f"scol{28 + wi}"], label=tag + "_2s")
        p.task("dve", lambda e, wi=wi, out_col=out_col: e.reciprocal(
            out=self.scol(out_col), in_=self.scol(28 + wi)),
               reads=[f"scol{28 + wi}"], writes=[f"scol{out_col}"], label=tag + "_rc")

    def wquant_pieces(self, nm, rscol_view, rskey, tag):
        p = self.p
        w = self.w_in[nm]
        dst = self.h_q8bf[nm]
        R, Cc = w.ap().shape
        w2 = w.ap().rearrange("(a p) b -> a p b", p=128)
        d2 = dst.ap().rearrange("(a p) b -> a p b", p=128)
        q8 = self.q8v
        pieces = []

        def mk_piece(a, cq):
            def emit():
                idx = self._wm_idx
                self._wm_idx += 1
                buf = self.wmtile[idx % 3]
                key = f"wmt{idx % 3}"
                i = idx % 2
                p.task("sp", lambda e, a=a, cq=cq, b=buf, w2=w2: e.dma_start(
                    out=b[:, :], in_=w2[a, :, cq * 1024:(cq + 1) * 1024]),
                       writes=[key], dma=key, label=tag + "_ld")
                p.task("dve", lambda e, b=buf, i=i: e.tensor_scalar(
                    out=q8[0:128, i * 1024:(i + 1) * 1024], in0=b[:, :],
                    scalar1=rscol_view, scalar2=None, op0=ALU.mult),
                       reads=[key, rskey], writes=[f"q8scr{i}"], label=tag + "_q")
                p.task("pool", lambda e, a=a, cq=cq, i=i, d2=d2: e.dma_start(
                    out=d2[a, :, cq * 1024:(cq + 1) * 1024],
                    in_=q8[0:128, i * 1024:(i + 1) * 1024]),
                       reads=[f"q8scr{i}"], writes=[f"hq8_{nm}"], dma=f"q8st_{nm}{i}",
                       label=tag + "_st")
            return emit
        for a in range(R // 128):
            for cq in range(Cc // 1024):
                pieces.append(mk_piece(a, cq))
        return pieces

    def phase_b(self):
        p = self.p
        fB = self.emit_fence("fenceB")
        self.tap("h_x1_full", self.h_x1.ap(), [D, SEQS * L], F32,
                 reads=[], deps=(fB,))
        self.ar_off = self.a_base
        w8T = self.view("w8T", 128, NKC * FF, FP8)
        land = [self.view(f"land{i}", 128, FF, BF16) for i in range(2)]
        x1t = [self.view(f"x1t{i}", 128, 512, F32) for i in range(4)]
        x8 = self.view("x8", 128, NKC * 512, FP8)
        rsc = [self.view(f"rsc{i}", 128, 512, F32) for i in range(4)]
        mx = self.view("mxB", 128, 4, F32)
        rsx2col = self.view("rsx2col", 128, 1, F32)
        p.task("dve", lambda e: e.memset(mx[:, 0:1], 0.0), writes=["mxB0"],
               deps=(fB,), always_sync=True)
        self.bcast_col(13, rsx2col[:, :], "rsx2col", "bcx", deps=(fB,))
        for k in range(NKC):
            i = k % 2
            lb = land[i]
            p.task("sp", lambda e, k=k, lb=lb: e.dma_start(
                out=lb[:, :], in_=self.h_q8bf["wfc"][0:FF, k * 128:(k + 1) * 128],
                transpose=True),
                   reads=["hq8_wfc"], writes=[f"land{i}"], deps=(fB,), dma=f"land{i}",
                   label=f"bslab{k}")
            p.task("dve", lambda e, k=k, lb=lb: e.tensor_copy(
                out=w8T[:, k * FF:(k + 1) * FF], in_=lb[:, :]),
                   reads=[f"land{i}"], writes=[f"w8T{k}"], label=f"bslabc{k}")
        pbanks = [self.p0, self.p1]
        for tile in range(SEQS * L // TT):
            gt0 = tile * TT
            tg = f"B{tile}"
            for f in range(NKC):
                i = f % 4
                buf = x1t[i]
                p.task("sp", lambda e, f=f, gt0=gt0, b=buf: e.dma_start(
                    out=b[:, :], in_=self.h_x1[f * 128:(f + 1) * 128, gt0:gt0 + TT]),
                       reads=[f"hx1_{f}_{gt0 // TT}"], writes=[f"x1t{i}"], dma=f"x1t{i}",
                       label=tg + f"_xld{f}")
                p.task("dve", lambda e, f=f, b=buf: e.tensor_scalar(
                    out=x8[:, f * 512:(f + 1) * 512], in0=b[:, :],
                    scalar1=rsx2col[:, :], scalar2=None, op0=ALU.mult),
                       reads=[f"x1t{i}", "rsx2col"], writes=[f"x8_{f}"],
                       label=tg + f"_q{f}")
            for m in range(FF // 128):
                pb = pbanks[m % 2]
                pkey = f"p{m % 2}"
                for k in range(NKC):
                    p.task("pe", lambda e, m=m, k=k, pb=pb: e.matmul(
                        pb[:, 0:512], w8T[:, k * FF + m * 128: k * FF + (m + 1) * 128],
                        x8[:, k * 512:(k + 1) * 512],
                        start=(k == 0), stop=(k == NKC - 1)),
                           reads=[f"w8T{k}", f"x8_{k}"], writes=[pkey],
                           label=tg + f"_mm{m}")
                i = m % 4
                rb = rsc[i]
                p.task("act", lambda e, pb=pb, rb=rb: e.activation(
                    out=rb[:, :], in_=pb[:, 0:512], func=ACTF.Relu),
                       reads=[pkey], writes=[f"rsc{i}"], label=tg + f"_relu{m}")
                p.task("dve", lambda e, rb=rb: e.tensor_reduce(
                    out=mx[:, 1:2], in_=rb[:, :], axis=AX.X, op=ALU.max),
                       reads=[f"rsc{i}"], writes=["mxB1"], label=tg + f"_mx{m}")
                p.task("dve", lambda e: e.tensor_tensor(
                    out=mx[:, 0:1], in0=mx[:, 0:1], in1=mx[:, 1:2], op=ALU.max),
                       reads=["mxB1", "mxB0"], writes=["mxB0"], label=tg + f"_mxa{m}")
                p.task("act", lambda e, m=m, gt0=gt0, rb=rb: e.dma_start(
                    out=self.h_r[m * 128:(m + 1) * 128, gt0:gt0 + TT], in_=rb[:, :]),
                       reads=[f"rsc{i}"], writes=[f"hr_{m}_{gt0 // TT}"], dma=f"rst{i}",
                       label=tg + f"_rst{m}")
        self.scalar_finalize_max(mx[:, 0:1], "mxB0", 16, "mxr")
        self.emit_collective("cc_sh", 16, 17, 1)
        p.task("dve", lambda e: e.tensor_tensor(
            out=self.scol(18), in0=self.scol(12), in1=self.scol(26), op=ALU.mult),
               reads=["scol12", "scol26"], writes=["scol18"], label="c_mul")
        p.task("dve", lambda e: e.tensor_scalar(
            out=self.scol(18), in0=self.scol(18), scalar1=4.0, scalar2=None, op0=ALU.mult),
               reads=["scol18"], writes=["scol18"], label="c_4")
        p.task("dve", lambda e: e.tensor_tensor(
            out=self.scol(24), in0=self.scol(17), in1=self.scol(18), op=ALU.mult),
               reads=["scol17", "scol18"], writes=["scol24"], label="mh1")
        p.task("dve", lambda e: e.tensor_tensor(
            out=self.scol(24), in0=self.scol(24), in1=self.scol(24), op=ALU.mult),
               reads=["scol24"], writes=["scol24"], label="mh2")
        p.task("dve", lambda e: e.tensor_scalar(
            out=self.scol(19), in0=self.scol(24), scalar1=1.0 / FP8MAX, scalar2=1e-6,
            op0=ALU.mult, op1=ALU.max),
               reads=["scol24"], writes=["scol19"], label="sh")
        p.task("dve", lambda e: e.tensor_tensor(
            out=self.scol(25), in0=self.scol(18), in1=self.scol(18), op=ALU.mult),
               reads=["scol18"], writes=["scol25"], label="kq1")
        p.task("dve", lambda e: e.tensor_scalar(
            out=self.scol(30), in0=self.scol(19), scalar1=2.0, scalar2=None, op0=ALU.mult),
               reads=["scol19"], writes=["scol30"], label="kq2")
        p.task("dve", lambda e: e.reciprocal(out=self.scol(31), in_=self.scol(30)),
               reads=["scol30"], writes=["scol31"], label="kq3")
        p.task("dve", lambda e: e.tensor_tensor(
            out=self.scol(20), in0=self.scol(25), in1=self.scol(31), op=ALU.mult),
               reads=["scol25", "scol31"], writes=["scol20"], label="kq4")
        p.task("dve", lambda e: e.tensor_tensor(
            out=self.scol(21), in0=self.scol(19), in1=self.scol(27), op=ALU.mult),
               reads=["scol19", "scol27"], writes=["scol21"], label="c21")
        p.task("dve", lambda e: e.tensor_scalar(
            out=self.scol(21), in0=self.scol(21), scalar1=4.0, scalar2=None, op0=ALU.mult),
               reads=["scol21"], writes=["scol21"], label="c22")

    def emit_fence(self, name):
        p = self.p
        exempt = ("hbf_", "hwt_", "hq8_", "scol", "c_", "convw", "wba", "dtb",
                  "onw", "nega", "alog")
        keys = [k for k in list(p.buf_writer.keys())
                if not any(k.startswith(x) for x in exempt)]
        return p.task("dve", lambda e: e.engine_nop(), writes=keys, label=name)

    def phase_ab_transition(self):
        p = self.p
        self.scalar_finalize_max(self.amax[:, 0:1], "amax0", 10, "amx")
        self.emit_collective("cc_sx", 10, 11, 0)
        p.task("dve", lambda e: e.tensor_scalar(
            out=self.scol(12), in0=self.scol(11), scalar1=1.0 / FP8MAX, scalar2=1e-6,
            op0=ALU.mult, op1=ALU.max),
               reads=["scol11"], writes=["scol12"], label="sx")
        p.task("dve", lambda e: e.tensor_scalar(
            out=self.scol(29), in0=self.scol(12), scalar1=2.0, scalar2=None, op0=ALU.mult),
               reads=["scol12"], writes=["scol29"], label="sx2")
        p.task("dve", lambda e: e.reciprocal(out=self.scol(13), in_=self.scol(29)),
               reads=["scol29"], writes=["scol13"], label="rsx2")

    def phase_c(self):
        p = self.p
        fC = self.emit_fence("fenceC")
        self.tap("h_r_full", self.h_r.ap(), [FF, SEQS * L], F32,
                 reads=[], deps=(fC,))
        self.ar_off = self.a_base
        wp8T = self.view("wp8T", 128, 32 * D, FP8)
        landc = [self.view(f"landc{i}", 128, D, BF16) for i in range(2)]
        h8 = self.view("h8", 128, 32 * 512, FP8)
        rt = [self.view(f"rt{i}", 128, 512, F32) for i in range(4)]
        pc = [self.view(f"pc{i}", 128, 512, F32) for i in range(4)]
        x1c = [self.view(f"x1c{i}", 128, 512, F32) for i in range(4)]
        y2 = self.view("y2", 128, 16 * 512, F32)
        otm = [self.view(f"otm{i}", 128, D, F32) for i in range(2)]
        kqcol = self.view("kqcol", 128, 1, F32)
        c2col = self.view("c2col", 128, 1, F32)
        csA = self.view("csA", 128, 512, BF16)
        csB = self.view("csB", 128, 512, BF16)
        sqA = self.view("sqA", 128, 512, BF16)
        sqB = self.view("sqB", 128, 512, BF16)
        stc = self.view("stc", 1, 3072, F32)
        self.bcast_col(20, kqcol[:, :], "kqcol", "bkq", deps=(fC,))
        self.bcast_col(21, c2col[:, :], "c2col", "bc2", deps=(fC,))
        pbanks = [self.p0, self.p1]
        NT8 = SEQS * L // TT
        for half in range(2):
            hk0 = half * 32
            for kk in range(32):
                k = hk0 + kk
                i = kk % 2
                lb = landc[i]
                p.task("sp", lambda e, k=k, lb=lb: e.dma_start(
                    out=lb[:, :], in_=self.h_q8bf["wproj"][0:D, k * 128:(k + 1) * 128],
                    transpose=True),
                       reads=["hq8_wproj"], writes=[f"landc{i}"], deps=(fC,),
                       dma=f"landc{i}", label=f"cslab{k}")
                p.task("dve", lambda e, kk=kk, lb=lb: e.tensor_copy(
                    out=wp8T[:, kk * D:(kk + 1) * D], in_=lb[:, :]),
                       reads=[f"landc{i}"], writes=[f"wp8T{kk}"], label=f"cslabc{kk}")
            for tile in range(NT8):
                gt0 = tile * TT
                tg = f"C{half}_{tile}"
                for kk in range(32):
                    k = hk0 + kk
                    i = kk % 4
                    buf = rt[i]
                    p.task("sp", lambda e, k=k, gt0=gt0, b=buf: e.dma_start(
                        out=b[:, :], in_=self.h_r[k * 128:(k + 1) * 128, gt0:gt0 + TT]),
                           reads=[f"hr_{k}_{gt0 // TT}"], writes=[f"rt{i}"], dma=f"rt{i}",
                           label=tg + f"_rld{kk}")
                    p.task("dve", lambda e, kk=kk, b=buf: e.scalar_tensor_tensor(
                        out=h8[:, kk * 512:(kk + 1) * 512], in0=b[:, :],
                        scalar=kqcol[:, :], in1=b[:, :], op0=ALU.mult, op1=ALU.mult),
                           reads=[f"rt{i}", "kqcol"], writes=[f"h8_{kk}"],
                           label=tg + f"_hq{kk}")
                for m in range(NKC):
                    pb = pbanks[m % 2]
                    pkey = f"p{m % 2}"
                    for kk in range(32):
                        p.task("pe", lambda e, m=m, kk=kk, pb=pb: e.matmul(
                            pb[:, 0:512], wp8T[:, kk * D + m * 128: kk * D + (m + 1) * 128],
                            h8[:, kk * 512:(kk + 1) * 512],
                            start=(kk == 0), stop=(kk == 31)),
                               reads=[f"wp8T{kk}", f"h8_{kk}"], writes=[pkey],
                               label=tg + f"_mm{m}")
                    if half == 0:
                        i = m % 4
                        ob = pc[i]
                        p.task("dve", lambda e, pb=pb, ob=ob: e.tensor_copy(
                            out=ob[:, :], in_=pb[:, 0:512]),
                               reads=[pkey], writes=[f"pc{i}"], label=tg + f"_pev{m}")
                        p.task("act", lambda e, m=m, gt0=gt0, ob=ob: e.dma_start(
                            out=self.h_pc[m * 128:(m + 1) * 128, gt0:gt0 + TT], in_=ob[:, :]),
                               reads=[f"pc{i}"], writes=[f"hpc_{m}_{tile}"], dma=f"pcst{i}",
                               label=tg + f"_pst{m}")
                    else:
                        i = m % 4
                        ob = pc[i]
                        xb = x1c[i]
                        p.task("sp", lambda e, m=m, gt0=gt0, ob=ob: e.dma_start(
                            out=ob[:, :], in_=self.h_pc[m * 128:(m + 1) * 128, gt0:gt0 + TT]),
                               reads=[f"hpc_{m}_{tile}"], writes=[f"pc{i}"], dma=f"pcld{i}",
                               label=tg + f"_pld{m}")
                        p.task("sp", lambda e, m=m, gt0=gt0, xb=xb: e.dma_start(
                            out=xb[:, :], in_=self.h_x1[m * 128:(m + 1) * 128, gt0:gt0 + TT]),
                               reads=[f"hx1_{m}_{gt0 // TT}"], writes=[f"x1c{i}"],
                               dma=f"x1c{i}", label=tg + f"_xld{m}")
                        p.task("dve", lambda e, m=m, pb=pb, ob=ob: e.tensor_tensor(
                            out=y2[:, m * 512:(m + 1) * 512], in0=pb[:, 0:512],
                            in1=ob[:, :], op=ALU.add),
                               reads=[pkey, f"pc{i}"], writes=[f"y2_{m}"],
                               label=tg + f"_acc{m}")
                        p.task("dve", lambda e, m=m: e.tensor_scalar(
                            out=y2[:, m * 512:(m + 1) * 512], in0=y2[:, m * 512:(m + 1) * 512],
                            scalar1=c2col[:, :], scalar2=None, op0=ALU.mult),
                               reads=["c2col", f"y2_{m}"], writes=[f"y2_{m}"],
                               label=tg + f"_dq{m}")
                        p.task("dve", lambda e, m=m, xb=xb: e.tensor_tensor(
                            out=y2[:, m * 512:(m + 1) * 512], in0=y2[:, m * 512:(m + 1) * 512],
                            in1=xb[:, :], op=ALU.add),
                               reads=[f"x1c{i}", f"y2_{m}"], writes=[f"y2_{m}"],
                               label=tg + f"_res{m}")
                if half == 1:
                    self.emit_ln2_out(tile, gt0, y2, csA, csB, sqA, sqB, stc, otm, tg)

    def emit_ln2_out(self, tile, gt0, y2, csA, csB, sqA, sqB, stc, otm, tg):
        p = self.p
        for f in range(NKC):
            sq = (sqA, sqB)[f % 2]
            yb = (csA, csB)[f % 2]
            sqk = ("sqA", "sqB")[f % 2]
            ybk = ("csA", "csB")[f % 2]
            p.task("act", lambda e, f=f, sq=sq: e.activation(
                out=sq[:, :], in_=y2[:, f * 512:(f + 1) * 512], func=ACTF.Square),
                   reads=[f"y2_{f}"], writes=[sqk], label=tg + f"_ysq{f}")
            p.task("dve", lambda e, f=f, yb=yb: e.tensor_copy(
                out=yb[:, :], in_=y2[:, f * 512:(f + 1) * 512]),
                   reads=[f"y2_{f}"], writes=[ybk], label=tg + f"_ybf{f}")
            p.task("pe", lambda e, f=f, yb=yb: e.matmul(
                self.p5[0:1, 0:512], self.c_ones_col_bf[:, :], yb[:, :],
                start=(f == 0), stop=(f == 15), skip_group_check=True),
                   reads=[ybk, "c_ones_col_bf"], writes=["p5"], label=tg + f"_ss{f}")
            p.task("pe", lambda e, f=f, sq=sq: e.matmul(
                self.p3[0:1, 0:512], self.c_ones_col_bf[:, :], sq[:, :],
                start=(f == 0), stop=(f == 15), skip_group_check=True),
                   reads=[sqk, "c_ones_col_bf"], writes=["p3"], label=tg + f"_sq{f}")
        st = stc
        p.task("dve", lambda e: e.tensor_scalar(
            out=st[0:1, 0:512], in0=self.p5[0:1, 0:512], scalar1=1.0 / D, scalar2=None,
            op0=ALU.mult), reads=["p5"], writes=["ct_mu"], label=tg + "_mu")
        p.task("dve", lambda e: e.tensor_scalar(
            out=st[0:1, 512:1024], in0=self.p3[0:1, 0:512], scalar1=1.0 / D, scalar2=None,
            op0=ALU.mult), reads=["p3"], writes=["ct_m2"], label=tg + "_m2")
        p.task("dve", lambda e: e.tensor_tensor(
            out=st[0:1, 1024:1536], in0=st[0:1, 0:512], in1=st[0:1, 0:512], op=ALU.mult),
               reads=["ct_mu"], writes=["ct_t"], label=tg + "_musq")
        p.task("dve", lambda e: e.tensor_tensor(
            out=st[0:1, 1024:1536], in0=st[0:1, 512:1024], in1=st[0:1, 1024:1536],
            op=ALU.subtract), reads=["ct_m2", "ct_t"], writes=["ct_t"], label=tg + "_var")
        p.task("dve", lambda e: e.tensor_scalar(
            out=st[0:1, 1024:1536], in0=st[0:1, 1024:1536], scalar1=EPS_LN, scalar2=None,
            op0=ALU.add), reads=["ct_t"], writes=["ct_t"], label=tg + "_veps")
        p.task("dve", lambda e: e.reciprocal(out=st[0:1, 1536:2048], in_=st[0:1, 1024:1536]),
               reads=["ct_t"], writes=["ct_rc"], label=tg + "_vrc")
        p.task("act", lambda e: e.activation(out=st[0:1, 1536:2048], in_=st[0:1, 1536:2048],
                                             func=ACTF.Sqrt),
               reads=["ct_rc"], writes=["ct_rc"], label=tg + "_inv")
        p.task("dve", lambda e: e.scalar_tensor_tensor(
            out=st[0:1, 2048:2560], in0=st[0:1, 1536:2048], scalar=-1.0, in1=st[0:1, 0:512],
            op0=ALU.mult, op1=ALU.mult),
               reads=["ct_rc", "ct_mu"], writes=["ct_b"], label=tg + "_bb")
        p.task("pe", lambda e: e.matmul(self.p3[:, 0:512], self.c_ones_row_f32[:, :],
                                        st[0:1, 1536:2048], start=True, stop=True,
                                        skip_group_check=True),
               reads=["ct_rc", "c_ones_row_f32"], writes=["p3"], label=tg + "_bca")
        p.task("pe", lambda e: e.matmul(self.p4[:, 0:512], self.c_ones_row_f32[:, :],
                                        st[0:1, 2048:2560], start=True, stop=True,
                                        skip_group_check=True),
               reads=["ct_b", "c_ones_row_f32"], writes=["p4"], label=tg + "_bcb")
        for f in range(NKC):
            p.task("dve", lambda e, f=f: e.tensor_tensor(
                out=y2[:, f * 512:(f + 1) * 512], in0=y2[:, f * 512:(f + 1) * 512],
                in1=self.p3[:, 0:512], op=ALU.mult),
                   reads=["p3", f"y2_{f}"], writes=[f"y2_{f}"], label=tg + f"_na{f}")
            p.task("dve", lambda e, f=f: e.tensor_tensor(
                out=y2[:, f * 512:(f + 1) * 512], in0=y2[:, f * 512:(f + 1) * 512],
                in1=self.p4[:, 0:512], op=ALU.add),
                   reads=["p4", f"y2_{f}"], writes=[f"y2_{f}"], label=tg + f"_nb{f}")
        s = gt0 // L
        l0 = gt0 % L
        for b in range(4):
            ob = otm[b % 2]
            obk = f"otm{b % 2}"
            for f in range(NKC):
                p.task("pe", lambda e, f=f, b=b: e.transpose(
                    self.tf32[:, (f % 4) * 128:(f % 4 + 1) * 128],
                    y2[:, f * 512 + b * 128: f * 512 + (b + 1) * 128],
                    self.c_ident_f32[:, :]),
                       reads=[f"y2_{f}", "c_ident_f32"], writes=["tf32"],
                       label=tg + f"_otr{b}_{f}")
                if f % 4 == 3:
                    p.task("dve", lambda e, f=f, ob=ob: e.tensor_copy(
                        out=ob[:, (f - 3) * 128:(f + 1) * 128], in_=self.tf32[:, 0:512]),
                           reads=["tf32"], writes=[obk], label=tg + f"_oev{b}_{f}")
            p.task("act", lambda e, s=s, l0=l0, b=b, ob=ob: e.dma_start(
                out=self.out[s, l0 + b * 128: l0 + (b + 1) * 128, :], in_=ob[:, :]),
                   reads=[obk], writes=[f"outst_{tile}_{b}"], dma=f"outst{b % 2}",
                   label=tg + f"_ost{b}")

    def finalize(self):
        p = self.p
        self.tap("scal_final", self.scal[0:1, 0:32], [1, 32], F32,
                 reads=[f"scol{i}" for i in (10, 11, 12, 13, 16, 17, 18, 19, 20, 21)
                        if f"scol{i}" in p.buf_writer])
        keys = [k for k in p.buf_writer.keys() if k.startswith("outst_")]
        p.task("sp", lambda e: e.nop(), reads=keys, deps=self.tap_list,
               label="final_fence")

    def build(self, phases="abc", ntile=NTILE, nseq=SEQS):
        self.dbg_ntile = ntile
        self.dbg_nseq = nseq
        self.setup_memory()
        self.declare_io()
        self.setup_views()
        self.tap_list = []
        self.prologue()
        wv0 = ()
        wq_pj = []
        if "b" in phases:
            self.rswfc = View(self.wmax.ap[:, 2:4], "rswfc")
            n_fc = (FF // 128) * (D // 1024)
            wm_fc, wm_pj = self.wm_pieces[:n_fc], self.wm_pieces[n_fc:]

            def sc_fc():
                self.emit_wscale(0, 6, "wsc_fc")
                self.bcast_col(6, self.rswfc[:, 0:1], "rswfc0", "br6")

            def sc_pj():
                self.emit_wscale(1, 7, "wsc_pj")
                self.bcast_col(7, self.rswfc[:, 1:2], "rswfc1", "br7")
            wq_fc = self.wquant_pieces("wfc", self.rswfc[:, 0:1], "rswfc0", "qfc")
            wq_pj = self.wquant_pieces("wproj", self.rswfc[:, 1:2], "rswfc1", "qpj")
            inter = [x for pair in zip(wm_pj, wq_fc) for x in pair]
            wv0 = wm_fc + [sc_fc] + inter + [sc_pj]
        self.phase_a_seq(0, weave=wv0)
        if nseq > 1:
            self.phase_a_seq(1, weave=wq_pj)
        else:
            for w in wq_pj:
                w()
        if "b" in phases:
            self.phase_ab_transition()
            self.phase_b()
        if "c" in phases:
            self.phase_c()
        self.finalize()




ENG2BLOCK = {"pe": "tensor", "act": "scalar", "dve": "vector", "pool": "gpsimd",
             "sp": "sync"}


def emit_program(bld):
    p = bld.p
    nc = bld.nc
    bulk = {"cld"}
    per_eng = p.plan()
    final_dma = dict(p.dma_sem_count)
    sems = {}
    for e in Prog.ENGS:
        sems[("eng", e)] = nc.alloc_semaphore("sem_" + e)
    for name in p.dma_sem_count:
        sems[("dma", name)] = nc.alloc_semaphore("dsem_" + name)
    bld.ccd = nc.alloc_semaphore("ccd")
    bld.ccc = nc.alloc_semaphore("ccc")
    bld.ccd_count = [0]
    bld.ccc_count = [0]

    def stream(ename):
        def emitter(eng):
            for t in per_eng[ename]:
                for (key, val) in t.waits:
                    if key[0] == "dma" and key[1] in bulk:
                        val = 16 * final_dma[key[1]]
                    eng.wait_ge(sems[key], val)
                last = t.emit(eng)
                if t.dma_sem is not None:
                    last.then_inc(sems[("dma", t.dma_sem)], 16)
                elif t.milestone is not None:
                    last.then_inc(sems[("eng", t.eng)], 1)
        return emitter

    with nc.Block() as block:
        block.tensor(stream("pe"))
        block.scalar(stream("act"))
        block.vector(stream("dve"))
        block.gpsimd(stream("pool"))
        block.sync(stream("sp"))
    return per_eng


def build_program(taps=(), phases="abc", ntile=NTILE, nseq=SEQS,
                  paranoid=frozenset({"dve"}), max_tasks=None, cc_local=False):
    nc = bass.Bass("TRN2", target_bir_lowering=False, debug=False,
                   detect_race_conditions=False)
    bld = Builder(nc, taps, paranoid=paranoid)
    bld.cc_local = cc_local
    bld.build(phases=phases, ntile=ntile, nseq=nseq)
    if max_tasks is not None:
        bld.p.tasks = bld.p.tasks[:max_tasks]
    emit_program(bld)
    return nc, bld



_CACHE = {}


def _in_maps(inputs):
    x = np.ascontiguousarray(inputs["L0_attn_x"])
    base = {
        "Wq": inputs["L0_attn_Wq"], "Wk": inputs["L0_attn_Wk"],
        "Wv": inputs["L0_attn_Wv"], "Wb": inputs["L0_attn_Wb"],
        "Wa": inputs["L0_attn_Wa"], "Wg": inputs["L0_attn_Wg"],
        "Wo": inputs["L0_attn_Wo"],
        "A_log": np.ascontiguousarray(inputs["L0_attn_A_log"]).reshape(8, 1),
        "dt_bias": np.ascontiguousarray(inputs["L0_attn_dt_bias"]).reshape(8, 1),
        "conv_q": inputs["L0_attn_conv_q"], "conv_k": inputs["L0_attn_conv_k"],
        "conv_v": inputs["L0_attn_conv_v"],
        "o_norm_w": np.ascontiguousarray(inputs["L0_attn_o_norm_w"]).reshape(128, 1),
        "wfc": inputs["L0_ffn_w_fc"], "wproj": inputs["L0_ffn_w_proj"],
    }
    base = {k: np.ascontiguousarray(v) for k, v in base.items()}
    for nm, arr in CONSTS.items():
        base["c" + nm] = arr
    maps = []
    for c in range(NCORES):
        m = dict(base)
        m["x"] = np.ascontiguousarray(x[SEQS * c:SEQS * (c + 1)])
        maps.append(m)
    return maps


def kernel(**inputs):
    if "nc" not in _CACHE:
        _CACHE["nc"] = build_program()[0]
    nc = _CACHE["nc"]
    maps = _in_maps(inputs)
    res = run_bass_kernel_spmd(nc, maps, core_ids=list(range(NCORES)))
    out = np.concatenate([res.results[c]["out"] for c in range(NCORES)], axis=0)
    return out.astype(np.float32)


# version 11
# speedup vs baseline: 1.0467x; 1.0205x over previous
import numpy as np
import ml_dtypes
import concourse.bass as bass
import concourse.mybir as mybir
from concourse.bass_utils import run_bass_kernel_spmd

F32 = mybir.dt.float32
BF16 = mybir.dt.bfloat16
FP8 = mybir.dt.float8e4
ALU = mybir.AluOpType
ACTF = mybir.ActivationFunctionType
AX = mybir.AxisListType
BF = ml_dtypes.bfloat16

B, L, D = 16, 2048, 2048
NCORES = 8
SEQS = 2
H, DH, NH = 8, 128, 2
KD = H * DH
TT = 512
NTILE = L // TT
CH = 64
CPT = TT // CH
FF = 4 * D
NKC = D // 128
EPS_LN = 1e-5
EPS_ORMS = 1e-5
FP8MAX = 448.0

PG_BETA, PG_PA, PG_PW, PG_MQ, PG_GEND, PG_NBGR, PG_W, PG_DECAY, PG_GRQ = range(9)
NPG = 9



class Task:
    __slots__ = ("tid", "eng", "emit", "deps", "milestone", "dma_sem", "dma_val",
                 "snapshot", "waits", "need_inc", "label", "always_sync")

    def __init__(self, tid, eng, emit, label=""):
        self.always_sync = False
        self.tid = tid
        self.eng = eng
        self.emit = emit
        self.deps = []
        self.milestone = None
        self.dma_sem = None
        self.dma_val = None
        self.snapshot = None
        self.waits = None
        self.need_inc = False
        self.label = label


class Prog:
    ENGS = ("pe", "act", "dve", "pool", "sp")

    def __init__(self):
        self.paranoid = False
        self.tasks = []
        self.buf_writer = {}
        self.buf_readers = {}
        self.dma_sem_count = {}

    def new_dma_sem(self, name):
        assert name not in self.dma_sem_count
        self.dma_sem_count[name] = 0
        return name

    def task(self, eng, emit, reads=(), writes=(), deps=(), dma=None, label="",
             always_sync=False):
        t = Task(len(self.tasks), eng, emit, label)
        par = self.paranoid
        if isinstance(par, (set, frozenset, list, tuple)):
            par = eng in par
        t.always_sync = always_sync or bool(par)
        dd = set()
        for d in deps:
            if d is not None:
                dd.add(d.tid if isinstance(d, Task) else int(d))
        for k in reads:
            w = self.buf_writer.get(k)
            if w is not None:
                dd.add(w)
        for k in writes:
            w = self.buf_writer.get(k)
            if w is not None:
                dd.add(w)
            for r in self.buf_readers.get(k, ()):
                dd.add(r)
        dd.discard(t.tid)
        t.deps = sorted(dd)
        if dma is not None:
            if dma not in self.dma_sem_count:
                self.new_dma_sem(dma)
            t.dma_sem = dma
            self.dma_sem_count[dma] += 1
            t.dma_val = 16 * self.dma_sem_count[dma]
        for k in reads:
            self.buf_readers.setdefault(k, []).append(t.tid)
        for k in writes:
            self.buf_writer[k] = t.tid
            self.buf_readers[k] = []
        self.tasks.append(t)
        return t

    def plan(self):
        for t in self.tasks:
            for d in t.deps:
                dt = self.tasks[d]
                if dt.dma_sem is None and (dt.eng != t.eng or dt.always_sync):
                    dt.need_inc = True
        counts = {e: 0 for e in self.ENGS}
        for t in self.tasks:
            if t.dma_sem is None and t.need_inc:
                counts[t.eng] += 1
                t.milestone = counts[t.eng]
        clocks = {e: {} for e in self.ENGS}

        def merge(dst, src):
            for k, v in src.items():
                if dst.get(k, 0) < v:
                    dst[k] = v

        per_eng = {e: [] for e in self.ENGS}
        for t in self.tasks:
            clk = clocks[t.eng]
            waits = []
            for d in t.deps:
                dt = self.tasks[d]
                if dt.dma_sem is not None:
                    key = ("dma", dt.dma_sem)
                    if clk.get(key, 0) < dt.dma_val:
                        waits.append((key, dt.dma_val))
                        clk[key] = dt.dma_val
                        merge(clk, dt.snapshot)
                elif dt.eng == t.eng and not dt.always_sync:
                    merge(clk, dt.snapshot)
                else:
                    key = ("eng", dt.eng)
                    if clk.get(key, 0) < dt.milestone:
                        waits.append((key, dt.milestone))
                        clk[key] = dt.milestone
                        merge(clk, dt.snapshot)
            t.waits = waits
            snap = dict(clk)
            if t.dma_sem is not None:
                k = ("dma", t.dma_sem)
                if snap.get(k, 0) < t.dma_val:
                    snap[k] = t.dma_val
            elif t.milestone is not None:
                snap[("eng", t.eng)] = t.milestone
            t.snapshot = snap
            per_eng[t.eng].append(t)
        return per_eng



def build_consts():
    c = {}
    c["ident_bf"] = np.eye(128, dtype=BF)
    c["ident_f32"] = np.eye(128, dtype=np.float32)
    c["ones_row_f32"] = np.ones((1, 128), np.float32)
    oh = np.zeros((128, 64), BF)
    for h in range(8):
        oh[:, 8 * h + h] = 1
    c["onehot8"] = oh
    c["ones_col_bf"] = np.ones((128, 1), BF)
    NEG = -1e30
    sub = np.array([2 * i if i < 64 else 2 * (i - 64) + 1 for i in range(128)])
    maskA = np.where(sub[None, :] < sub[:, None], 0.0, NEG).astype(np.float32)
    c["maskA"] = maskA
    subo = 2 * np.arange(64) + 1
    maskWT = np.where(sub[:, None] <= subo[None, :], 0.0, NEG).astype(np.float32)
    c["maskWT"] = maskWT
    return c


CONSTS = build_consts()
C_SHAPES = {k: (list(v.shape), F32 if v.dtype == np.float32 else BF16)
            for k, v in CONSTS.items()}



class View:

    def __init__(self, ap, key):
        self.ap = ap
        self.key = key

    def __getitem__(self, idx):
        return self.ap[idx]

    @property
    def full(self):
        return self.ap


class Builder:
    def __init__(self, nc, taps=(), paranoid=False):
        self.nc = nc
        self.p = Prog()
        self.p.paranoid = paranoid
        self.taps = set(taps)
        self.tap_tensors = {}
        self.views = {}
        self.hbm = {}

    def setup_memory(self):
        nc = self.nc
        self.p0 = nc.alloc_psum_tensor("p0", [128, 512], F32)
        self.p1 = nc.alloc_psum_tensor("p1", [128, 512], F32)
        self.p2 = nc.alloc_psum_tensor("p2", [128, 512], F32)
        self.p3 = nc.alloc_psum_tensor("p3", [128, 512], F32)
        self.p4 = nc.alloc_psum_tensor("p4", [128, 512], F32)
        self.p5 = nc.alloc_psum_tensor("p5", [128, 512], F32)
        self.tbf = nc.alloc_psum_tensor("tbf", [128, 1024], BF16)
        self.tf32 = nc.alloc_psum_tensor("tf32", [128, 512], F32)

        for name, (shape, dt) in C_SHAPES.items():
            setattr(self, "c_" + name, nc.alloc_sbuf_tensor("c_" + name, shape, dt))
        self.convw = nc.alloc_sbuf_tensor("convw", [128, 40 * 4], F32)
        self.wba = nc.alloc_sbuf_tensor("wba", [128, NKC * 24], BF16)
        self.dtb_col = nc.alloc_sbuf_tensor("dtb_col", [8, 1], F32)
        self.nega_col = nc.alloc_sbuf_tensor("nega_col", [8, 1], F32)
        self.onw_col = nc.alloc_sbuf_tensor("onw_col", [128, 1], F32)
        self.scal = nc.alloc_sbuf_tensor("scal", [128, 32], F32)
        self.q8scr = nc.alloc_sbuf_tensor("q8scr", [128, 1024], FP8)
        rem = nc.sbuf_bytes_remaining
        arn = rem // 2
        self.AR = nc.alloc_sbuf_tensor("AR", [128, arn], BF16)
        self.ar_off = 0
        self.ar_n = arn

    def view(self, name, parts, n, dtype, offset=None):
        if dtype == F32:
            units = 2 * n
        elif dtype == FP8:
            units = (n + 1) // 2
        else:
            units = n
        units = (units + 31) & ~31
        if offset is None:
            offset = self.ar_off
            self.ar_off += units
            assert self.ar_off <= self.ar_n, f"arena overflow at {name} ({self.ar_off}/{self.ar_n})"
        ap = self.AR[0:parts, offset:offset + units]
        if dtype == F32:
            ap = ap.bitcast(F32)[0:parts, 0:n]
        elif dtype == FP8:
            ap = ap.bitcast(FP8)[0:parts, 0:n]
        else:
            ap = ap[0:parts, 0:n]
        v = View(ap, name)
        self.views[name] = v
        return v

    def dram(self, name, shape, dtype, kind=None):
        if kind:
            h = self.nc.dram_tensor(name, shape, dtype, kind=kind)
        else:
            h = self.nc.dram_tensor(name, shape, dtype)
        self.hbm[name] = h
        return h

    def tap(self, name, src_ap, shape, dtype, reads, deps=()):
        if name not in self.taps:
            return
        t = self.dram("tap_" + name, shape, dtype, kind="ExternalOutput")
        tk = self.p.task("act", lambda e, t=t, s=src_ap: e.dma_start(out=t.ap(), in_=s),
                         reads=list(reads), deps=list(deps), dma="tap_" + name,
                         label="tap_" + name)
        self.tap_list.append(tk)
        self.tap_tensors[name] = t

    def declare_io(self):
        d = self.dram
        self.x_in = d("x", [SEQS, L, D], F32, kind="ExternalInput")
        self.w_in = {}
        for nm, shp in [("Wq", [KD, D]), ("Wk", [2 * KD, D]), ("Wv", [2 * KD, D]),
                        ("Wb", [16, D]), ("Wa", [8, D]), ("Wg", [KD, D]),
                        ("Wo", [D, KD]), ("wfc", [FF, D]), ("wproj", [D, FF])]:
            self.w_in[nm] = d(nm, shp, F32, kind="ExternalInput")
        for nm in ["A_log", "dt_bias"]:
            self.w_in[nm] = d(nm, [8, 1], F32, kind="ExternalInput")
        self.w_in["o_norm_w"] = d("o_norm_w", [128, 1], F32, kind="ExternalInput")
        for nm in ["conv_q", "conv_k", "conv_v"]:
            shp = [KD, 4] if nm == "conv_q" else [2 * KD, 4]
            self.w_in[nm] = d(nm, shp, F32, kind="ExternalInput")
        for nm, (shape, dt) in C_SHAPES.items():
            self.w_in[nm] = d("c" + nm, shape, dt, kind="ExternalInput")
        self.out = d("out", [SEQS, L, D], F32, kind="ExternalOutput")
        self.h_wbf = {}
        for nm, shp in [("Wq", [KD, D]), ("Wk", [2 * KD, D]), ("Wv", [2 * KD, D]),
                        ("Wg", [KD, D]), ("Wo", [D, KD])]:
            self.h_wbf[nm] = d("bf_" + nm, shp, BF16)
        self.h_wt = {}
        for nm, O, IN in [("Wq", KD, D), ("Wk", 2 * KD, D), ("Wv", 2 * KD, D),
                          ("Wg", KD, D), ("Wo", D, KD)]:
            self.h_wt[nm] = d("wt_" + nm, [O // 128, IN // 128, 128, 128], BF16)
        self.h_x1 = d("x1h", [D, SEQS * L], F32)
        self.h_r = d("rh", [FF, SEQS * L], F32)
        self.h_pc = d("pch", [D, SEQS * L], F32)
        self.h_q8bf = {"wfc": d("q8bf_fc", [FF, D], BF16),
                       "wproj": d("q8bf_pj", [D, FF], BF16)}
        self.h_cc_in = d("cc_in", [1, 4], F32)
        self.h_cc_out = d("cc_out", [1, 4], F32)

    def setup_views(self):
        v = self.view
        self.wmtile = [v(f"wmtile{i}", 128, 1024, F32) for i in range(3)]
        self.q8v = v("q8v", 128, 2048, FP8)
        self.wmax = v("wmax", 128, 4, F32)
        self.a_base = self.ar_off
        self.xtm = [v("xtm0", 128, 2048, F32), v("xtm1", 128, 2048, F32)]
        self.xfm = v("xfm", 128, 16 * 512, F32)
        self.xb = v("xb", 128, 16 * 512, BF16)
        self.kbuf = v("kbuf", 128, 8 * 1024, BF16)
        self.vbuf = v("vbuf", 128, 8 * 1024, BF16)
        self.qbuf = v("qbuf", 128, 8 * 512, BF16)
        self.sgbuf = v("sgbuf", 128, 8 * 512, BF16)
        self.pre = [v(f"pre{i}", 128, 520, BF16) for i in range(2)]
        self.cs = [v(f"cs{i}", 128, 512, BF16) for i in range(2)]
        self.hist = v("hist", 128, 40 * 4, BF16)
        rows = ["G", "g", "tmpB", "tmpC"]
        self.r_ = {nm: v("row_" + nm, 8, 512, F32) for nm in rows}
        self.lnrk = v("row_lnrk", 8, 1024, F32)
        self.PWs = v("row_PWs", 8, 1024, F32)
        self.beta_s = v("row_beta_s", 8, 1024, F32)
        self.PA_s = v("row_PA_s", 8, 1024, F32)
        self.MQ_s = v("row_MQ_s", 8, 1024, F32)
        self.GEND_s = v("row_GEND_s", 8, 1024, F32)
        self.param = v("param", 128, CPT * NPG * 8, F32)
        self.S = v("S", 128, 8 * 128, F32)
        self.Sbf = v("Sbf", 128, 8 * 128, BF16)
        self.eA = v("eA", 128, 512, F32)
        self.eW = View(self.eA.ap[:, 0:256], "eA")
        self.kw = v("kw", 128, 512, BF16)
        self.Ngen = [v(f"Ngen{i}", 128, 512, BF16) for i in range(2)]
        self.NTgen = [v(f"NTgen{i}", 128, 512, BF16) for i in range(2)]
        self.U0 = v("U0", 128, 512, BF16)
        self.Ucur = [v(f"Ucur{i}", 128, 512, BF16) for i in range(2)]
        self.watt = View(self.U0.ap[:, 0:256], "U0")
        self.t1s = v("t1s", 128, 512, BF16)
        self.osq = View(self.t1s.ap[0:64, :], "t1s")
        self.dWT = View(self.t1s.ap[:, 0:256], "t1s")
        self.dA = View(self.Ngen[0].ap, "Ngen0")
        self.oraw = View(self.eA.ap[0:64, 0:512], "eA")
        self.rmsacc = v("rmsacc", 64, 16, F32)
        self.on_ = View(self.Ngen[1].ap[0:64, :], "Ngen1")
        self.ofm = v("ofm", 128, 8 * 512, BF16)
        self.wg_ = [v("wg0", 128, 16 * 128, BF16), v("wg1", 128, 16 * 128, BF16)]
        self.ba_bf = View(self.wg_[0].ap[0:24, :], "wg0")
        self.amax = v("amax", 128, 4, F32)
        self.rbank = v("rbank", 1, 768, F32)
        self.a_end = self.ar_off
        assert self.ar_off <= self.ar_n

    def prologue(self):
        p, nc = self.p, self.nc
        for nm in C_SHAPES:
            t = getattr(self, "c_" + nm)
            src = self.w_in[nm]
            p.task("sp", lambda e, t=t, s=src: e.dma_start(out=t[:, :], in_=s.ap()),
                   writes=[f"c_{nm}"], dma="cld", label=f"ld_{nm}")
        for i, nm in enumerate(["conv_k", "conv_v", "conv_q"]):
            nch = 16 if nm != "conv_q" else 8
            base = (0, 16, 32)[i]
            src = self.w_in[nm]
            for c in range(nch):
                cw = self.convw
                p.task("sp", lambda e, cw=cw, s=src, c=c, base=base:
                       e.dma_start(out=cw[:, (base + c) * 4:(base + c) * 4 + 4],
                                   in_=s[c * 128:(c + 1) * 128, :]),
                       writes=[f"convw{base + c}"], dma="cld", label="ld_conv")
        p.task("sp", lambda e: e.dma_start(out=self.dtb_col[:, :], in_=self.w_in["dt_bias"].ap()),
               writes=["dtb"], dma="cld")
        p.task("sp", lambda e: e.dma_start(out=self.scal[0:8, 0:1], in_=self.w_in["A_log"].ap()),
               writes=["alog_raw"], dma="cld")
        p.task("act", lambda e: e.activation(out=self.scal[0:8, 1:2], in_=self.scal[0:8, 0:1],
                                             func=ACTF.Exp),
               reads=["alog_raw"], writes=["alog_exp"], label="expA")
        p.task("dve", lambda e: e.tensor_scalar(out=self.nega_col[:, :], in0=self.scal[0:8, 1:2],
                                                scalar1=-1.0, scalar2=None, op0=ALU.mult),
               reads=["alog_exp"], writes=["nega"], label="negA")
        p.task("sp", lambda e: e.dma_start(out=self.onw_col[:, :], in_=self.w_in["o_norm_w"].ap()),
               writes=["onw"], dma="cld")

        p.task("pool", lambda e: e.dma_start(out=self.ba_bf[0:16, :], in_=self.w_in["Wb"].ap()),
               writes=["wg0"], dma="bald", label="ba_ld1")
        p.task("pool", lambda e: e.dma_start(out=self.ba_bf[16:24, :], in_=self.w_in["Wa"].ap()),
               writes=["wg0"], dma="bald", label="ba_ld2")
        for k in range(NKC):
            p.task("pe", lambda e, k=k: e.transpose(
                self.tbf[0:128, 0:24], self.ba_bf[0:24, k * 128:(k + 1) * 128],
                self.c_ident_bf[0:24, 0:24]),
                   reads=["wg0", "c_ident_bf"], writes=["tbf"], label="ba_tr")
            p.task("dve", lambda e, k=k: e.tensor_copy(
                out=self.wba[:, k * 24:(k + 1) * 24], in_=self.tbf[0:128, 0:24]),
                   reads=["tbf"], writes=[f"wba{k}"], label="ba_ev")

        for nm in ["Wk", "Wv", "Wq", "Wg", "Wo"]:
            src, dst = self.w_in[nm], self.h_wbf[nm]
            p.task("pool", lambda e, s=src, d=dst: e.dma_start(out=d.ap(), in_=s.ap()),
                   reads=[], writes=[f"hbf_{nm}"], dma=f"wcast_{nm}", label=f"cast_{nm}")
        for nm, O, IN in [("Wk", 2 * KD, D), ("Wv", 2 * KD, D), ("Wq", KD, D),
                          ("Wg", KD, D), ("Wo", D, KD)]:
            wbf, wt = self.h_wbf[nm], self.h_wt[nm]
            for ki in range(IN // 128):
                for og in range(O // 512):
                    buf = self.wg_[(ki * (O // 512) + og) % 2]
                    key = f"wg{(ki * (O // 512) + og) % 2}"
                    p.task("sp", lambda e, w=wbf, ki=ki, og=og, b=buf:
                           e.dma_start(out=b[:, 0:512],
                                       in_=w[og * 512:(og + 1) * 512, ki * 128:(ki + 1) * 128],
                                       transpose=True),
                           reads=[f"hbf_{nm}"], writes=[key], dma=key, label=f"wtt_{nm}")
                    if nm == "Wo":
                        p.task("dve", lambda e, b=buf: e.tensor_scalar(
                            out=b[:, 0:512], in0=b[:, 0:512], scalar1=self.onw_col[:, :],
                            scalar2=None, op0=ALU.mult),
                               reads=[key, "onw"], writes=[key], label="wo_scale")
                    src_ap = buf[:, 0:512].rearrange("p (m c) -> p m c", c=128)
                    dst_ap = wt[og * 4:(og + 1) * 4, ki, :, :].rearrange("m p c -> p m c")
                    p.task("act", lambda e, s=src_ap, d=dst_ap: e.dma_start(out=d, in_=s),
                           reads=[key, f"hbf_{nm}"], writes=[f"hwt_{nm}"], dma=f"wtst_{nm}",
                           label=f"wtst_{nm}")

        self.wm_pieces = []
        self._wm_idx = 0
        for wi, nm in enumerate(["wfc", "wproj"]):
            w = self.w_in[nm]
            R, Cc = (FF, D) if nm == "wfc" else (D, FF)
            w2 = w.ap().rearrange("(a p) b -> a p (b)", p=128)
            p.task("dve", lambda e, wi=wi: e.memset(self.wmax[:, wi:wi + 1], 0.0),
                   writes=[f"wmax{wi}"], label="wm0", always_sync=True)

            def mk_piece(a, cgs, wi=wi, w2=w2):
                def emit():
                    idx = self._wm_idx
                    self._wm_idx += 1
                    buf = self.wmtile[idx % 3]
                    key = f"wmt{idx % 3}"
                    p.task("act", lambda e, a=a, c=cgs, b=buf, w2=w2:
                           e.dma_start(out=b[:, :], in_=w2[a, :, c * 1024:(c + 1) * 1024]),
                           writes=[key], dma=key, label="wm_ld")
                    p.task("dve", lambda e, b=buf, wi=wi: e.tensor_reduce(
                        out=self.scal[:, 8 + wi:9 + wi], in_=b[:, :], axis=AX.X,
                        op=ALU.max, apply_absolute_value=True),
                           reads=[key], writes=[f"wmr{wi}"], label="wm_red")
                    p.task("dve", lambda e, wi=wi: e.tensor_tensor(
                        out=self.wmax[:, wi:wi + 1], in0=self.wmax[:, wi:wi + 1],
                        in1=self.scal[:, 8 + wi:9 + wi], op=ALU.max),
                           reads=[f"wmr{wi}", f"wmax{wi}"], writes=[f"wmax{wi}"],
                           label="wm_acc")
                return emit
            for a in range(R // 128):
                for cgs in range(Cc // 1024):
                    self.wm_pieces.append(mk_piece(a, cgs))

    def phase_a_tile(self, s, t):
        p = self.p
        tt0 = t * TT
        gt0 = s * L + tt0
        tag = f"s{s}t{t}"

        for half in range(2):
            for bi in range(2):
                blk = half * 2 + bi
                buf = self.xtm[bi]
                key = f"xtm{bi}"
                p.task("sp", lambda e, blk=blk, b=buf, s=s:
                       e.dma_start(out=b[:, :],
                                   in_=self.x_in[s, tt0 + blk * 128: tt0 + (blk + 1) * 128, :]),
                       writes=[key], dma=key, label=f"{tag}_xld{blk}")
            for f in range(16):
                for bi in range(2):
                    p.task("pe", lambda e, f=f, bi=bi: e.transpose(
                        self.tf32[:, bi * 128:(bi + 1) * 128],
                        self.xtm[bi][:, f * 128:(f + 1) * 128], self.c_ident_f32[:, :]),
                           reads=[f"xtm{bi}", "c_ident_f32"], writes=["tf32"],
                           label=f"{tag}_xtr")
                co = half * 256
                p.task("dve", lambda e, f=f, co=co: e.tensor_copy(
                    out=self.xfm[:, f * 512 + co: f * 512 + co + 256], in_=self.tf32[:, 0:256]),
                       reads=["tf32"], writes=[f"xfm{f}"], label=f"{tag}_xev")
                p.task("act", lambda e, f=f, co=co: e.activation(
                    out=self.xb[:, f * 512 + co: f * 512 + co + 256],
                    in_=self.xfm[:, f * 512 + co: f * 512 + co + 256],
                    func=ACTF.Copy),
                       reads=[f"xfm{f}"], writes=[f"xb{f}"], label=f"{tag}_xevb")

        mlist = ([("kv", "Wk", m) for m in range(16)] +
                 [("kv", "Wv", m) for m in range(16)] +
                 [("ba", None, 0), ("ba", None, 1), ("ba", None, 2)] +
                 [("q", "Wq", m) for m in range(8)] +
                 [("g", "Wg", m) for m in range(8)])
        pbanks = [self.p0, self.p1]
        rot = 0
        prot = 0
        for kind, wname, m in mlist:
            pb = pbanks[prot % 2]
            pkey = f"p{prot % 2}"
            prot += 1
            if kind == "ba":
                off = m * 8
                for k in range(NKC):
                    p.task("pe", lambda e, k=k, off=off, pb=pb: e.matmul(
                        pb[0:8, 0:512], self.wba[:, k * 24 + off: k * 24 + off + 8],
                        self.xb[:, k * 512:(k + 1) * 512],
                        start=(k == 0), stop=(k == NKC - 1)),
                           reads=[f"wba{k}", f"xb{k}"], writes=[pkey],
                           label=f"{tag}_mm_ba{m}")
                self.emit_ba_evac(m, pb, pkey, tag)
                continue
            wt = self.h_wt[wname]
            nk = NKC if wname != "Wo" else 8
            buf = self.wg_[rot % 2]
            wkey = f"wg{rot % 2}"
            rot += 1
            p.task("sp", lambda e, wt=wt, m=m, b=buf, nk=nk:
                   e.dma_start(out=b[:, 0:nk * 128].rearrange("p (k c) -> p k c", c=128),
                               in_=wt[m, :, :, :].rearrange("k p c -> p k c")),
                   reads=[f"hwt_{wname}"], writes=[wkey], dma=wkey,
                   label=f"{tag}_wld_{wname}{m}")
            for k in range(nk):
                p.task("pe", lambda e, k=k, b=buf, pb=pb: e.matmul(
                    pb[:, 0:512], b[:, k * 128:(k + 1) * 128],
                    self.xb[:, k * 512:(k + 1) * 512],
                    start=(k == 0), stop=(k == nk - 1)),
                       reads=[wkey, f"xb{k}"], writes=[pkey],
                       label=f"{tag}_mm_{wname}{m}")
            if kind == "kv":
                cv = m if wname == "Wk" else 16 + m
                dst = self.kbuf if wname == "Wk" else self.vbuf
                nn, hh = m // 8, m % 8
                self.emit_conv(pb, pkey, cv, dst, hh, nn, s, t, tag)
            elif kind == "q":
                cv = 32 + m
                self.emit_conv(pb, pkey, cv, self.qbuf, m, None, s, t, tag)
            elif kind == "g":
                p.task("act", lambda e, m=m, pb=pb: e.activation(
                    out=self.sgbuf[:, m * 512:(m + 1) * 512], in_=pb[:, 0:512],
                    func=ACTF.Silu),
                       reads=[pkey], writes=[f"sg{m}"], label=f"{tag}_sg{m}")

        self.emit_norms(tag)
        self.emit_rows(tag)
        for c in range(CPT):
            self.emit_param(c, tag)
        self.tap(f"kbuf_{tag}", self.kbuf.ap, [128, 8192], BF16,
                 reads=[f"k{h}" for h in range(8)])
        self.tap(f"vbuf_{tag}", self.vbuf.ap, [128, 8192], BF16,
                 reads=[f"v{h}" for h in range(8)])
        self.tap(f"qbuf_{tag}", self.qbuf.ap, [128, 4096], BF16,
                 reads=[f"q{h}" for h in range(8)])
        self.tap(f"sgbuf_{tag}", self.sgbuf.ap, [128, 4096], BF16,
                 reads=[f"sg{h}" for h in range(8)])
        self.tap(f"G_{tag}", self.r_["G"].ap, [8, 512], F32, reads=["row_G"])
        self.tap(f"lnrk_{tag}", self.lnrk.ap, [8, 1024], F32, reads=["row_lnrk"])
        self.tap(f"betas_{tag}", self.beta_s.ap, [8, 1024], F32, reads=["row_beta_s"])
        self.tap(f"MQ_{tag}", self.MQ_s.ap, [8, 1024], F32, reads=["row_MQ_s"])
        self.tap(f"param_{tag}", self.param.ap, [128, CPT * NPG * 8], F32,
                 reads=[f"param{c}" for c in range(CPT)])
        for c in range(CPT):
            for G2 in range(2):
                self.emit_scan_group(s, t, c, G2, tag)
        self.tap(f"ofm_{tag}", self.ofm.ap, [128, 4096], BF16,
                 reads=[f"ofm{h}" for h in range(8)])
        self.tap(f"S_{tag}", self.S.ap, [128, 1024], F32,
                 reads=[f"S{h}" for h in range(8)])
        self.emit_epilogue(s, t, gt0, tag)
        self.tap(f"x1_{tag}", self.xfm.ap, [128, 8192], F32,
                 reads=[f"xfm{f}" for f in range(16)])

    def emit_conv(self, pb, pkey, cv, dstbuf, hh, nn, s, t, tag):
        p = self.p
        rot = cv % 2
        pre, cs = self.pre[rot], self.cs[rot]
        prek, csk = f"pre{rot}", f"cs{rot}"
        histk = f"hist{cv}"
        if t == 0:
            p.task("dve", lambda e, pre=pre: e.memset(pre[:, 0:3], 0.0),
                   writes=[prek], label=f"{tag}_hrz{cv}", always_sync=True)
        else:
            p.task("dve", lambda e, pre=pre, cv=cv: e.tensor_copy(
                out=pre[:, 0:3], in_=self.hist[:, cv * 4:cv * 4 + 3]),
                   reads=[histk], writes=[prek], label=f"{tag}_hrs{cv}")
        p.task("dve", lambda e, pre=pre, pb=pb: e.tensor_copy(
            out=pre[:, 3:515], in_=pb[:, 0:512]),
               reads=[pkey], writes=[prek], label=f"{tag}_pre{cv}")
        p.task("dve", lambda e, pre=pre, cv=cv: e.tensor_copy(
            out=self.hist[:, cv * 4:cv * 4 + 3], in_=pre[:, 512:515]),
               reads=[prek], writes=[histk], label=f"{tag}_hsv{cv}")
        w = self.convw
        p.task("dve", lambda e, pre=pre, cs=cs, cv=cv: e.tensor_scalar(
            out=cs[:, :], in0=pre[:, 0:512], scalar1=w[:, cv * 4:cv * 4 + 1],
            scalar2=None, op0=ALU.mult),
               reads=[prek, f"convw{cv}"], writes=[csk], label=f"{tag}_cv0_{cv}")
        for j in range(1, 4):
            p.task("dve", lambda e, pre=pre, cs=cs, cv=cv, j=j: e.scalar_tensor_tensor(
                out=cs[:, :], in0=pre[:, j:512 + j], scalar=w[:, cv * 4 + j:cv * 4 + j + 1],
                in1=cs[:, :], op0=ALU.mult, op1=ALU.add),
                   reads=[prek, csk, f"convw{cv}"], writes=[csk], label=f"{tag}_cv{j}_{cv}")
        if nn is None:
            out_ap = dstbuf[:, hh * 512:(hh + 1) * 512]
            dkey = f"q{hh}"
        else:
            out_ap = dstbuf.ap[:, hh * 1024:(hh + 1) * 1024].rearrange(
                "p (c q) -> p c q", q=128)[:, :, nn * 64:(nn + 1) * 64]
            dkey = (f"k{hh}" if dstbuf is self.kbuf else f"v{hh}")
        p.task("act", lambda e, out_ap=out_ap, cs=cs: e.activation(
            out=out_ap, in_=cs[:, :], func=ACTF.Silu),
               reads=[csk], writes=[dkey], label=f"{tag}_silu{cv}")

    def emit_ba_evac(self, m, pb, pkey, tag):
        p = self.p
        r = self.r_
        if m in (0, 1):
            dst = self.beta_s.ap.rearrange("p (c n q) -> p c n q", n=2, q=64)[:, :, m, :]
            tmp = r["tmpB"]
            p.task("act", lambda e, pb=pb, tmp=tmp: e.activation(
                out=tmp[:, :], in_=pb[0:8, 0:512], func=ACTF.Sigmoid),
                   reads=[pkey], writes=["row_tmpB"], label=f"{tag}_bsig{m}")
            p.task("dve", lambda e, dst=dst, tmp=tmp: e.tensor_scalar(
                out=dst, in0=tmp[:, :].rearrange("p (c q) -> p c q", q=64),
                scalar1=2.0, scalar2=None, op0=ALU.mult),
                   reads=["row_tmpB"], writes=["row_beta_s"], label=f"{tag}_bx2_{m}")
        else:
            p.task("act", lambda e, pb=pb: e.activation(
                out=r["tmpB"][:, :], in_=pb[0:8, 0:512], func=ACTF.Exp,
                bias=self.dtb_col[:, :]),
                   reads=[pkey, "dtb"], writes=["row_tmpB"], label=f"{tag}_gexp")
            p.task("act", lambda e: e.activation(
                out=r["tmpC"][:, :], in_=r["tmpB"][:, :], func=ACTF.Ln, bias=1.0),
                   reads=["row_tmpB"], writes=["row_tmpC"], label=f"{tag}_gln")
            p.task("dve", lambda e: e.tensor_scalar(
                out=r["g"][:, :], in0=r["tmpC"][:, :], scalar1=self.nega_col[:, :],
                scalar2=None, op0=ALU.mult),
                   reads=["row_tmpC", "nega"], writes=["row_g"], label=f"{tag}_gmul")

    def emit_norms(self, tag):
        p = self.p
        for half in range(2):
            for h in range(8):
                sq = self.cs[h % 2]
                sqk = f"cs{h % 2}"
                p.task("act", lambda e, h=h, half=half, sq=sq: e.activation(
                    out=sq[:, :], in_=self.kbuf[:, h * 1024 + half * 512: h * 1024 + half * 512 + 512],
                    func=ACTF.Square),
                       reads=[f"k{h}"], writes=[sqk], label=f"{tag}_ksq")
                p.task("pe", lambda e, h=h, sq=sq: e.matmul(
                    self.p5[0:8, 0:512], self.c_onehot8[:, 8 * h:8 * h + 8], sq[:, :],
                    start=(h == 0), stop=(h == 7)),
                       reads=[sqk, "c_onehot8"], writes=["p5"], label=f"{tag}_knrm")
            p.task("dve", lambda e: e.tensor_scalar(
                out=self.r_["tmpC"][:, :], in0=self.p5[0:8, 0:512], scalar1=1e-24,
                scalar2=None, op0=ALU.max),
                   reads=["p5"], writes=["row_tmpC"], label=f"{tag}_kcl")
            p.task("act", lambda e: e.activation(
                out=self.r_["tmpC"][:, :], in_=self.r_["tmpC"][:, :], func=ACTF.Ln),
                   reads=["row_tmpC"], writes=["row_tmpC"], label=f"{tag}_kln")
            p.task("dve", lambda e, half=half: e.tensor_scalar(
                out=self.lnrk[:, half * 512:(half + 1) * 512], in0=self.r_["tmpC"][:, :],
                scalar1=-0.5, scalar2=None, op0=ALU.mult),
                   reads=["row_tmpC"], writes=["row_lnrk"], label=f"{tag}_kl2")
        for h in range(8):
            sq = self.cs[h % 2]
            sqk = f"cs{h % 2}"
            p.task("act", lambda e, h=h, sq=sq: e.activation(
                out=sq[:, :], in_=self.qbuf[:, h * 512:(h + 1) * 512], func=ACTF.Square),
                   reads=[f"q{h}"], writes=[sqk], label=f"{tag}_qsq")
            p.task("pe", lambda e, h=h, sq=sq: e.matmul(
                self.p5[0:8, 0:512], self.c_onehot8[:, 8 * h:8 * h + 8], sq[:, :],
                start=(h == 0), stop=(h == 7)),
                   reads=[sqk, "c_onehot8"], writes=["p5"], label=f"{tag}_qnrm")
        p.task("dve", lambda e: e.tensor_scalar(
            out=self.r_["tmpC"][:, :], in0=self.p5[0:8, 0:512], scalar1=1e-24,
            scalar2=None, op0=ALU.max),
               reads=["p5"], writes=["row_tmpC"], label=f"{tag}_qcl")
        p.task("act", lambda e: e.activation(
            out=self.r_["tmpC"][:, :], in_=self.r_["tmpC"][:, :], func=ACTF.Ln),
               reads=["row_tmpC"], writes=["row_tmpC"], label=f"{tag}_qln")
        p.task("dve", lambda e: e.tensor_scalar(
            out=self.r_["tmpB"][:, :], in0=self.r_["tmpC"][:, :],
            scalar1=-0.5, scalar2=float(-0.5 * np.log(128.0)), op0=ALU.mult, op1=ALU.add),
               reads=["row_tmpC"], writes=["row_tmpB"], label=f"{tag}_ql2")

    def emit_rows(self, tag):
        p = self.p
        r = self.r_

        def ev(ap1024, par):
            return ap1024.rearrange("p (c n q) -> p c n q", n=2, q=64)[:, :, par, :]

        def pv(ap512):
            return ap512.rearrange("p (c q) -> p c q", q=64)

        for c in range(CPT):
            p.task("dve", lambda e, c=c: e.tensor_tensor_scan(
                out=r["G"][:, c * 64:(c + 1) * 64], data0=r["g"][:, c * 64:(c + 1) * 64],
                data1=r["g"][:, c * 64:(c + 1) * 64], initial=0.0,
                op0=ALU.add, op1=ALU.bypass),
                   reads=["row_g"], writes=["row_G"], label=f"{tag}_gcum{c}")
        for c in range(CPT):
            gsrc = r["G"][:, c * 64 + 63:c * 64 + 64]
            for par in range(2):
                dsl = self.GEND_s[:, c * 128 + par * 64: c * 128 + (par + 1) * 64]
                p.task("dve", lambda e, gsrc=gsrc, dsl=dsl: e.tensor_copy(
                    out=dsl, in_=bass.broadcast_tensor_aps(gsrc, dsl)[0]),
                       reads=["row_G"], writes=["row_GEND_s"], label=f"{tag}_ge{c}_{par}")
        for par in range(2):
            p.task("dve", lambda e, par=par: e.tensor_tensor(
                out=ev(self.MQ_s.ap, par), in0=pv(r["G"].ap), in1=pv(r["tmpB"].ap),
                op=ALU.add),
                   reads=["row_G", "row_tmpB"], writes=["row_MQ_s"], label=f"{tag}_mq{par}")
        for par in range(2):
            p.task("dve", lambda e, par=par: e.tensor_tensor(
                out=ev(self.PWs.ap, par), in0=ev(self.lnrk.ap, par),
                in1=pv(r["G"].ap), op=ALU.subtract),
                   reads=["row_lnrk", "row_G"], writes=["row_PWs"], label=f"{tag}_pw{par}")
        for par in range(2):
            bsl = ev(self.beta_s.ap, par)
            p.task("act", lambda e, par=par, bsl=bsl: e.activation(
                out=pv(r["tmpC"].ap), in_=bsl, func=ACTF.Ln),
                   reads=["row_beta_s"], writes=["row_tmpC"], label=f"{tag}_lnb{par}")
            p.task("dve", lambda e, par=par: e.tensor_tensor(
                out=pv(r["tmpC"].ap), in0=pv(r["tmpC"].ap), in1=ev(self.lnrk.ap, par),
                op=ALU.add),
                   reads=["row_tmpC", "row_lnrk"], writes=["row_tmpC"], label=f"{tag}_pb{par}")
            p.task("dve", lambda e, par=par: e.tensor_tensor(
                out=ev(self.PA_s.ap, par), in0=pv(r["G"].ap), in1=pv(r["tmpC"].ap),
                op=ALU.add),
                   reads=["row_tmpC", "row_G"], writes=["row_PA_s"], label=f"{tag}_pa{par}")

    def emit_param(self, c, tag):
        p = self.p
        groups = [(PG_BETA, self.beta_s, "row_beta_s"), (PG_PA, self.PA_s, "row_PA_s"),
                  (PG_PW, self.PWs, "row_PWs"), (PG_MQ, self.MQ_s, "row_MQ_s"),
                  (PG_GEND, self.GEND_s, "row_GEND_s")]
        for g, row, rk in groups:
            src = row[:, c * 128:(c + 1) * 128]
            p.task("pe", lambda e, g=g, src=src: e.transpose(
                self.tf32[:, g * 8:(g + 1) * 8], src, self.c_ident_f32[0:8, 0:8]),
                   reads=[rk, "c_ident_f32"], writes=["tf32"], label=f"{tag}_ptr{c}_{g}")
        pcol = lambda g: self.param[:, c * NPG * 8 + g * 8: c * NPG * 8 + (g + 1) * 8]
        p.task("dve", lambda e, c=c: e.tensor_copy(
            out=self.param[:, c * NPG * 8: c * NPG * 8 + 40], in_=self.tf32[:, 0:40]),
               reads=["tf32"], writes=[f"param{c}"], label=f"{tag}_pev{c}")
        p.task("act", lambda e: e.activation(out=pcol(PG_DECAY), in_=pcol(PG_GEND),
                                             func=ACTF.Exp),
               reads=[f"param{c}"], writes=[f"param{c}"], label=f"{tag}_pd1{c}")
        p.task("dve", lambda e: e.tensor_tensor(out=pcol(PG_W), in0=pcol(PG_PW),
                                                in1=pcol(PG_GEND), op=ALU.add),
               reads=[f"param{c}"], writes=[f"param{c}"], label=f"{tag}_pd2{c}")
        p.task("act", lambda e: e.activation(out=pcol(PG_W), in_=pcol(PG_W),
                                             func=ACTF.Exp),
               reads=[f"param{c}"], writes=[f"param{c}"], label=f"{tag}_pd3{c}")
        p.task("act", lambda e: e.activation(out=pcol(PG_GRQ), in_=pcol(PG_MQ),
                                             func=ACTF.Exp),
               reads=[f"param{c}"], writes=[f"param{c}"], label=f"{tag}_pd4{c}")
        p.task("act", lambda e: e.activation(out=pcol(PG_NBGR), in_=pcol(PG_PA),
                                             func=ACTF.Exp),
               reads=[f"param{c}"], writes=[f"param{c}"], label=f"{tag}_pd5{c}")
        p.task("dve", lambda e: e.tensor_scalar(
            out=pcol(PG_NBGR), in0=pcol(PG_NBGR), scalar1=-1.0, scalar2=None,
            op0=ALU.mult),
               reads=[f"param{c}"], writes=[f"param{c}"], label=f"{tag}_pd6{c}")

    def emit_scan_group(self, s, t, c, G2, tag):
        p = self.p
        hs = [4 * G2 + i for i in range(4)]
        pc = lambda g, h: self.param[:, c * NPG * 8 + g * 8 + h: c * NPG * 8 + g * 8 + h + 1]
        pc64 = lambda g, h: self.param[0:64, c * NPG * 8 + g * 8 + h: c * NPG * 8 + g * 8 + h + 1]
        kslice = lambda h: self.kbuf[:, h * 1024 + c * 128: h * 1024 + (c + 1) * 128]
        vslice = lambda h: self.vbuf[:, h * 1024 + c * 128: h * 1024 + (c + 1) * 128]
        qslice = lambda h: self.qbuf[:, h * 512 + c * 64: h * 512 + (c + 1) * 64]
        pk = f"param{c}"
        g = f"{tag}c{c}g{G2}"

        for si, h in enumerate(hs):
            p.task("pe", lambda e, si=si, h=h: e.transpose(
                self.tf32[0:1, si * 128:(si + 1) * 128], pc(PG_PW, h),
                self.c_ident_f32[:, :]),
                   reads=[pk, "c_ident_f32"], writes=["tf32"], label=g + "_rtP")
        for si, h in enumerate(hs):
            p.task("pe", lambda e, si=si, h=h: e.transpose(
                self.p1[0:1, 256 + si * 64: 256 + (si + 1) * 64], pc64(PG_MQ, h),
                self.c_ident_f32[0:64, 0:64]),
                   reads=[pk, "c_ident_f32"], writes=["p1"], label=g + "_rtM")
        p.task("dve", lambda e: e.tensor_copy(out=self.rbank[0:1, 0:512],
                                              in_=self.tf32[0:1, 0:512]),
               reads=["tf32"], writes=["rbank_pw"], label=g + "_rbp")
        p.task("dve", lambda e: e.tensor_copy(out=self.rbank[0:1, 512:768],
                                              in_=self.p1[0:1, 256:512]),
               reads=["p1"], writes=["rbank_mq"], label=g + "_rbm")
        for si, h in enumerate(hs):
            p.task("pe", lambda e, si=si, h=h: e.matmul(
                self.tf32[:, si * 128:(si + 1) * 128], self.c_ones_row_f32[:, :],
                self.rbank[0:1, si * 128:(si + 1) * 128], start=True, stop=True,
                skip_group_check=True),
                   reads=["c_ones_row_f32", "rbank_pw"], writes=["tf32"], label=g + "_oA")
        for si, h in enumerate(hs):
            p.task("dve", lambda e, si=si, h=h: e.scalar_tensor_tensor(
                out=self.eA[:, si * 128:(si + 1) * 128], in0=self.tf32[:, si * 128:(si + 1) * 128],
                scalar=pc(PG_PA, h), in1=self.c_maskA[:, :], op0=ALU.add, op1=ALU.add),
                   reads=["tf32", pk, "c_maskA"], writes=["eA"], label=g + "_eA")
        p.task("act", lambda e: e.activation(out=self.dA[:, :], in_=self.eA[:, :],
                                             func=ACTF.Exp),
               reads=["eA"], writes=["Ngen0"], label=g + "_expA")
        for si, h in enumerate(hs):
            p.task("pe", lambda e, si=si, h=h: e.transpose(
                self.tbf[:, si * 128:(si + 1) * 128], kslice(h), self.c_ident_bf[:, :]),
                   reads=[f"k{h}", "c_ident_bf"], writes=["tbf"], label=g + "_trK")
        for si, h in enumerate(hs):
            p.task("pe", lambda e, si=si, h=h: e.transpose(
                self.tbf[:, 512 + si * 128: 512 + (si + 1) * 128], vslice(h),
                self.c_ident_bf[:, :]),
                   reads=[f"v{h}", "c_ident_bf"], writes=["tbf"], label=g + "_trV")
        for si, h in enumerate(hs):
            p.task("dve", lambda e, si=si, h=h: e.tensor_scalar(
                out=self.kw[:, si * 128:(si + 1) * 128], in0=self.tbf[:, si * 128:(si + 1) * 128],
                scalar1=pc(PG_W, h), scalar2=None, op0=ALU.mult),
                   reads=["tbf", pk], writes=["kw"], label=g + "_kw")
        for si, h in enumerate(hs):
            p.task("pe", lambda e, si=si, h=h: e.matmul(
                self.p2[:, si * 128:(si + 1) * 128], kslice(h), kslice(h),
                start=True, stop=True, skip_group_check=True),
                   reads=[f"k{h}"], writes=["p2"], label=g + "_KK")
        p.task("dve", lambda e: e.scalar_tensor_tensor(
            out=self.Ngen[0][:, :], in0=self.p2[:, :], scalar=-1.0, in1=self.dA[:, :],
            op0=ALU.mult, op1=ALU.mult),
               reads=["p2", "Ngen0"], writes=["Ngen0"], label=g + "_mkN")
        if g.startswith("s0t0c0g0"):
            self.tap("scan_N1", self.Ngen[0].ap, [128, 512], BF16, reads=["Ngen0"])
            self.tap("scan_dA", self.dA.ap, [128, 512], BF16, reads=["Ngen0"])
        for si in range(4):
            p.task("pe", lambda e, si=si: e.transpose(
                self.tbf[:, si * 128:(si + 1) * 128], self.Ngen[0][:, si * 128:(si + 1) * 128],
                self.c_ident_bf[:, :]),
                   reads=["Ngen0", "c_ident_bf"], writes=["tbf"], label=g + "_trN")
        p.task("dve", lambda e: e.tensor_copy(out=self.NTgen[0][:, :], in_=self.tbf[:, 0:512]),
               reads=["tbf"], writes=["NTgen0"], label=g + "_evNT")
        if g.startswith("s0t0c0g0"):
            self.tap("scan_NT1", self.NTgen[0].ap, [128, 512], BF16, reads=["NTgen0"])
        for si, h in enumerate(hs):
            p.task("pe", lambda e, si=si, h=h: e.matmul(
                self.p0[:, si * 128:(si + 1) * 128], kslice(h),
                self.Sbf[:, h * 128:(h + 1) * 128], start=True, stop=True,
                skip_group_check=True),
                   reads=[f"k{h}", f"Sbf{h}"], writes=["p0"], label=g + "_KS")
        for si, h in enumerate(hs):
            p.task("dve", lambda e, si=si, h=h: e.tensor_scalar(
                out=self.t1s[:, si * 128:(si + 1) * 128], in0=self.p0[:, si * 128:(si + 1) * 128],
                scalar1=pc(PG_NBGR, h), scalar2=None, op0=ALU.mult),
                   reads=["p0", pk], writes=["t1s"], label=g + "_t1")
        for si, h in enumerate(hs):
            p.task("dve", lambda e, si=si, h=h: e.scalar_tensor_tensor(
                out=self.U0[:, si * 128:(si + 1) * 128],
                in0=self.tbf[:, 512 + si * 128: 512 + (si + 1) * 128],
                scalar=pc(PG_BETA, h), in1=self.t1s[:, si * 128:(si + 1) * 128],
                op0=ALU.mult, op1=ALU.add),
                   reads=["tbf", pk, "t1s"], writes=["U0"], label=g + "_R")
        for j in range(7):
            if j >= 1:
                src = (j - 1) % 2
                dst = j % 2
                for si in range(4):
                    p.task("pe", lambda e, si=si, src=src: e.matmul(
                        self.p3[:, si * 128:(si + 1) * 128],
                        self.NTgen[src][:, si * 128:(si + 1) * 128],
                        self.Ngen[src][:, si * 128:(si + 1) * 128],
                        start=True, stop=True, skip_group_check=True),
                           reads=[f"NTgen{src}", f"Ngen{src}"], writes=["p3"],
                           label=g + f"_sqN{j}")
                for si in range(4):
                    p.task("pe", lambda e, si=si, src=src: e.matmul(
                        self.p4[:, si * 128:(si + 1) * 128],
                        self.Ngen[src][:, si * 128:(si + 1) * 128],
                        self.NTgen[src][:, si * 128:(si + 1) * 128],
                        start=True, stop=True, skip_group_check=True),
                           reads=[f"NTgen{src}", f"Ngen{src}"], writes=["p4"],
                           label=g + f"_sqNT{j}")
                if j < 6:
                    p.task("act", lambda e, dst=dst: e.activation(
                        out=self.Ngen[dst][:, :], in_=self.p3[:, :], func=ACTF.Copy),
                           reads=["p3"], writes=[f"Ngen{dst}"], label=g + f"_evN{j}")
                p.task("dve", lambda e, dst=dst: e.tensor_copy(
                    out=self.NTgen[dst][:, :], in_=self.p4[:, :]),
                       reads=["p4"], writes=[f"NTgen{dst}"], label=g + f"_evNT{j}")
            ntb = f"NTgen{j % 2}"
            u_rhs = self.U0 if j == 0 else self.Ucur[(j - 1) % 2]
            u_rhs_k = "U0" if j == 0 else f"Ucur{(j - 1) % 2}"
            for si in range(4):
                p.task("pe", lambda e, si=si, j=j, u=u_rhs: e.matmul(
                    self.p5[:, si * 128:(si + 1) * 128],
                    self.NTgen[j % 2][:, si * 128:(si + 1) * 128],
                    u[:, si * 128:(si + 1) * 128],
                    start=(j == 0 and si == 0), stop=(j == 6 and si == 3),
                    skip_group_check=True),
                       reads=[ntb, u_rhs_k], writes=["p5"], label=g + f"_ap{j}")
            p.task("dve", lambda e, j=j: e.tensor_tensor(
                out=self.Ucur[j % 2][:, :], in0=self.p5[:, :], in1=self.U0[:, :],
                op=ALU.add),
                   reads=["p5", "U0"], writes=[f"Ucur{j % 2}"], label=g + f"_U{j}")
        if g.startswith("s0t0c0g0"):
            self.tap("scan_U0", self.U0.ap, [128, 512], BF16, reads=["U0"])
            self.tap("scan_U7", self.Ucur[0].ap, [128, 512], BF16, reads=["Ucur0"])
            self.tap("scan_kw", self.kw.ap, [128, 512], BF16, reads=["kw"])
        u7, u7k = self.Ucur[0], "Ucur0"
        for si, h in enumerate(hs):
            p.task("pe", lambda e, si=si, h=h: e.matmul(
                self.p1[:, si * 64:(si + 1) * 64], kslice(h), qslice(h),
                start=True, stop=True, skip_group_check=True),
                   reads=[f"k{h}", f"q{h}"], writes=["p1"], label=g + "_KQ")
        for si, h in enumerate(hs):
            p.task("pe", lambda e, si=si, h=h: e.matmul(
                self.p1[:, 256 + si * 64: 256 + (si + 1) * 64], self.c_ones_row_f32[:, :],
                self.rbank[0:1, 512 + si * 64: 512 + (si + 1) * 64], start=True, stop=True,
                skip_group_check=True),
                   reads=["c_ones_row_f32", "rbank_mq"], writes=["p1"], label=g + "_oW")
        for si, h in enumerate(hs):
            p.task("dve", lambda e, si=si, h=h: e.scalar_tensor_tensor(
                out=self.eW[:, si * 64:(si + 1) * 64],
                in0=self.p1[:, 256 + si * 64: 256 + (si + 1) * 64],
                scalar=pc(PG_PW, h), in1=self.c_maskWT[:, :], op0=ALU.add, op1=ALU.add),
                   reads=["p1", pk, "c_maskWT"], writes=["eA"], label=g + "_eW")
        p.task("act", lambda e: e.activation(out=self.dWT[:, :], in_=self.eW[:, :],
                                             func=ACTF.Exp),
               reads=["eA"], writes=["t1s"], label=g + "_expW")
        p.task("dve", lambda e: e.tensor_tensor(
            out=self.watt[:, :], in0=self.p1[:, 0:256], in1=self.dWT[:, :], op=ALU.mult),
               reads=["p1", "t1s"], writes=["U0"], label=g + "_mkW")
        if g.startswith("s0t0c0g0"):
            self.tap("scan_watt", self.watt.ap, [128, 256], BF16, reads=["U0"])
        for si, h in enumerate(hs):
            p.task("pe", lambda e, si=si, h=h: e.matmul(
                self.p0[0:64, si * 128:(si + 1) * 128], self.watt[:, si * 64:(si + 1) * 64],
                u7[:, si * 128:(si + 1) * 128], start=True, stop=True,
                skip_group_check=True),
                   reads=["U0", u7k], writes=["p0"], label=g + "_Oi")
        for si, h in enumerate(hs):
            p.task("pe", lambda e, si=si, h=h: e.matmul(
                self.p4[0:64, si * 128:(si + 1) * 128], qslice(h),
                self.Sbf[:, h * 128:(h + 1) * 128], start=True, stop=True,
                skip_group_check=True),
                   reads=[f"q{h}", f"Sbf{h}"], writes=["p4"], label=g + "_Oe")
        for si, h in enumerate(hs):
            p.task("dve", lambda e, si=si, h=h: e.tensor_scalar(
                out=self.oraw[0:64, si * 128:(si + 1) * 128],
                in0=self.p4[0:64, si * 128:(si + 1) * 128],
                scalar1=pc64(PG_GRQ, h), scalar2=None, op0=ALU.mult),
                   reads=["p4", pk], writes=["eA"], label=g + "_t2")
        for si in range(4):
            p.task("dve", lambda e, si=si: e.tensor_tensor(
                out=self.oraw[0:64, si * 128:(si + 1) * 128],
                in0=self.oraw[0:64, si * 128:(si + 1) * 128],
                in1=self.p0[0:64, si * 128:(si + 1) * 128], op=ALU.add),
                   reads=["eA", "p0"], writes=["eA"], label=g + "_oadd")
        if g.startswith("s0t0c0g0"):
            self.tap("scan_oraw", self.oraw.ap, [64, 512], F32, reads=["eA"])
        p.task("act", lambda e: e.activation(out=self.osq[0:64, :], in_=self.oraw[0:64, :],
                                             func=ACTF.Square),
               reads=["eA"], writes=["t1s"], label=g + "_osq")
        for si in range(4):
            p.task("dve", lambda e, si=si: e.reduce_sum(
                out=self.rmsacc[0:64, si:si + 1], in_=self.osq[0:64, si * 128:(si + 1) * 128],
                axis=AX.X),
                   reads=["t1s"], writes=["rms0"], label=g + "_ssum")
        p.task("dve", lambda e: e.tensor_scalar(
            out=self.rmsacc[0:64, 4:8], in0=self.rmsacc[0:64, 0:4],
            scalar1=1.0 / 128.0, scalar2=EPS_ORMS, op0=ALU.mult, op1=ALU.add),
               reads=["rms0"], writes=["rms1"], label=g + "_rmu")
        p.task("dve", lambda e: e.reciprocal(out=self.rmsacc[0:64, 8:12],
                                             in_=self.rmsacc[0:64, 4:8]),
               reads=["rms1"], writes=["rms2"], label=g + "_rrc")
        p.task("act", lambda e: e.activation(out=self.rmsacc[0:64, 12:16],
                                             in_=self.rmsacc[0:64, 8:12], func=ACTF.Sqrt),
               reads=["rms2"], writes=["rms3"], label=g + "_rsq")
        for si, h in enumerate(hs):
            p.task("dve", lambda e, si=si: e.tensor_scalar(
                out=self.on_[0:64, si * 128:(si + 1) * 128],
                in0=self.oraw[0:64, si * 128:(si + 1) * 128],
                scalar1=self.rmsacc[0:64, 12 + si:13 + si], scalar2=None, op0=ALU.mult),
                   reads=["rms3", "eA"], writes=["Ngen1"], label=g + "_on")
        for si in range(4):
            p.task("pe", lambda e, si=si: e.transpose(
                self.tbf[:, si * 128: si * 128 + 64], self.on_[0:64, si * 128:(si + 1) * 128],
                self.c_ident_bf[0:64, 0:64]),
                   reads=["Ngen1", "c_ident_bf"], writes=["tbf"], label=g + "_trO")
        for si, h in enumerate(hs):
            p.task("dve", lambda e, si=si, h=h: e.tensor_tensor(
                out=self.ofm[:, h * 512 + c * 64: h * 512 + (c + 1) * 64],
                in0=self.tbf[:, si * 128: si * 128 + 64],
                in1=self.sgbuf[:, h * 512 + c * 64: h * 512 + (c + 1) * 64], op=ALU.mult),
                   reads=["tbf", f"sg{h}"], writes=[f"ofm{h}"], label=g + "_gate")
        for si, h in enumerate(hs):
            p.task("pe", lambda e, si=si: e.matmul(
                self.p3[:, si * 128:(si + 1) * 128], self.kw[:, si * 128:(si + 1) * 128],
                u7[:, si * 128:(si + 1) * 128], start=True, stop=True,
                skip_group_check=True),
                   reads=["kw", u7k], writes=["p3"], label=g + "_dS")
        for si, h in enumerate(hs):
            p.task("dve", lambda e, si=si, h=h: e.scalar_tensor_tensor(
                out=self.S[:, h * 128:(h + 1) * 128], in0=self.S[:, h * 128:(h + 1) * 128],
                scalar=pc(PG_DECAY, h), in1=self.p3[:, si * 128:(si + 1) * 128],
                op0=ALU.mult, op1=ALU.add),
                   reads=["p3", pk, f"S{h}"], writes=[f"S{h}"], label=g + "_Snew")
        for si, h in enumerate(hs):
            p.task("dve", lambda e, h=h: e.tensor_copy(
                out=self.Sbf[:, h * 128:(h + 1) * 128], in_=self.S[:, h * 128:(h + 1) * 128]),
                   reads=[f"S{h}"], writes=[f"Sbf{h}"], label=g + "_Sbf")
        if g.startswith("s0t0c0g0"):
            self.tap("scan_S1", self.S.ap, [128, 1024], F32,
                     reads=[f"S{h}" for h in hs])

    def emit_epilogue(self, s, t, gt0, tag):
        p = self.p
        pbanks = [self.p0, self.p1]
        for m in range(16):
            pb = pbanks[m % 2]
            pkey = f"p{m % 2}"
            buf = self.wg_[m % 2]
            wkey = f"wg{m % 2}"
            wt = self.h_wt["Wo"]
            p.task("sp", lambda e, m=m, b=buf, wt=wt:
                   e.dma_start(out=b[:, 0:8 * 128].rearrange("p (k c) -> p k c", c=128),
                               in_=wt[m, :, :, :].rearrange("k p c -> p k c")),
                   reads=["hwt_Wo"], writes=[wkey], dma=wkey, label=f"{tag}_wldWo{m}")
            for k in range(8):
                p.task("pe", lambda e, k=k, b=buf, pb=pb: e.matmul(
                    pb[:, 0:512], b[:, k * 128:(k + 1) * 128],
                    self.ofm[:, k * 512:(k + 1) * 512],
                    start=(k == 0), stop=(k == 7)),
                       reads=[wkey, f"ofm{k}"], writes=[pkey], label=f"{tag}_mmWo{m}")
            p.task("dve", lambda e, m=m, pb=pb: e.tensor_tensor(
                out=self.xfm[:, m * 512:(m + 1) * 512], in0=self.xfm[:, m * 512:(m + 1) * 512],
                in1=pb[:, 0:512], op=ALU.add),
                   reads=[pkey, f"xfm{m}"], writes=[f"xfm{m}"], label=f"{tag}_res{m}")
        self.tap(f"resid_{tag}", self.xfm.ap, [128, 8192], F32,
                 reads=[f"xfm{f}" for f in range(16)])
        for f in range(16):
            sq = self.pre[f % 2][0:128, 0:512]
            yb = self.cs[f % 2]
            p.task("act", lambda e, f=f, sq=sq: e.activation(
                out=sq[:, :], in_=self.xfm[:, f * 512:(f + 1) * 512], func=ACTF.Square),
                   reads=[f"xfm{f}"], writes=[f"pre{f % 2}"], label=f"{tag}_ysq{f}")
            p.task("dve", lambda e, f=f, yb=yb: e.tensor_copy(
                out=yb[:, :], in_=self.xfm[:, f * 512:(f + 1) * 512]),
                   reads=[f"xfm{f}"], writes=[f"cs{f % 2}"], label=f"{tag}_ybf{f}")
            p.task("pe", lambda e, f=f, yb=yb: e.matmul(
                self.p5[0:1, 0:512], self.c_ones_col_bf[:, :], yb[:, :],
                start=(f == 0), stop=(f == 15), skip_group_check=True),
                   reads=[f"cs{f % 2}", "c_ones_col_bf"], writes=["p5"],
                   label=f"{tag}_stsum{f}")
            p.task("pe", lambda e, f=f, sq=sq: e.matmul(
                self.p3[0:1, 0:512], self.c_ones_col_bf[:, :], sq[:, :],
                start=(f == 0), stop=(f == 15), skip_group_check=True),
                   reads=[f"pre{f % 2}", "c_ones_col_bf"], writes=["p3"],
                   label=f"{tag}_stsq{f}")
        mu = self.r_["tmpB"][0:1, 0:512]
        m2 = self.r_["tmpC"][0:1, 0:512]
        var = self.r_["g"][0:1, 0:512]
        inv = self.r_["G"][0:1, 0:512]
        bseg = self.lnrk[0:1, 0:512]
        p.task("dve", lambda e: e.tensor_scalar(
            out=mu, in0=self.p5[0:1, 0:512], scalar1=1.0 / D, scalar2=None, op0=ALU.mult),
               reads=["p5"], writes=["row_tmpB"], label=f"{tag}_mu")
        p.task("dve", lambda e: e.tensor_scalar(
            out=m2, in0=self.p3[0:1, 0:512], scalar1=1.0 / D, scalar2=None, op0=ALU.mult),
               reads=["p3"], writes=["row_tmpC"], label=f"{tag}_m2")
        p.task("dve", lambda e: e.tensor_tensor(out=var, in0=mu, in1=mu, op=ALU.mult),
               reads=["row_tmpB"], writes=["row_g"], label=f"{tag}_musq")
        p.task("dve", lambda e: e.tensor_tensor(out=var, in0=m2, in1=var, op=ALU.subtract),
               reads=["row_tmpC", "row_g"], writes=["row_g"], label=f"{tag}_var")
        p.task("dve", lambda e: e.tensor_scalar(
            out=var, in0=var, scalar1=EPS_LN, scalar2=None, op0=ALU.add),
               reads=["row_g"], writes=["row_g"], label=f"{tag}_veps")
        p.task("dve", lambda e: e.reciprocal(out=inv, in_=var),
               reads=["row_g"], writes=["row_G"], label=f"{tag}_vrc")
        p.task("act", lambda e: e.activation(out=inv, in_=inv, func=ACTF.Sqrt),
               reads=["row_G"], writes=["row_G"], label=f"{tag}_inv")
        p.task("dve", lambda e: e.scalar_tensor_tensor(
            out=bseg, in0=inv, scalar=-1.0, in1=mu, op0=ALU.mult, op1=ALU.mult),
               reads=["row_G", "row_tmpB"], writes=["row_lnrk"], label=f"{tag}_bb")
        p.task("pe", lambda e: e.matmul(self.p3[:, 0:512], self.c_ones_row_f32[:, :],
                                        inv, start=True, stop=True,
                                        skip_group_check=True),
               reads=["row_G", "c_ones_row_f32"], writes=["p3"], label=f"{tag}_bca")
        p.task("pe", lambda e: e.matmul(self.p4[:, 0:512], self.c_ones_row_f32[:, :],
                                        bseg, start=True, stop=True,
                                        skip_group_check=True),
               reads=["row_lnrk", "c_ones_row_f32"], writes=["p4"], label=f"{tag}_bcb")
        for f in range(16):
            p.task("dve", lambda e, f=f: e.tensor_tensor(
                out=self.xfm[:, f * 512:(f + 1) * 512], in0=self.xfm[:, f * 512:(f + 1) * 512],
                in1=self.p3[:, 0:512], op=ALU.mult),
                   reads=["p3", f"xfm{f}"], writes=[f"xfm{f}"], label=f"{tag}_x1a{f}")
            p.task("dve", lambda e, f=f: e.tensor_tensor(
                out=self.xfm[:, f * 512:(f + 1) * 512], in0=self.xfm[:, f * 512:(f + 1) * 512],
                in1=self.p4[:, 0:512], op=ALU.add),
                   reads=["p4", f"xfm{f}"], writes=[f"xfm{f}"], label=f"{tag}_x1b{f}")
            p.task("dve", lambda e, f=f: e.tensor_reduce(
                out=self.amax[:, 1:2], in_=self.xfm[:, f * 512:(f + 1) * 512],
                axis=AX.X, op=ALU.max, apply_absolute_value=True),
                   reads=[f"xfm{f}"], writes=["amax1"], label=f"{tag}_am{f}")
            p.task("dve", lambda e: e.tensor_tensor(
                out=self.amax[:, 0:1], in0=self.amax[:, 0:1], in1=self.amax[:, 1:2],
                op=ALU.max),
                   reads=["amax1", "amax0"], writes=["amax0"], label=f"{tag}_ama{f}")
            p.task("sp", lambda e, f=f, gt0=gt0: e.dma_start(
                out=self.h_x1[f * 128:(f + 1) * 128, gt0:gt0 + TT],
                in_=self.xfm[:, f * 512:(f + 1) * 512]),
                   reads=[f"xfm{f}"], writes=[f"hx1_{f}_{gt0 // TT}"], dma=f"x1st{f}",
                   label=f"{tag}_x1st{f}")

    def phase_a_seq(self, s, weave=()):
        p = self.p
        weave = list(weave)
        nw = len(weave)
        ntile = self.dbg_ntile
        if s == 0:
            p.task("dve", lambda e: e.memset(self.amax[:, 0:1], 0.0), writes=["amax0"],
                   always_sync=True)
        for h in range(8):
            p.task("dve", lambda e, h=h: e.memset(self.S[:, h * 128:(h + 1) * 128], 0.0),
                   writes=[f"S{h}"], label=f"s{s}_S0_{h}", always_sync=True)
            p.task("dve", lambda e, h=h: e.memset(self.Sbf[:, h * 128:(h + 1) * 128], 0.0),
                   writes=[f"Sbf{h}"], label=f"s{s}_Sbf0_{h}", always_sync=True)
        for t in range(self.dbg_ntile):
            self.phase_a_tile(s, t)
            lo = nw * t // max(ntile, 1)
            hi = nw * (t + 1) // max(ntile, 1)
            for w in weave[lo:hi]:
                w()

    def emit_scalar_chain(self, name, emitlist, reads, writes, eng="dve"):
        for i, (fn, r, w) in enumerate(emitlist):
            self.p.task(eng, fn, reads=r, writes=w, label=f"{name}{i}")

    def emit_collective(self, name, src_col, dst_col, ccd_base):
        p = self.p
        nco = self.nc
        if getattr(self, "cc_local", False):
            return p.task("dve", lambda e: e.tensor_copy(
                out=self.scal[0:1, dst_col:dst_col + 1],
                in_=self.scal[0:1, src_col:src_col + 1]),
                          reads=[f"scol{src_col}"], writes=[f"scol{dst_col}"],
                          label=name + "_local")

        def emit(e):
            e.dma_start(out=self.h_cc_in[0:1, 0:1], in_=self.scal[0:1, src_col:src_col + 1]
                        ).then_inc(self.ccd, 16)
            e.wait_ge(self.ccd, self.ccd_count[0] + 16)
            self.ccd_count[0] += 16
            e.collective_compute(
                "AllReduce", ALU.max, replica_groups=[list(range(NCORES))],
                ins=[self.h_cc_in[0:1, 0:1]], outs=[self.h_cc_out[0:1, 0:1]]
                ).then_inc(self.ccc, 1)
            self.ccc_count[0] += 1
            e.wait_ge(self.ccc, self.ccc_count[0])
            e.dma_start(out=self.scal[0:1, dst_col:dst_col + 1], in_=self.h_cc_out[0:1, 0:1]
                        ).then_inc(self.ccd, 16)
            self.ccd_count[0] += 16
            return e.wait_ge(self.ccd, self.ccd_count[0])

        return p.task("pool", emit, reads=[f"scol{src_col}"], writes=[f"scol{dst_col}"],
                      label=name)

    def scol(self, i, width=1):
        return self.scal[0:1, i:i + width]

    def scalar_finalize_max(self, acc_col_ap, acc_key, out_col, tag):
        p = self.p
        p.task("pe", lambda e: e.transpose(self.tf32[0:1, 0:128], acc_col_ap,
                                           self.c_ident_f32[:, :]),
               reads=[acc_key, "c_ident_f32"], writes=["tf32"], label=tag + "_tr")
        p.task("dve", lambda e: e.tensor_reduce(
            out=self.scal[0:1, out_col:out_col + 1], in_=self.tf32[0:1, 0:128],
            axis=AX.X, op=ALU.max),
               reads=["tf32"], writes=[f"scol{out_col}"], label=tag + "_red")

    def bcast_col(self, src_col, dst_view, dst_key, tag, parts=128, deps=()):
        p = self.p
        p.task("pe", lambda e: e.matmul(self.tf32[:, 0:1], self.c_ones_row_f32[:, :],
                                        self.scal[0:1, src_col:src_col + 1],
                                        start=True, stop=True, skip_group_check=True),
               reads=[f"scol{src_col}", "c_ones_row_f32"], writes=["tf32"],
               label=tag + "_bc")
        p.task("dve", lambda e: e.tensor_copy(out=dst_view, in_=self.tf32[:, 0:1]),
               reads=["tf32"], writes=[dst_key], deps=deps, label=tag + "_bcv")

    def emit_wscale(self, wi, out_col, tag):
        p = self.p
        self.scalar_finalize_max(self.wmax[:, wi:wi + 1], f"wmax{wi}", 24 + wi, tag)
        p.task("dve", lambda e, wi=wi: e.tensor_scalar(
            out=self.scol(26 + wi), in0=self.scol(24 + wi), scalar1=1.0 / FP8MAX,
            scalar2=1e-6, op0=ALU.mult, op1=ALU.max),
               reads=[f"scol{24 + wi}"], writes=[f"scol{26 + wi}"], label=tag + "_sw")
        p.task("dve", lambda e, wi=wi: e.tensor_scalar(
            out=self.scol(28 + wi), in0=self.scol(26 + wi), scalar1=2.0, scalar2=None,
            op0=ALU.mult),
               reads=[f"scol{26 + wi}"], writes=[f"scol{28 + wi}"], label=tag + "_2s")
        p.task("dve", lambda e, wi=wi, out_col=out_col: e.reciprocal(
            out=self.scol(out_col), in_=self.scol(28 + wi)),
               reads=[f"scol{28 + wi}"], writes=[f"scol{out_col}"], label=tag + "_rc")

    def wquant_pieces(self, nm, rscol_view, rskey, tag):
        p = self.p
        w = self.w_in[nm]
        dst = self.h_q8bf[nm]
        R, Cc = w.ap().shape
        w2 = w.ap().rearrange("(a p) b -> a p b", p=128)
        d2 = dst.ap().rearrange("(a p) b -> a p b", p=128)
        q8 = self.q8v
        pieces = []

        def mk_piece(a, cq):
            def emit():
                idx = self._wm_idx
                self._wm_idx += 1
                buf = self.wmtile[idx % 3]
                key = f"wmt{idx % 3}"
                i = idx % 2
                p.task("sp", lambda e, a=a, cq=cq, b=buf, w2=w2: e.dma_start(
                    out=b[:, :], in_=w2[a, :, cq * 1024:(cq + 1) * 1024]),
                       writes=[key], dma=key, label=tag + "_ld")
                p.task("dve", lambda e, b=buf, i=i: e.tensor_scalar(
                    out=q8[0:128, i * 1024:(i + 1) * 1024], in0=b[:, :],
                    scalar1=rscol_view, scalar2=None, op0=ALU.mult),
                       reads=[key, rskey], writes=[f"q8scr{i}"], label=tag + "_q")
                p.task("pool", lambda e, a=a, cq=cq, i=i, d2=d2: e.dma_start(
                    out=d2[a, :, cq * 1024:(cq + 1) * 1024],
                    in_=q8[0:128, i * 1024:(i + 1) * 1024]),
                       reads=[f"q8scr{i}"], writes=[f"hq8_{nm}"], dma=f"q8st_{nm}{i}",
                       label=tag + "_st")
            return emit
        for a in range(R // 128):
            for cq in range(Cc // 1024):
                pieces.append(mk_piece(a, cq))
        return pieces

    def phase_b(self):
        p = self.p
        fB = self.emit_fence("fenceB")
        self.tap("h_x1_full", self.h_x1.ap(), [D, SEQS * L], F32,
                 reads=[], deps=(fB,))
        self.ar_off = self.a_base
        w8T = self.view("w8T", 128, NKC * FF, FP8)
        land = [self.view(f"land{i}", 128, FF, BF16) for i in range(2)]
        x1t = [self.view(f"x1t{i}", 128, 512, F32) for i in range(4)]
        x8 = self.view("x8", 128, NKC * 512, FP8)
        rsc = [self.view(f"rsc{i}", 128, 512, F32) for i in range(4)]
        mx = self.view("mxB", 128, 4, F32)
        rsx2col = self.view("rsx2col", 128, 1, F32)
        p.task("dve", lambda e: e.memset(mx[:, 0:1], 0.0), writes=["mxB0"],
               deps=(fB,), always_sync=True)
        self.bcast_col(13, rsx2col[:, :], "rsx2col", "bcx", deps=(fB,))
        for k in range(NKC):
            i = k % 2
            lb = land[i]
            p.task("sp", lambda e, k=k, lb=lb: e.dma_start(
                out=lb[:, :], in_=self.h_q8bf["wfc"][0:FF, k * 128:(k + 1) * 128],
                transpose=True),
                   reads=["hq8_wfc"], writes=[f"land{i}"], deps=(fB,), dma=f"land{i}",
                   label=f"bslab{k}")
            p.task("dve", lambda e, k=k, lb=lb: e.tensor_copy(
                out=w8T[:, k * FF:(k + 1) * FF], in_=lb[:, :]),
                   reads=[f"land{i}"], writes=[f"w8T{k}"], label=f"bslabc{k}")
        pbanks = [self.p0, self.p1]
        for tile in range(SEQS * L // TT):
            gt0 = tile * TT
            tg = f"B{tile}"
            for f in range(NKC):
                i = f % 4
                buf = x1t[i]
                p.task("sp", lambda e, f=f, gt0=gt0, b=buf: e.dma_start(
                    out=b[:, :], in_=self.h_x1[f * 128:(f + 1) * 128, gt0:gt0 + TT]),
                       reads=[f"hx1_{f}_{gt0 // TT}"], writes=[f"x1t{i}"], dma=f"x1t{i}",
                       label=tg + f"_xld{f}")
                p.task("dve", lambda e, f=f, b=buf: e.tensor_scalar(
                    out=x8[:, f * 512:(f + 1) * 512], in0=b[:, :],
                    scalar1=rsx2col[:, :], scalar2=None, op0=ALU.mult),
                       reads=[f"x1t{i}", "rsx2col"], writes=[f"x8_{f}"],
                       label=tg + f"_q{f}")
            for m in range(FF // 128):
                pb = pbanks[m % 2]
                pkey = f"p{m % 2}"
                for k in range(NKC):
                    p.task("pe", lambda e, m=m, k=k, pb=pb: e.matmul(
                        pb[:, 0:512], w8T[:, k * FF + m * 128: k * FF + (m + 1) * 128],
                        x8[:, k * 512:(k + 1) * 512],
                        start=(k == 0), stop=(k == NKC - 1)),
                           reads=[f"w8T{k}", f"x8_{k}"], writes=[pkey],
                           label=tg + f"_mm{m}")
                i = m % 4
                rb = rsc[i]
                p.task("act", lambda e, pb=pb, rb=rb: e.activation(
                    out=rb[:, :], in_=pb[:, 0:512], func=ACTF.Relu),
                       reads=[pkey], writes=[f"rsc{i}"], label=tg + f"_relu{m}")
                p.task("dve", lambda e, rb=rb: e.tensor_reduce(
                    out=mx[:, 1:2], in_=rb[:, :], axis=AX.X, op=ALU.max),
                       reads=[f"rsc{i}"], writes=["mxB1"], label=tg + f"_mx{m}")
                p.task("dve", lambda e: e.tensor_tensor(
                    out=mx[:, 0:1], in0=mx[:, 0:1], in1=mx[:, 1:2], op=ALU.max),
                       reads=["mxB1", "mxB0"], writes=["mxB0"], label=tg + f"_mxa{m}")
                p.task("act", lambda e, m=m, gt0=gt0, rb=rb: e.dma_start(
                    out=self.h_r[m * 128:(m + 1) * 128, gt0:gt0 + TT], in_=rb[:, :]),
                       reads=[f"rsc{i}"], writes=[f"hr_{m}_{gt0 // TT}"], dma=f"rst{i}",
                       label=tg + f"_rst{m}")
        self.scalar_finalize_max(mx[:, 0:1], "mxB0", 16, "mxr")
        self.emit_collective("cc_sh", 16, 17, 1)
        p.task("dve", lambda e: e.tensor_tensor(
            out=self.scol(18), in0=self.scol(12), in1=self.scol(26), op=ALU.mult),
               reads=["scol12", "scol26"], writes=["scol18"], label="c_mul")
        p.task("dve", lambda e: e.tensor_scalar(
            out=self.scol(18), in0=self.scol(18), scalar1=4.0, scalar2=None, op0=ALU.mult),
               reads=["scol18"], writes=["scol18"], label="c_4")
        p.task("dve", lambda e: e.tensor_tensor(
            out=self.scol(24), in0=self.scol(17), in1=self.scol(18), op=ALU.mult),
               reads=["scol17", "scol18"], writes=["scol24"], label="mh1")
        p.task("dve", lambda e: e.tensor_tensor(
            out=self.scol(24), in0=self.scol(24), in1=self.scol(24), op=ALU.mult),
               reads=["scol24"], writes=["scol24"], label="mh2")
        p.task("dve", lambda e: e.tensor_scalar(
            out=self.scol(19), in0=self.scol(24), scalar1=1.0 / FP8MAX, scalar2=1e-6,
            op0=ALU.mult, op1=ALU.max),
               reads=["scol24"], writes=["scol19"], label="sh")
        p.task("dve", lambda e: e.tensor_tensor(
            out=self.scol(25), in0=self.scol(18), in1=self.scol(18), op=ALU.mult),
               reads=["scol18"], writes=["scol25"], label="kq1")
        p.task("dve", lambda e: e.tensor_scalar(
            out=self.scol(30), in0=self.scol(19), scalar1=2.0, scalar2=None, op0=ALU.mult),
               reads=["scol19"], writes=["scol30"], label="kq2")
        p.task("dve", lambda e: e.reciprocal(out=self.scol(31), in_=self.scol(30)),
               reads=["scol30"], writes=["scol31"], label="kq3")
        p.task("dve", lambda e: e.tensor_tensor(
            out=self.scol(20), in0=self.scol(25), in1=self.scol(31), op=ALU.mult),
               reads=["scol25", "scol31"], writes=["scol20"], label="kq4")
        p.task("dve", lambda e: e.tensor_tensor(
            out=self.scol(21), in0=self.scol(19), in1=self.scol(27), op=ALU.mult),
               reads=["scol19", "scol27"], writes=["scol21"], label="c21")
        p.task("dve", lambda e: e.tensor_scalar(
            out=self.scol(21), in0=self.scol(21), scalar1=4.0, scalar2=None, op0=ALU.mult),
               reads=["scol21"], writes=["scol21"], label="c22")

    def emit_fence(self, name):
        p = self.p
        exempt = ("hbf_", "hwt_", "hq8_", "scol", "c_", "convw", "wba", "dtb",
                  "onw", "nega", "alog")
        keys = [k for k in list(p.buf_writer.keys())
                if not any(k.startswith(x) for x in exempt)]
        return p.task("dve", lambda e: e.engine_nop(), writes=keys, label=name)

    def phase_ab_transition(self):
        p = self.p
        self.scalar_finalize_max(self.amax[:, 0:1], "amax0", 10, "amx")
        self.emit_collective("cc_sx", 10, 11, 0)
        p.task("dve", lambda e: e.tensor_scalar(
            out=self.scol(12), in0=self.scol(11), scalar1=1.0 / FP8MAX, scalar2=1e-6,
            op0=ALU.mult, op1=ALU.max),
               reads=["scol11"], writes=["scol12"], label="sx")
        p.task("dve", lambda e: e.tensor_scalar(
            out=self.scol(29), in0=self.scol(12), scalar1=2.0, scalar2=None, op0=ALU.mult),
               reads=["scol12"], writes=["scol29"], label="sx2")
        p.task("dve", lambda e: e.reciprocal(out=self.scol(13), in_=self.scol(29)),
               reads=["scol29"], writes=["scol13"], label="rsx2")

    def phase_c(self):
        p = self.p
        fC = self.emit_fence("fenceC")
        self.tap("h_r_full", self.h_r.ap(), [FF, SEQS * L], F32,
                 reads=[], deps=(fC,))
        self.ar_off = self.a_base
        wp8T = self.view("wp8T", 128, 32 * D, FP8)
        landc = [self.view(f"landc{i}", 128, D, BF16) for i in range(2)]
        h8 = self.view("h8", 128, 32 * 512, FP8)
        rt = [self.view(f"rt{i}", 128, 512, F32) for i in range(8)]
        pc = [self.view(f"pc{i}", 128, 512, F32) for i in range(4)]
        x1c = [self.view(f"x1c{i}", 128, 512, F32) for i in range(4)]
        y2 = self.view("y2", 128, 16 * 512, F32)
        otm = [self.view(f"otm{i}", 128, D, F32) for i in range(2)]
        kqcol = self.view("kqcol", 128, 1, F32)
        c2col = self.view("c2col", 128, 1, F32)
        csA = self.view("csA", 128, 512, BF16)
        csB = self.view("csB", 128, 512, BF16)
        sqA = self.view("sqA", 128, 512, BF16)
        sqB = self.view("sqB", 128, 512, BF16)
        stc = self.view("stc", 1, 3072, F32)
        self.bcast_col(20, kqcol[:, :], "kqcol", "bkq", deps=(fC,))
        self.bcast_col(21, c2col[:, :], "c2col", "bc2", deps=(fC,))
        pbanks = [self.p0, self.p1]
        NT8 = SEQS * L // TT
        for half in range(2):
            hk0 = half * 32
            for kk in range(32):
                k = hk0 + kk
                i = kk % 2
                lb = landc[i]
                p.task("sp", lambda e, k=k, lb=lb: e.dma_start(
                    out=lb[:, :], in_=self.h_q8bf["wproj"][0:D, k * 128:(k + 1) * 128],
                    transpose=True),
                       reads=["hq8_wproj"], writes=[f"landc{i}"], deps=(fC,),
                       dma=f"landc{i}", label=f"cslab{k}")
                p.task("dve", lambda e, kk=kk, lb=lb: e.tensor_copy(
                    out=wp8T[:, kk * D:(kk + 1) * D], in_=lb[:, :]),
                       reads=[f"landc{i}"], writes=[f"wp8T{kk}"], label=f"cslabc{kk}")
            for tile in range(NT8):
                gt0 = tile * TT
                tg = f"C{half}_{tile}"
                for kk in range(32):
                    k = hk0 + kk
                    i = kk % 8
                    buf = rt[i]
                    p.task("sp", lambda e, k=k, gt0=gt0, b=buf: e.dma_start(
                        out=b[:, :], in_=self.h_r[k * 128:(k + 1) * 128, gt0:gt0 + TT]),
                           reads=[f"hr_{k}_{gt0 // TT}"], writes=[f"rt{i}"], dma=f"rt{i}",
                           label=tg + f"_rld{kk}")
                    p.task("dve", lambda e, kk=kk, b=buf: e.scalar_tensor_tensor(
                        out=h8[:, kk * 512:(kk + 1) * 512], in0=b[:, :],
                        scalar=kqcol[:, :], in1=b[:, :], op0=ALU.mult, op1=ALU.mult),
                           reads=[f"rt{i}", "kqcol"], writes=[f"h8_{kk}"],
                           label=tg + f"_hq{kk}")
                for m in range(NKC):
                    pb = pbanks[m % 2]
                    pkey = f"p{m % 2}"
                    for kk in range(32):
                        p.task("pe", lambda e, m=m, kk=kk, pb=pb: e.matmul(
                            pb[:, 0:512], wp8T[:, kk * D + m * 128: kk * D + (m + 1) * 128],
                            h8[:, kk * 512:(kk + 1) * 512],
                            start=(kk == 0), stop=(kk == 31)),
                               reads=[f"wp8T{kk}", f"h8_{kk}"], writes=[pkey],
                               label=tg + f"_mm{m}")
                    if half == 0:
                        i = m % 4
                        ob = pc[i]
                        p.task("dve", lambda e, pb=pb, ob=ob: e.tensor_copy(
                            out=ob[:, :], in_=pb[:, 0:512]),
                               reads=[pkey], writes=[f"pc{i}"], label=tg + f"_pev{m}")
                        p.task("act", lambda e, m=m, gt0=gt0, ob=ob: e.dma_start(
                            out=self.h_pc[m * 128:(m + 1) * 128, gt0:gt0 + TT], in_=ob[:, :]),
                               reads=[f"pc{i}"], writes=[f"hpc_{m}_{tile}"], dma=f"pcst{i}",
                               label=tg + f"_pst{m}")
                    else:
                        i = m % 4
                        ob = pc[i]
                        xb = x1c[i]
                        p.task("sp", lambda e, m=m, gt0=gt0, ob=ob: e.dma_start(
                            out=ob[:, :], in_=self.h_pc[m * 128:(m + 1) * 128, gt0:gt0 + TT]),
                               reads=[f"hpc_{m}_{tile}"], writes=[f"pc{i}"], dma=f"pcld{i}",
                               label=tg + f"_pld{m}")
                        p.task("sp", lambda e, m=m, gt0=gt0, xb=xb: e.dma_start(
                            out=xb[:, :], in_=self.h_x1[m * 128:(m + 1) * 128, gt0:gt0 + TT]),
                               reads=[f"hx1_{m}_{gt0 // TT}"], writes=[f"x1c{i}"],
                               dma=f"x1c{i}", label=tg + f"_xld{m}")
                        p.task("dve", lambda e, m=m, pb=pb, ob=ob: e.tensor_tensor(
                            out=y2[:, m * 512:(m + 1) * 512], in0=pb[:, 0:512],
                            in1=ob[:, :], op=ALU.add),
                               reads=[pkey, f"pc{i}"], writes=[f"y2_{m}"],
                               label=tg + f"_acc{m}")
                        p.task("dve", lambda e, m=m: e.tensor_scalar(
                            out=y2[:, m * 512:(m + 1) * 512], in0=y2[:, m * 512:(m + 1) * 512],
                            scalar1=c2col[:, :], scalar2=None, op0=ALU.mult),
                               reads=["c2col", f"y2_{m}"], writes=[f"y2_{m}"],
                               label=tg + f"_dq{m}")
                        p.task("dve", lambda e, m=m, xb=xb: e.tensor_tensor(
                            out=y2[:, m * 512:(m + 1) * 512], in0=y2[:, m * 512:(m + 1) * 512],
                            in1=xb[:, :], op=ALU.add),
                               reads=[f"x1c{i}", f"y2_{m}"], writes=[f"y2_{m}"],
                               label=tg + f"_res{m}")
                if half == 1:
                    self.emit_ln2_out(tile, gt0, y2, csA, csB, sqA, sqB, stc, otm, tg)

    def emit_ln2_out(self, tile, gt0, y2, csA, csB, sqA, sqB, stc, otm, tg):
        p = self.p
        for f in range(NKC):
            sq = (sqA, sqB)[f % 2]
            yb = (csA, csB)[f % 2]
            sqk = ("sqA", "sqB")[f % 2]
            ybk = ("csA", "csB")[f % 2]
            p.task("act", lambda e, f=f, sq=sq: e.activation(
                out=sq[:, :], in_=y2[:, f * 512:(f + 1) * 512], func=ACTF.Square),
                   reads=[f"y2_{f}"], writes=[sqk], label=tg + f"_ysq{f}")
            p.task("dve", lambda e, f=f, yb=yb: e.tensor_copy(
                out=yb[:, :], in_=y2[:, f * 512:(f + 1) * 512]),
                   reads=[f"y2_{f}"], writes=[ybk], label=tg + f"_ybf{f}")
            p.task("pe", lambda e, f=f, yb=yb: e.matmul(
                self.p5[0:1, 0:512], self.c_ones_col_bf[:, :], yb[:, :],
                start=(f == 0), stop=(f == 15), skip_group_check=True),
                   reads=[ybk, "c_ones_col_bf"], writes=["p5"], label=tg + f"_ss{f}")
            p.task("pe", lambda e, f=f, sq=sq: e.matmul(
                self.p3[0:1, 0:512], self.c_ones_col_bf[:, :], sq[:, :],
                start=(f == 0), stop=(f == 15), skip_group_check=True),
                   reads=[sqk, "c_ones_col_bf"], writes=["p3"], label=tg + f"_sq{f}")
        st = stc
        p.task("dve", lambda e: e.tensor_scalar(
            out=st[0:1, 0:512], in0=self.p5[0:1, 0:512], scalar1=1.0 / D, scalar2=None,
            op0=ALU.mult), reads=["p5"], writes=["ct_mu"], label=tg + "_mu")
        p.task("dve", lambda e: e.tensor_scalar(
            out=st[0:1, 512:1024], in0=self.p3[0:1, 0:512], scalar1=1.0 / D, scalar2=None,
            op0=ALU.mult), reads=["p3"], writes=["ct_m2"], label=tg + "_m2")
        p.task("dve", lambda e: e.tensor_tensor(
            out=st[0:1, 1024:1536], in0=st[0:1, 0:512], in1=st[0:1, 0:512], op=ALU.mult),
               reads=["ct_mu"], writes=["ct_t"], label=tg + "_musq")
        p.task("dve", lambda e: e.tensor_tensor(
            out=st[0:1, 1024:1536], in0=st[0:1, 512:1024], in1=st[0:1, 1024:1536],
            op=ALU.subtract), reads=["ct_m2", "ct_t"], writes=["ct_t"], label=tg + "_var")
        p.task("dve", lambda e: e.tensor_scalar(
            out=st[0:1, 1024:1536], in0=st[0:1, 1024:1536], scalar1=EPS_LN, scalar2=None,
            op0=ALU.add), reads=["ct_t"], writes=["ct_t"], label=tg + "_veps")
        p.task("dve", lambda e: e.reciprocal(out=st[0:1, 1536:2048], in_=st[0:1, 1024:1536]),
               reads=["ct_t"], writes=["ct_rc"], label=tg + "_vrc")
        p.task("act", lambda e: e.activation(out=st[0:1, 1536:2048], in_=st[0:1, 1536:2048],
                                             func=ACTF.Sqrt),
               reads=["ct_rc"], writes=["ct_rc"], label=tg + "_inv")
        p.task("dve", lambda e: e.scalar_tensor_tensor(
            out=st[0:1, 2048:2560], in0=st[0:1, 1536:2048], scalar=-1.0, in1=st[0:1, 0:512],
            op0=ALU.mult, op1=ALU.mult),
               reads=["ct_rc", "ct_mu"], writes=["ct_b"], label=tg + "_bb")
        p.task("pe", lambda e: e.matmul(self.p3[:, 0:512], self.c_ones_row_f32[:, :],
                                        st[0:1, 1536:2048], start=True, stop=True,
                                        skip_group_check=True),
               reads=["ct_rc", "c_ones_row_f32"], writes=["p3"], label=tg + "_bca")
        p.task("pe", lambda e: e.matmul(self.p4[:, 0:512], self.c_ones_row_f32[:, :],
                                        st[0:1, 2048:2560], start=True, stop=True,
                                        skip_group_check=True),
               reads=["ct_b", "c_ones_row_f32"], writes=["p4"], label=tg + "_bcb")
        for f in range(NKC):
            p.task("dve", lambda e, f=f: e.tensor_tensor(
                out=y2[:, f * 512:(f + 1) * 512], in0=y2[:, f * 512:(f + 1) * 512],
                in1=self.p3[:, 0:512], op=ALU.mult),
                   reads=["p3", f"y2_{f}"], writes=[f"y2_{f}"], label=tg + f"_na{f}")
            p.task("dve", lambda e, f=f: e.tensor_tensor(
                out=y2[:, f * 512:(f + 1) * 512], in0=y2[:, f * 512:(f + 1) * 512],
                in1=self.p4[:, 0:512], op=ALU.add),
                   reads=["p4", f"y2_{f}"], writes=[f"y2_{f}"], label=tg + f"_nb{f}")
        s = gt0 // L
        l0 = gt0 % L
        for b in range(4):
            ob = otm[b % 2]
            obk = f"otm{b % 2}"
            for f in range(NKC):
                p.task("pe", lambda e, f=f, b=b: e.transpose(
                    self.tf32[:, (f % 4) * 128:(f % 4 + 1) * 128],
                    y2[:, f * 512 + b * 128: f * 512 + (b + 1) * 128],
                    self.c_ident_f32[:, :]),
                       reads=[f"y2_{f}", "c_ident_f32"], writes=["tf32"],
                       label=tg + f"_otr{b}_{f}")
                if f % 4 == 3:
                    p.task("dve", lambda e, f=f, ob=ob: e.tensor_copy(
                        out=ob[:, (f - 3) * 128:(f + 1) * 128], in_=self.tf32[:, 0:512]),
                           reads=["tf32"], writes=[obk], label=tg + f"_oev{b}_{f}")
            p.task("act", lambda e, s=s, l0=l0, b=b, ob=ob: e.dma_start(
                out=self.out[s, l0 + b * 128: l0 + (b + 1) * 128, :], in_=ob[:, :]),
                   reads=[obk], writes=[f"outst_{tile}_{b}"], dma=f"outst{b % 2}",
                   label=tg + f"_ost{b}")

    def finalize(self):
        p = self.p
        self.tap("scal_final", self.scal[0:1, 0:32], [1, 32], F32,
                 reads=[f"scol{i}" for i in (10, 11, 12, 13, 16, 17, 18, 19, 20, 21)
                        if f"scol{i}" in p.buf_writer])
        keys = [k for k in p.buf_writer.keys() if k.startswith("outst_")]
        p.task("sp", lambda e: e.nop(), reads=keys, deps=self.tap_list,
               label="final_fence")

    def build(self, phases="abc", ntile=NTILE, nseq=SEQS):
        self.dbg_ntile = ntile
        self.dbg_nseq = nseq
        self.setup_memory()
        self.declare_io()
        self.setup_views()
        self.tap_list = []
        self.prologue()
        wv0 = ()
        wq_pj = []
        if "b" in phases:
            self.rswfc = View(self.wmax.ap[:, 2:4], "rswfc")
            n_fc = (FF // 128) * (D // 1024)
            wm_fc, wm_pj = self.wm_pieces[:n_fc], self.wm_pieces[n_fc:]

            def sc_fc():
                self.emit_wscale(0, 6, "wsc_fc")
                self.bcast_col(6, self.rswfc[:, 0:1], "rswfc0", "br6")

            def sc_pj():
                self.emit_wscale(1, 7, "wsc_pj")
                self.bcast_col(7, self.rswfc[:, 1:2], "rswfc1", "br7")
            wq_fc = self.wquant_pieces("wfc", self.rswfc[:, 0:1], "rswfc0", "qfc")
            wq_pj = self.wquant_pieces("wproj", self.rswfc[:, 1:2], "rswfc1", "qpj")
            inter = [x for pair in zip(wm_pj, wq_fc) for x in pair]
            wv0 = wm_fc + [sc_fc] + inter + [sc_pj]
        self.phase_a_seq(0, weave=wv0)
        if nseq > 1:
            self.phase_a_seq(1, weave=wq_pj)
        else:
            for w in wq_pj:
                w()
        if "b" in phases:
            self.phase_ab_transition()
            self.phase_b()
        if "c" in phases:
            self.phase_c()
        self.finalize()




ENG2BLOCK = {"pe": "tensor", "act": "scalar", "dve": "vector", "pool": "gpsimd",
             "sp": "sync"}


def emit_program(bld):
    p = bld.p
    nc = bld.nc
    bulk = {"cld"}
    per_eng = p.plan()
    final_dma = dict(p.dma_sem_count)
    sems = {}
    for e in Prog.ENGS:
        sems[("eng", e)] = nc.alloc_semaphore("sem_" + e)
    for name in p.dma_sem_count:
        sems[("dma", name)] = nc.alloc_semaphore("dsem_" + name)
    bld.ccd = nc.alloc_semaphore("ccd")
    bld.ccc = nc.alloc_semaphore("ccc")
    bld.ccd_count = [0]
    bld.ccc_count = [0]

    def stream(ename):
        def emitter(eng):
            for t in per_eng[ename]:
                for (key, val) in t.waits:
                    if key[0] == "dma" and key[1] in bulk:
                        val = 16 * final_dma[key[1]]
                    eng.wait_ge(sems[key], val)
                last = t.emit(eng)
                if t.dma_sem is not None:
                    last.then_inc(sems[("dma", t.dma_sem)], 16)
                elif t.milestone is not None:
                    last.then_inc(sems[("eng", t.eng)], 1)
        return emitter

    with nc.Block() as block:
        block.tensor(stream("pe"))
        block.scalar(stream("act"))
        block.vector(stream("dve"))
        block.gpsimd(stream("pool"))
        block.sync(stream("sp"))
    return per_eng


def build_program(taps=(), phases="abc", ntile=NTILE, nseq=SEQS,
                  paranoid=frozenset({"dve"}), max_tasks=None, cc_local=False):
    nc = bass.Bass("TRN2", target_bir_lowering=False, debug=False,
                   detect_race_conditions=False)
    bld = Builder(nc, taps, paranoid=paranoid)
    bld.cc_local = cc_local
    bld.build(phases=phases, ntile=ntile, nseq=nseq)
    if max_tasks is not None:
        bld.p.tasks = bld.p.tasks[:max_tasks]
    emit_program(bld)
    return nc, bld



_CACHE = {}


def _in_maps(inputs):
    x = np.ascontiguousarray(inputs["L0_attn_x"])
    base = {
        "Wq": inputs["L0_attn_Wq"], "Wk": inputs["L0_attn_Wk"],
        "Wv": inputs["L0_attn_Wv"], "Wb": inputs["L0_attn_Wb"],
        "Wa": inputs["L0_attn_Wa"], "Wg": inputs["L0_attn_Wg"],
        "Wo": inputs["L0_attn_Wo"],
        "A_log": np.ascontiguousarray(inputs["L0_attn_A_log"]).reshape(8, 1),
        "dt_bias": np.ascontiguousarray(inputs["L0_attn_dt_bias"]).reshape(8, 1),
        "conv_q": inputs["L0_attn_conv_q"], "conv_k": inputs["L0_attn_conv_k"],
        "conv_v": inputs["L0_attn_conv_v"],
        "o_norm_w": np.ascontiguousarray(inputs["L0_attn_o_norm_w"]).reshape(128, 1),
        "wfc": inputs["L0_ffn_w_fc"], "wproj": inputs["L0_ffn_w_proj"],
    }
    base = {k: np.ascontiguousarray(v) for k, v in base.items()}
    for nm, arr in CONSTS.items():
        base["c" + nm] = arr
    maps = []
    for c in range(NCORES):
        m = dict(base)
        m["x"] = np.ascontiguousarray(x[SEQS * c:SEQS * (c + 1)])
        maps.append(m)
    return maps


def kernel(**inputs):
    if "nc" not in _CACHE:
        _CACHE["nc"] = build_program()[0]
    nc = _CACHE["nc"]
    maps = _in_maps(inputs)
    res = run_bass_kernel_spmd(nc, maps, core_ids=list(range(NCORES)))
    out = np.concatenate([res.results[c]["out"] for c in range(NCORES)], axis=0)
    return out.astype(np.float32)
